# version 23
# baseline (speedup 1.0000x reference)
import numpy as np
import ml_dtypes
import concourse.bass as bass
import concourse.mybir as mybir
from concourse.bass_utils import run_bass_kernel_spmd
from contextlib import ExitStack

F32 = mybir.dt.float32
BF16 = mybir.dt.bfloat16
I8 = mybir.dt.int8
AF = mybir.ActivationFunctionType
ALU = mybir.AluOpType
AX = mybir.AxisListType

C_RND = float(np.float32(1.5 * 2.0**23))
D = 1024
INNER = 4096
HEADS = 8
FEAT = 64
NCORE = 8
T = 2048
TH = 2054
TQG = 2176
FC = 8
EPS = 1e-5
SW = 264

FFN1_CH = [(0, 259)] + [(259 + 256 * i, 515 + 256 * i) for i in range(6)] + [(1795, 2054)]
FFN2_CH = [(3 + 256 * i, 259 + 256 * i) for i in range(8)]


class Rec:
    __slots__ = ("idx", "eng", "emit", "deps", "dma", "risky", "needs_inc", "sem", "val",
                 "dsem")

    def __init__(self, idx, eng, emit, deps, dma, risky):
        self.idx, self.eng, self.emit = idx, eng, emit
        self.deps, self.dma, self.risky = deps, dma, risky
        self.needs_inc = False
        self.sem = None
        self.val = None


class Buf:
    def __init__(self, name):
        self.name = name
        self.last_write = None
        self.readers = []


class Prog:
    ENGS = ("TE", "VE", "SE", "GP", "SY")

    def __init__(self):
        self.ops = []

    def op(self, eng, emit, deps=(), reads=(), writes=(), dma=False, risky=False,
           dsem=None):
        dl = [d for d in deps if d is not None]
        for b in reads:
            if b.last_write is not None:
                dl.append(b.last_write)
        for b in writes:
            if b.last_write is not None:
                dl.append(b.last_write)
            dl.extend(b.readers)
        r = Rec(len(self.ops), eng, emit, dl, dma, risky)
        r.dsem = dsem
        self.ops.append(r)
        for b in reads:
            b.readers.append(r)
        for b in writes:
            b.last_write = r
            b.readers = []
        return r

    def fence(self, from_bufs, to_bufs):
        deps = []
        for b in from_bufs:
            if b.last_write is not None:
                deps.append(b.last_write)
            deps.extend(b.readers)
        rec = self.op("VE", lambda e: e.engine_nop(), deps=deps, risky=True)
        for b in to_bufs:
            b.last_write = rec
            b.readers = []
        return rec

    def finalize(self, block, sems):
        for r in self.ops:
            for d in r.deps:
                if d.dma:
                    continue
                if d.eng == r.eng and not d.risky:
                    continue
                d.needs_inc = True
        cnt = {e: 0 for e in self.ENGS}
        dcnt = {}
        for r in self.ops:
            if r.dma:
                nm = r.dsem if r.dsem is not None else ("d" + r.eng)
                dcnt[nm] = dcnt.get(nm, 0) + 16
                r.sem = nm
                r.val = dcnt[nm]
            elif r.needs_inc:
                cnt[r.eng] += 1
                r.sem = "s" + r.eng
                r.val = cnt[r.eng]
        self.dma_sems = sorted(dcnt.keys())
        for eng_name, starter in (("TE", block.tensor), ("VE", block.vector),
                                  ("SE", block.scalar), ("GP", block.gpsimd),
                                  ("SY", block.sync)):
            my = [r for r in self.ops if r.eng == eng_name]
            if not my:
                continue

            def body(e, my=my):
                last_wait = {}
                for r in my:
                    need = {}
                    for d in r.deps:
                        if d.sem is None:
                            continue
                        if d.eng == r.eng and not d.dma and not d.risky:
                            continue
                        if last_wait.get(d.sem, -1) >= d.val:
                            continue
                        if need.get(d.sem, -1) < d.val:
                            need[d.sem] = d.val
                    for sm, v in need.items():
                        e.wait_ge(sems[sm], v)
                        last_wait[sm] = v
                    inst = r.emit(e)
                    if r.sem is not None:
                        inst.then_inc(sems[r.sem], 16 if r.dma else 1)
            starter(body)


def secs(width, step=128):
    out, off = [], 0
    while off < width:
        w = min(step, width - off)
        out.append((off, w))
        off += w
    return out


DEBUG_TAPS = False


def build_nc():
    nc = bass.Bass("TRN2", target_bir_lowering=False)
    P = Prog()

    din = lambda n, s, dt=F32: nc.dram_tensor(n, s, dt, kind="ExternalInput").ap()
    x_d = din("x_fm", [D, TH])
    wqkv_d = din("wqkv", [D, 3 * D])
    proj_d = din("proj", [128, FEAT])
    wout_d = din("wout", [D, D])
    bout_d = din("bout", [D])
    wgT_d = din("wgT", [D, INNER])
    wuT_d = din("wuT", [D, INNER])
    wdT_d = din("wdT", [INNER, D])
    maskk_d = din("maskk", [128, TQG], BF16)
    maskA_d = din("maskA", [128, 128])
    maskB_d = din("maskB", [8, 128])
    eye_d = din("eye", [128, 128])
    ones_col_d = din("ones_col", [128, 1], BF16)
    ones_row_d = din("ones_row", [1, 128])
    pairmask_d = din("pairmask", [128, 2], BF16)
    y_d = nc.dram_tensor("y_fm", [D, T], F32, kind="ExternalOutput").ap()

    scr = lambda n, s, dt: nc.dram_tensor(n, s, dt, kind="Internal").ap()
    wg_s = scr("wg_s", [D, INNER], BF16)
    wu_s = scr("wu_s", [D, INNER], BF16)
    wd_s = scr("wd_s", [INNER, D], BF16)
    wqkv_bf = scr("wqkv_bf", [D, 3 * D], BF16)
    wout_bf = scr("wout_bf", [D, D], BF16)
    rows_s = scr("rows_s", [12, TH], F32)
    zrow_s = scr("zrow_s", [2, 512], BF16)
    R_M1, R_RSTD1, R_M2, R_RSTD2, R_M3, R_RSTD3, R_SU, R_FG, R_FDU, R_FD, R_SH = range(11)

    xb_t = nc.alloc_sbuf_tensor("xb", [128, FC * TH], F32)
    PF = nc.alloc_sbuf_tensor("PF", [128, 13440], F32)
    pcstage = nc.alloc_sbuf_tensor("pcstage", [128, 1024], BF16)
    PB = nc.alloc_sbuf_tensor("PB", [128, 29056], BF16)
    w8stage = nc.alloc_sbuf_tensor("w8stage", [128, 2048], I8)
    u8roll = nc.alloc_sbuf_tensor("u8roll", [128, 2 * SW], I8)
    h8roll = nc.alloc_sbuf_tensor("h8roll", [128, 2 * SW], I8)
    rstdb = nc.alloc_sbuf_tensor("rstdb", [1, TH], F32)
    BCb = nc.alloc_sbuf_tensor("BCb", [128, 5 * SW], F32)
    rowb = nc.alloc_sbuf_tensor("rowb", [1, 8 * SW], F32)
    colb = nc.alloc_sbuf_tensor("colb", [128, 72], F32)
    maskk_b = nc.alloc_sbuf_tensor("maskk_b", [128, TQG], BF16)
    maskA_b = nc.alloc_sbuf_tensor("maskA_b", [128, 128], F32)
    maskB_b = nc.alloc_sbuf_tensor("maskB_b", [8, 128], F32)
    eye_b = nc.alloc_sbuf_tensor("eye_b", [128, 128], F32)
    ones_col_b = nc.alloc_sbuf_tensor("ones_col_b", [128, 1], BF16)
    ones_row_b = nc.alloc_sbuf_tensor("ones_row_b", [1, 128], F32)
    pairmask_b = nc.alloc_sbuf_tensor("pairmask_b", [128, 2], BF16)
    bout_b = nc.alloc_sbuf_tensor("bout_b", [128, 8], F32)
    proj_b = nc.alloc_sbuf_tensor("proj_b", [128, FEAT], BF16)
    zq = nc.alloc_sbuf_tensor("zq", [128, 16], F32)
    wsc = nc.alloc_sbuf_tensor("wsc", [128, 4], F32)
    wsrow = nc.alloc_sbuf_tensor("wsrow", [1, 8], F32)
    sumc = nc.alloc_sbuf_tensor("sumc", [128, 8], F32)
    p1scr = nc.alloc_sbuf_tensor("p1scr", [128, 64], F32)
    print('SBUF remaining after allocs:', nc.sbuf_bytes_remaining)

    Hb = PF[:, 0:8448]
    vrow = PF[0:1, 0:TH]
    mrow = PF[0:1, TH:2 * TH]
    esqrow = PF[0:1, 2 * TH:3 * TH]
    Cb = PF[:, 8448:10560]
    GTb = PF[:, 10560:11616]
    MbU = PF[:, 11616:11880]
    WSTG = PF[:, 11880:13416]
    MbH = PF[:, 10560 + 3 * SW:10560 + 4 * SW]
    colbU = colb[:, 0:36]
    colbH = colb[:, 36:72]
    rowbU = rowb[0:1, 0:6 * SW]
    rowbH = rowb[0:1, 6 * SW:8 * SW]
    PHt = PF[:, 0:1280]
    PHt2 = PF[:, 1280:2560]
    zrowb = PF[0:8, 2560:3072]

    HIb = PB[:, 0:8448]
    XBFb = PB[:, 0:2112]
    X2Bb = PB[:, 2112:4224]
    UIb = PB[:, 8448:12672]
    wbfstage = PB[:, 8448:10496]
    WGb = PB[:, 12672:16768]
    WUb = PB[:, 16768:20864]
    WDb = PB[:, 20864:29056]
    U2b = PB[:, 0:5120]
    stashb = PB[:, 5120:6144]
    Vb = PB[:, 6144:11264]
    QPb = PB[:, 11264:13824]
    KPb = PB[:, 13824:16384]
    KWb = PB[:, 16384:17024]
    QRb = PB[:, 17024:18304]
    ZBb = PB[:, 18304:18848]
    AFMb = PB[:, 18848:22944]
    WQKVb = PB[:, 22944:24992]
    WVb = PB[:, 24992:27040]
    WOb = PB[:, 27040:28064]
    SBA = PB[:, 28064:28192]
    SBB = PB[0:8, 28192:28320]

    pbk = [nc.alloc_psum_tensor(f"pb{i}", [128, 512], F32) for i in range(8)]

    names = (["Cb", "GTb", "MbU", "MbH", "WSTG", "rowbU", "rowbH", "colbU", "colbH", "colb", "rstdb", "vrow", "mrow",
              "esqrow", "wsc", "wsrow", "sumc", "p1V", "p1G", "p1q0", "p1q1", "p1q2", "p1q3", "consts", "w8stage", "w8stage1", "wbf0", "wbf1", "WSTG0", "WSTG1", "WSTG2", "WSTG3", "u8a", "u8b", "h8a", "h8b",
              "wqb0", "wqb1",
              "Hb", "HIb", "UIb0", "UIb1", "WGb0", "WGb1", "WUb0", "WUb1", "WDb0", "WDb1",
              "XBFb", "X2Bb", "PHt", "PHt2", "zrowb",
              "U2b", "stash", "Vb0", "Vb1", "Vb2", "Vb3", "Vb4", "QPb", "KPb",
              "KWb", "QRb", "ZBb", "AFMb", "WQKVb0", "WQKVb1", "WVb0", "WVb1",
              "WOb", "SBA", "SBB", "zq", "zrow_s",
              "wg_s", "wu_s", "wd_s"]
             + [f"BC{i}" for i in range(5)]
             + [f"pb{i}" for i in range(8)]
             + [f"rows{i}" for i in range(12)])
    B = {n: Buf(n) for n in names}
    XR = [Buf(f"x{i}") for i in range(17)]

    def xbufs(a, b):
        return [XR[i] for i in range(a // 128, min((b + 127) // 128, 17))]

    CHUNKSET = [B[n] for n in ["Hb", "HIb", "UIb0", "UIb1", "WGb0", "WGb1", "WUb0",
                               "WUb1", "WDb0", "WDb1", "Cb", "GTb", "MbU", "MbH",
                               "colbU", "colbH", "rowbU", "rowbH"]]
    PROSET = [B[n] for n in ["XBFb", "X2Bb", "vrow", "mrow", "esqrow", "colb"]]
    ATTNSET = [B[n] for n in ["U2b", "stash", "Vb0", "Vb1", "Vb2", "Vb3", "Vb4",
                              "QPb", "KPb", "KWb", "QRb", "ZBb", "AFMb",
                              "WQKVb0", "WQKVb1", "WVb0", "WVb1", "WOb",
                              "SBA", "SBB", "PHt", "PHt2", "zrowb", "WSTG"]]

    def dma(eng, out, in_, reads=(), writes=(), deps=(), dsem=None):
        return P.op(eng, lambda e, o=out, i=in_: e.dma_start(out=o, in_=i),
                    deps=deps, reads=reads, writes=writes, dma=True, dsem=dsem)

    def t_copy(eng, out, in_, **kw):
        if eng == "SE":
            return P.op(eng, lambda e, o=out, i=in_: e.copy(out=o, in_=i), **kw)
        return P.op(eng, lambda e, o=out, i=in_: e.tensor_copy(out=o, in_=i), **kw)

    def t_tt(eng, out, in0, in1, op, **kw):
        return P.op(eng, lambda e, o=out, a=in0, b=in1, p=op:
                    e.tensor_tensor(out=o, in0=a, in1=b, op=p), **kw)

    def t_ts(eng, out, in0, s1, s2, op0, op1=None, **kw):
        if op1 is None:
            return P.op(eng, lambda e, o=out, i=in0, a=s1, p0=op0:
                        e.tensor_single_scalar(out=o, in_=i, scalar=a, op=p0), **kw)
        return P.op(eng, lambda e, o=out, i=in0, a=s1, b=s2, p0=op0, p1=op1:
                    e.tensor_scalar(out=o, in0=i, scalar1=a, scalar2=b, op0=p0, op1=p1), **kw)

    def t_red(eng, out, in_, op, absval=False, **kw):
        return P.op(eng, lambda e, o=out, i=in_, p=op, a=absval:
                    e.tensor_reduce(out=o, in_=i, axis=AX.X, op=p, apply_absolute_value=a), **kw)

    def t_act(out, in_, func, bias=0.0, scale=1.0, **kw):
        return P.op("SE", lambda e, o=out, i=in_, f=func, b=bias, s=scale:
                    e.activation(out=o, in_=i, func=f, bias=b, scale=s), **kw)

    def t_mm(out, lhsT, rhs, start, stop, **kw):
        return P.op("TE", lambda e, o=out, l=lhsT, r=rhs, st=start, sp=stop:
                    e.matmul(o, l, r, start=st, stop=sp), **kw)

    def t_tr(out, in_, **kw):
        pd = in_.partition_size
        pd = pd() if callable(pd) else pd
        return P.op("TE", lambda e, o=out, i=in_, p=pd:
                    e.transpose(o, i, eye_b[0:p, 0:p]), **kw)

    def t_stt(eng, out, in0, scalar, in1, op0, op1, **kw):
        return P.op(eng, lambda e, o=out, a=in0, s=scalar, b=in1, p0=op0, p1=op1:
                    e.scalar_tensor_tensor(out=o, in0=a, scalar=s, in1=b, op0=p0, op1=p1), **kw)

    def t_recip(eng, out, in_, **kw):
        return P.op(eng, lambda e, o=out, i=in_: e.reciprocal(out=o, in_=i), **kw)

    def t_memset(eng, ap, val, **kw):
        return P.op(eng, lambda e, a=ap, v=val: e.memset(a, v), **kw)

    def bcast(dram_row_slice, dst_tile, writes, deps=(), dsem=None):
        return dma("SE", dst_tile, dram_row_slice.partition_broadcast(128),
                   deps=deps, writes=writes, dsem=dsem)

    def rowslice(slot, a, b):
        return rows_s[slot, a:b].rearrange("(one t) -> one t", one=1)

    c = B["consts"]
    for dst, src in [(eye_b[:, :], eye_d), (maskA_b[:, :], maskA_d),
                     (maskB_b[:, :], maskB_d), (maskk_b[:, :], maskk_d),
                     (ones_col_b[:, :], ones_col_d), (ones_row_b[:, :], ones_row_d),
                     (pairmask_b[:, :], pairmask_d)]:
        dma("SY", dst, src, writes=(c,), dsem="cst")
    dma("GP", proj_b[:, :], proj_d, writes=(c,), dsem="cst2")
    for oc in range(8):
        dma("SY", bout_b[:, oc:oc + 1],
            bout_d[oc * 128:(oc + 1) * 128].rearrange("(p one) -> p one", one=1),
            writes=(c,), dsem="cst")
    dma("SY", xb_t[:, :].rearrange("p (fc t) -> p fc t", fc=FC),
        x_d.rearrange("(fc p) t -> p fc t", p=128), writes=tuple(XR), dsem="xld")

    def quant_p1(w_d, nrows, ncols, ws_idx):
        hw = 512
        tiles = [(r, h) for r in range(nrows // 128) for h in range(ncols // hw)]
        nt = len(tiles)
        NS = 3
        for i, (r, h) in enumerate(tiles):
            sl0 = (i % NS) * hw
            wB = B[f"WSTG{i % NS}"]
            src = w_d[r * 128:(r + 1) * 128, h * hw:(h + 1) * hw]
            dma("SY", WSTG[:, sl0:sl0 + hw], src, writes=(wB,), dsem=f"stg{i % NS}")
            t_red("VE", p1scr[:, i:i + 1], WSTG[:, sl0:sl0 + hw], ALU.add, absval=True,
                  reads=(wB,), writes=(B[f"p1q{i % 4}"],), risky=True)
        cb = t_red("VE", sumc[:, ws_idx:ws_idx + 1], p1scr[:, 0:nt], ALU.add,
                   reads=(B["p1q0"], B["p1q1"], B["p1q2"], B["p1q3"]),
                   writes=(B["sumc"],), risky=True)
        tr = t_tr(pbk[7][0:1, 0:128], sumc[:, ws_idx:ws_idx + 1],
                  deps=[cb], reads=(B["sumc"], B["consts"]), writes=(B["pb7"],))
        t_red("VE", wsrow[0:1, 4:5], pbk[7][0:1, 0:128], ALU.add,
              deps=[tr], reads=(B["pb7"],), writes=(B["wsrow"],), risky=True)
        t_ts("VE", wsrow[0:1, ws_idx:ws_idx + 1], wsrow[0:1, 4:5], 1.0 / (nrows * ncols),
             1e-5, ALU.mult, ALU.max, reads=(B["wsrow"],), writes=(B["wsrow"],), risky=True)
        t_recip("VE", wsrow[0:1, 5:6], wsrow[0:1, ws_idx:ws_idx + 1],
                reads=(B["wsrow"],), writes=(B["wsrow"],), risky=True)
        bc = t_mm(pbk[7][:, 128:129], ones_row_b[0:1, :], wsrow[0:1, 5:6], True, True,
                  reads=(B["wsrow"], B["consts"]), writes=(B["pb7"],))
        t_copy("VE", wsc[:, ws_idx:ws_idx + 1], pbk[7][:, 128:129],
               deps=[bc], reads=(B["pb7"],), writes=(B["wsc"],), risky=True)

    def quant_p2(w_d, w_scr, nrows, ncols, ws_idx, ssem, stage):
        hw = 512
        tiles = [(r, h) for r in range(nrows // 128) for h in range(ncols // hw)]
        NS = 3
        stores = []
        for i, (r, h) in enumerate(tiles):
            sl0 = (i % NS) * hw
            wB = B[f"WSTG{i % NS}"]
            w8B = B["w8stage"] if i % 2 == 0 else B["w8stage1"]
            s8 = (i % 2) * 1024
            eng = "VE" if i % 2 == 0 else "GP"
            src = w_d[r * 128:(r + 1) * 128, h * hw:(h + 1) * hw]
            dma("SE", WSTG[:, sl0:sl0 + hw], src, writes=(wB,), dsem=f"sg2{i % NS}")
            t_ts(eng, WSTG[:, sl0:sl0 + hw], WSTG[:, sl0:sl0 + hw], wsc[:, ws_idx:ws_idx + 1],
                 1.0, ALU.mult, ALU.min, reads=(wB, B["wsc"]), writes=(wB,))
            t_ts(eng, WSTG[:, sl0:sl0 + hw], WSTG[:, sl0:sl0 + hw], -1.0, None,
                 ALU.max, reads=(wB,), writes=(wB,))
            t_copy("VE", w8stage[:, s8:s8 + hw], WSTG[:, sl0:sl0 + hw],
                   reads=(wB,), writes=(w8B,))
            wbfB = B["wbf0"] if i % 2 == 0 else B["wbf1"]
            t_copy("SE", stage[:, s8:s8 + hw], w8stage[:, s8:s8 + hw],
                   reads=(w8B,), writes=(wbfB,))
            st = dma("SE", w_scr[r * 128:(r + 1) * 128, h * hw:(h + 1) * hw],
                     stage[:, s8:s8 + hw], reads=(wbfB,), dsem=ssem)
            stores.append(st)
        return stores

    def precast(w_d, w_bf, rlo, rhi, ncols, stores):
        hw = 1024
        for r in range(rlo // 128, rhi // 128):
            for h in range(ncols // hw):
                dma("GP", pcstage[:, :],
                    w_d[r * 128:(r + 1) * 128, h * hw:(h + 1) * hw],
                    writes=(B["wqb0"],), dsem="pcL")
                st = dma("GP", w_bf[r * 128:(r + 1) * 128, h * hw:(h + 1) * hw],
                         pcstage[:, :], reads=(B["wqb0"],), dsem="pcS")
                stores.append(st)

    qd, qg, qu, pcs = [], [], [], []

    def ln_prologue(chunks, slot_m, slot_rstd):
        lo, hi = chunks[0][0], chunks[-1][1]
        for (c0, c1) in chunks:
            w = c1 - c0
            cps, sqs = [], []
            for fc in range(FC):
                cp = t_copy("VE", XBFb[:, fc * SW: fc * SW + w],
                            xb_t[:, fc * TH + c0: fc * TH + c0 + w],
                            reads=tuple(xbufs(c0, c1)), writes=(B["XBFb"],))
                cps.append(cp)
            for fc in range(FC):
                sq = t_act(X2Bb[:, fc * SW: fc * SW + w], XBFb[:, fc * SW: fc * SW + w],
                           AF.Square, deps=[cps[fc]], reads=(B["XBFb"],), writes=(B["X2Bb"],))
                sqs.append(sq)
            mm = None
            for fc in range(FC):
                mm = t_mm(pbk[5][0:1, 0:w], ones_col_b[:, 0:1], XBFb[:, fc * SW: fc * SW + w],
                          fc == 0, fc == FC - 1, deps=[cps[fc]],
                          reads=(B["XBFb"], B["consts"]), writes=(B["pb5"],))
            t_copy("VE", mrow[0:1, c0:c1], pbk[5][0:1, 0:w], deps=[mm],
                   reads=(B["pb5"],), writes=(B["mrow"],), risky=True)
            mm = None
            for fc in range(FC):
                mm = t_mm(pbk[6][0:1, 0:w], ones_col_b[:, 0:1], X2Bb[:, fc * SW: fc * SW + w],
                          fc == 0, fc == FC - 1, deps=[sqs[fc]],
                          reads=(B["X2Bb"], B["consts"]), writes=(B["pb6"],))
            t_copy("VE", esqrow[0:1, c0:c1], pbk[6][0:1, 0:w], deps=[mm],
                   reads=(B["pb6"],), writes=(B["esqrow"],), risky=True)
        t_tt("VE", vrow[0:1, lo:hi], mrow[0:1, lo:hi], mrow[0:1, lo:hi], ALU.mult,
             reads=(B["mrow"],), writes=(B["vrow"],), risky=True)
        t_tt("VE", vrow[0:1, lo:hi], esqrow[0:1, lo:hi], vrow[0:1, lo:hi], ALU.subtract,
             reads=(B["esqrow"], B["vrow"]), writes=(B["vrow"],), risky=True)
        sl = secs(hi - lo)
        ns = len(sl)
        assert ns <= 17
        for si, (off, w) in enumerate(sl):
            tr = t_tr(pbk[7][0:w, si:si + 1], vrow[0:1, lo + off:lo + off + w],
                      reads=(B["vrow"], B["consts"]), writes=(B["pb7"],))
        ev = t_copy("VE", colb[:, 0:ns], pbk[7][:, 0:ns],
                    reads=(B["pb7"],), writes=(B["colb"],), risky=True)
        t_ts("VE", colb[:, 0:ns], colb[:, 0:ns], EPS, None, ALU.add,
             reads=(B["colb"],), writes=(B["colb"],), risky=True)
        t_recip("VE", colb[:, 17:17 + ns], colb[:, 0:ns],
                reads=(B["colb"],), writes=(B["colb"],), risky=True)
        t_act(colb[:, 34:34 + ns], colb[:, 17:17 + ns], AF.Sqrt,
              reads=(B["colb"],), writes=(B["colb"],), risky=True)
        t_tt("VE", colb[:, 51:51 + ns], colb[:, 34:34 + ns], colb[:, 34:34 + ns], ALU.mult,
             reads=(B["colb"],), writes=(B["colb"],), risky=True)
        t_tt("VE", colb[:, 51:51 + ns], colb[:, 51:51 + ns], colb[:, 0:ns], ALU.mult,
             reads=(B["colb"],), writes=(B["colb"],), risky=True)
        t_ts("VE", colb[:, 51:51 + ns], colb[:, 51:51 + ns], -0.5, 1.5, ALU.mult, ALU.add,
             reads=(B["colb"],), writes=(B["colb"],), risky=True)
        t_tt("VE", colb[:, 34:34 + ns], colb[:, 34:34 + ns], colb[:, 51:51 + ns], ALU.mult,
             reads=(B["colb"],), writes=(B["colb"],), risky=True)
        for si, (off, w) in enumerate(sl):
            tr = t_tr(pbk[5][0:1, 0:w], colb[0:w, 34 + si:35 + si],
                      reads=(B["colb"], B["consts"]), writes=(B["pb5"],))
            t_copy("VE", rstdb[0:1, lo + off:lo + off + w], pbk[5][0:1, 0:w],
                   deps=[tr], reads=(B["pb5"],), writes=(B["rstdb"],), risky=True)
        dm = dma("SE", rowslice(slot_m, lo, hi), mrow[0:1, lo:hi],
                 reads=(B["mrow"],), writes=(B[f"rows{slot_m}"],), dsem=f"rs{slot_m}")
        dr = dma("SE", rowslice(slot_rstd, lo, hi), rstdb[0:1, lo:hi],
                 reads=(B["rstdb"],), writes=(B[f"rows{slot_rstd}"],), dsem=f"rs{slot_rstd}")
        return dm, dr

    def mx_chain(w, src_red, src_bufs, pband, pbB, colap, colB, rowap, rowB, mbap, mbB,
                 bdst, bdstB, pre_m1=None):
        if pre_m1 is None:
            ap, n_inner = src_red
            m1 = t_red("VE", mbap[:, 0:w], ap, ALU.max, absval=True,
                       reads=src_bufs, writes=(mbB,), risky=True)
        else:
            m1 = pre_m1
        sl = secs(w)
        ns = len(sl)
        for si, (off, sw_) in enumerate(sl):
            reg = 128 + (si % 2) * 128
            tr = t_tr(pband[0:sw_, reg:reg + 128], mbap[:, off:off + sw_],
                      deps=[m1], reads=(mbB, B["consts"]), writes=(pbB,))
            t_red("VE", colap[0:sw_, si:si + 1], pband[0:sw_, reg:reg + 128],
                  ALU.max, deps=[tr], reads=(pbB,), writes=(colB,), risky=True)
        t_ts("VE", colap[:, 3:3 + ns], colap[:, 0:ns], EPS, None,
             ALU.max, reads=(colB,), writes=(colB,), risky=True)
        t_recip("VE", colap[:, 6:6 + ns], colap[:, 3:3 + ns],
                reads=(colB,), writes=(colB,), risky=True)
        t_ts("VE", colap[:, 9:9 + ns], colap[:, 6:6 + ns],
             127.0, None, ALU.mult, reads=(colB,), writes=(colB,), risky=True)
        for si, (off, sw_) in enumerate(sl):
            tr = t_tr(pband[0:1, 0:sw_], colap[0:sw_, 9 + si:10 + si],
                      reads=(colB, B["consts"]), writes=(pbB,))
            t_copy("VE", rowap[0:1, off:off + sw_], pband[0:1, 0:sw_],
                   deps=[tr], reads=(pbB,), writes=(rowB,), risky=True)
            tr2 = t_tr(pband[0:1, 0:sw_], colap[0:sw_, 3 + si:4 + si],
                       reads=(colB, B["consts"]), writes=(pbB,))
            t_copy("VE", rowap[0:1, SW + off:SW + off + sw_], pband[0:1, 0:sw_],
                   deps=[tr2], reads=(pbB,), writes=(rowB,), risky=True)
        return mmbcast(rowap[0:1, 0:w], rowB, w, pband, pbB, bdst, bdstB)

    def mmbcast(row_ap, rowB, w, pband, pbB, dst, dstB):
        mm = t_mm(pband[:, 0:w], ones_row_b[0:1, :], row_ap, True, True,
                  reads=(rowB, B["consts"]), writes=(pbB,))
        return t_copy("VE", dst, pband[:, 0:w], deps=[mm],
                      reads=(pbB,), writes=(dstB,), risky=True)

    def ffn_call(chunks, slot_m, ev_m, ev_r, dbg=False, emit_extra=None):
        pend = []

        def emit_down(st_):
            c0, c1, w, rnd, bfd = st_["c0"], st_["c1"], st_["w"], st_["rnd"], st_["bfd"]
            for hc in range(8):
                ws_ = hc % 2
                wdB = B[f"WDb{ws_}"]
                wdt = WDb[:, ws_ * 4096:(ws_ + 1) * 4096]
                dma("SY", wdt,
                    wd_s.rearrange("(ic p) h -> p ic h", p=128)[:, :, hc * 128:(hc + 1) * 128],
                    deps=[qd[-1]], writes=(wdB,), dsem=f"wd{ws_}")
                bank = pbk[4 + hc % 2]
                bB = B[f"pb{4 + hc % 2}"]
                dmm = None
                for ic in range(32):
                    dmm = t_mm(bank[:, 0:w], wdt[:, ic * 128:(ic + 1) * 128],
                               HIb[:, ic * SW: ic * SW + w], ic == 0, ic == 31,
                               deps=[rnd[ic]], reads=(wdB, B["HIb"]), writes=(bB,))
                tt = t_tt("VE", GTb[:, 2 * SW:2 * SW + w], bank[:, 0:w],
                          BCb[:, 4 * SW:4 * SW + w], ALU.mult, deps=[dmm, bfd],
                          reads=(bB, B["BC4"]), writes=(B["GTb"],))
                t_tt("VE", xb_t[:, hc * TH + c0: hc * TH + c0 + w],
                     xb_t[:, hc * TH + c0: hc * TH + c0 + w], GTb[:, 2 * SW:2 * SW + w],
                     ALU.add, deps=[tt], reads=(B["GTb"],) + tuple(xbufs(c0, c1)),
                     writes=tuple(xbufs(c0, c1)))

        def dtap(name, src, srcbuf, shape, dt):
            if not (DEBUG_TAPS and dbg):
                return
            td = nc.dram_tensor(name, shape, dt, kind="ExternalOutput").ap()
            dma("SY", td, src, reads=(srcbuf,))
        qstate = {}

        def emit_quant(ci):
            c0, c1 = chunks[ci]
            w = c1 - c0
            ui = ci % 2
            uB = B[f"UIb{ui}"]
            uiT = UIb[:, ui * FC * SW:(ui + 1) * FC * SW]
            mb = bcast(rowslice(slot_m, c0, c1), BCb[:, 0:w], writes=(B["BC0"],),
                       deps=[ev_m], dsem="bc0")
            for fc in range(FC):
                t_tt("GP", Cb[:, fc * SW: fc * SW + w], xb_t[:, fc * TH + c0: fc * TH + c0 + w],
                     BCb[:, 0:w], ALU.subtract, deps=[mb],
                     reads=tuple(xbufs(c0, c1)) + (B["BC0"],), writes=(B["Cb"],))
            bsu = mx_chain(w, (Cb[:, 0:FC * SW].rearrange("p (fc t) -> p t fc", fc=FC)[:, 0:w, :], FC),
                           (B["Cb"],), pbk[6], B["pb6"], colbU, B["colbU"], rowbU, B["rowbU"],
                           MbU, B["MbU"], BCb[:, SW:SW + w], B["BC1"])
            t0 = t_tt("VE", rowbU[0:1, 2 * SW:2 * SW + w], rowbU[0:1, SW:SW + w],
                      rstdb[0:1, c0:c1], ALU.mult, deps=[ev_r],
                      reads=(B["rowbU"], B["rstdb"]), writes=(B["rowbU"],), risky=True)
            fg = t_ts("VE", rowbU[0:1, 3 * SW:3 * SW + w], rowbU[0:1, 2 * SW:2 * SW + w],
                      wsrow[0:1, 0:1], 1.0 / 127, ALU.mult, ALU.mult, deps=[wsud],
                      reads=(B["rowbU"], B["wsrow"]), writes=(B["rowbU"],), risky=True)
            fsl = 4 * SW + (ci % 2) * SW
            fdu = t_ts("VE", rowbU[0:1, fsl:fsl + w], rowbU[0:1, 2 * SW:2 * SW + w],
                       wsrow[0:1, 3:4], 0.5 / (127.0 * 127.0), ALU.mult, ALU.mult,
                       reads=(B["rowbU"], B["wsrow"]), writes=(B["rowbU"],), risky=True)
            bfg = mmbcast(rowbU[0:1, 3 * SW:3 * SW + w], B["rowbU"], w, pbk[6], B["pb6"],
                          BCb[:, 2 * SW:2 * SW + w], B["BC2"])
            for fc in range(FC):
                sl8 = (fc % 2) * SW
                u8B = B["u8a"] if fc % 2 == 0 else B["u8b"]
                t_tt("VE", Cb[:, fc * SW: fc * SW + w], Cb[:, fc * SW: fc * SW + w],
                     BCb[:, SW:SW + w], ALU.mult, deps=[bsu],
                     reads=(B["Cb"], B["BC1"]), writes=(B["Cb"],))
                t_copy("VE", u8roll[:, sl8:sl8 + w], Cb[:, fc * SW: fc * SW + w],
                       reads=(B["Cb"],), writes=(u8B,), risky=True)
                t_copy("SE", uiT[:, fc * SW: fc * SW + w], u8roll[:, sl8:sl8 + w],
                       reads=(u8B,), writes=(uB,), risky=True)
            ev_ui = P.ops[-1]
            qstate[ci] = dict(uiT=uiT, uB=uB, ev_ui=ev_ui, bfg=bfg, w=w, c0=c0, c1=c1)

        emit_quant(0)
        extras = list(emit_extra or [])
        for ci, (c0, c1) in enumerate(chunks):
            st0 = qstate[ci]
            uiT, uB, ev_ui, bfg, w = st0["uiT"], st0["uB"], st0["ev_ui"], st0["bfg"], st0["w"]
            for oc in range(32):
                pr = oc // 2
                j = oc % 2
                ws_ = pr % 2
                wgB, wuB = B[f"WGb{ws_}"], B[f"WUb{ws_}"]
                wgt = WGb[:, ws_ * 2048:(ws_ + 1) * 2048]
                wut = WUb[:, ws_ * 2048:(ws_ + 1) * 2048]
                if j == 0:
                    dma("SY", wgt,
                        wg_s.rearrange("(fc p) o -> p fc o", p=128)[:, :, pr * 256:(pr + 1) * 256],
                        deps=[qg[-1]], writes=(wgB,), dsem=f"wg{ws_}")
                    dma("SY", wut,
                        wu_s.rearrange("(fc p) o -> p fc o", p=128)[:, :, pr * 256:(pr + 1) * 256],
                        deps=[qu[-1]], writes=(wuB,), dsem=f"wu{ws_}")
                gbank = pbk[oc % 3]
                gB = B[f"pb{oc % 3}"]
                ubank = pbk[3 + oc % 2]
                uB2 = B[f"pb{3 + oc % 2}"]
                gmm = None
                for f in range(FC):
                    gmm = t_mm(gbank[:, 0:w], wgt[:, f * 256 + j * 128: f * 256 + j * 128 + 128],
                               uiT[:, f * SW: f * SW + w], f == 0, f == FC - 1,
                               deps=[ev_ui], reads=(wgB, uB), writes=(gB,))
                umm = None
                for f in range(FC):
                    umm = t_mm(ubank[:, 0:w], wut[:, f * 256 + j * 128: f * 256 + j * 128 + 128],
                               uiT[:, f * SW: f * SW + w], f == 0, f == FC - 1,
                               deps=[ev_ui], reads=(wuB, uB), writes=(uB2,))
                gt = GTb[:, (oc % 2) * SW:(oc % 2) * SW + w]
                g1 = t_tt("VE", gt, gbank[:, 0:w], BCb[:, 2 * SW:2 * SW + w], ALU.mult,
                          deps=[gmm, bfg], reads=(gB, B["BC2"]), writes=(B["GTb"],))
                sl_ = t_act(gt, gt, AF.Silu, deps=[g1], reads=(B["GTb"],), writes=(B["GTb"],))
                hh = t_tt("VE", Hb[:, oc * SW: oc * SW + w], ubank[:, 0:w], gt, ALU.mult,
                          deps=[umm, sl_], reads=(uB2, B["GTb"]), writes=(B["Hb"],))
                if oc == 15:
                    m2a = t_red("VE", MbH[:, 0:w],
                                Hb[:, 0:16 * SW].rearrange("p (oc t) -> p t oc", oc=16)[:, 0:w, :],
                                ALU.max, absval=True, deps=[hh], reads=(B["Hb"],),
                                writes=(B["MbH"],), risky=True)
                if oc == 31:
                    m2h = t_red("VE", MbU[:, 0:w],
                                Hb[:, 16 * SW:32 * SW].rearrange("p (oc t) -> p t oc", oc=16)[:, 0:w, :],
                                ALU.max, absval=True, deps=[hh], reads=(B["Hb"],),
                                writes=(B["MbU"],), risky=True)
                    m2b = t_tt("VE", MbH[:, 0:w], MbH[:, 0:w], MbU[:, 0:w], ALU.max,
                               deps=[m2a, m2h], reads=(B["MbH"], B["MbU"]),
                               writes=(B["MbH"],), risky=True)
            if ci == 0:
                dtap("tap_h", Hb[:, :], B["Hb"], [128, 32 * SW], F32)
            if ci + 1 < len(chunks):
                emit_quant(ci + 1)
            if pend:
                emit_down(pend.pop(0))
            if extras:
                extras.pop(0)()
            bsh = mx_chain(w, None, (B["Hb"],), pbk[7], B["pb7"], colbH, B["colbH"],
                           rowbH, B["rowbH"], MbH, B["MbH"],
                           BCb[:, 3 * SW:3 * SW + w], B["BC3"], pre_m1=m2b)
            fsl2 = 4 * SW + (ci % 2) * SW
            fd = t_tt("VE", rowbH[0:1, SW:SW + w], rowbH[0:1, SW:SW + w],
                      rowbU[0:1, fsl2:fsl2 + w], ALU.mult,
                      reads=(B["rowbH"], B["rowbU"]), writes=(B["rowbH"],), risky=True)
            bfd = mmbcast(rowbH[0:1, SW:SW + w], B["rowbH"], w, pbk[7], B["pb7"],
                          BCb[:, 4 * SW:4 * SW + w], B["BC4"])
            rnd = []
            for oc in range(32):
                sl8 = (oc % 2) * SW
                h8B = B["h8a"] if oc % 2 == 0 else B["h8b"]
                t_tt("VE", Hb[:, oc * SW: oc * SW + w], Hb[:, oc * SW: oc * SW + w],
                     BCb[:, 3 * SW:3 * SW + w], ALU.mult, deps=[bsh],
                     reads=(B["Hb"], B["BC3"]), writes=(B["Hb"],))
                t_copy("VE", h8roll[:, sl8:sl8 + w], Hb[:, oc * SW: oc * SW + w],
                       reads=(B["Hb"],), writes=(h8B,), risky=True)
                rn = t_copy("SE", HIb[:, oc * SW: oc * SW + w], h8roll[:, sl8:sl8 + w],
                            reads=(h8B,), writes=(B["HIb"],), risky=True)
                rnd.append(rn)
            if ci == 0:
                dtap("tap_hi", HIb[:, :], B["HIb"], [128, 32 * SW], BF16)
            pend.append(dict(c0=c0, c1=c1, w=w, rnd=rnd, bfd=bfd))
        emit_down(pend.pop(0))

    def attn_phase(ev_m2, ev_r2):
        bV = [B[f"Vb{i}"] for i in range(5)]
        for q in range(4):
            Qs = 512 * q
            new0 = 0 if q == 0 else 128
            hi_local = min(640, TH - Qs)
            if q > 0:
                for fc in range(FC):
                    t_copy("SE", U2b[:, fc * 640: fc * 640 + 128],
                           stashb[:, fc * 128:(fc + 1) * 128],
                           reads=(B["stash"],), writes=(B["U2b"],), risky=True)
                t_copy("SE", Vb[:, 0:1024], Vb[:, 4 * 1024:5 * 1024],
                       reads=(bV[4],), writes=(bV[0],), risky=True)
                for g in range(4):
                    t_copy("SE", KPb[:, g * 640:g * 640 + 128],
                           KPb[:, g * 640 + 512:g * 640 + 640],
                           reads=(B["KPb"],), writes=(B["KPb"],), risky=True)
            g0, g1_ = Qs + new0, Qs + hi_local
            nw = g1_ - g0
            bm2 = bcast(rows_s[R_M2, g0:g1_].rearrange("(one t) -> one t", one=1),
                        PHt2[:, 0:nw], writes=(B["PHt2"],), deps=[ev_m2], dsem="ph2")
            br2 = bcast(rows_s[R_RSTD2, g0:g1_].rearrange("(one t) -> one t", one=1),
                        PHt2[:, 640:640 + nw], writes=(B["PHt2"],), deps=[ev_r2], dsem="ph2")
            u2evs = []
            for fc in range(FC):
                s1 = t_tt("GP", PHt[:, 0:nw], xb_t[:, fc * TH + g0: fc * TH + g1_],
                          PHt2[:, 0:nw], ALU.subtract, deps=[bm2],
                          reads=tuple(xbufs(g0, g1_)) + (B["PHt2"],), writes=(B["PHt"],))
                s2 = t_tt("GP", U2b[:, fc * 640 + new0: fc * 640 + hi_local], PHt[:, 0:nw],
                          PHt2[:, 640:640 + nw], ALU.mult, deps=[br2, s1],
                          reads=(B["PHt"], B["PHt2"]), writes=(B["U2b"],))
                u2evs.append(s2)
            if hi_local < 640:
                for fc in range(FC):
                    u2evs.append(t_memset("VE", U2b[:, fc * 640 + hi_local: fc * 640 + 640],
                                          0.0, writes=(B["U2b"],), risky=True))
            ev_u2 = u2evs[-1]
            if q < 3:
                for fc in range(FC):
                    t_copy("SE", stashb[:, fc * 128:(fc + 1) * 128],
                           U2b[:, fc * 640 + 512: fc * 640 + 640],
                           deps=[ev_u2], reads=(B["U2b"],), writes=(B["stash"],), risky=True)
            tl_range = list(range(0 if q == 0 else 1, 5))
            for tl in tl_range:
                if min(128, max(0, TH - (Qs + tl * 128))) < 128:
                    t_memset("VE", Vb[:, tl * 1024:(tl + 1) * 1024], 0.0,
                             writes=(bV[tl],), risky=True)
            for vb in range(4):
                if vb % 2 == 0:
                    vreg, vbufs = WVb[:, 0:2048], (B["WVb0"], B["WVb1"])
                else:
                    vreg, vbufs = WQKVb[:, 0:2048], (B["WQKVb0"], B["WQKVb1"])
                ldv = dma("SY", vreg,
                          wqkv_bf.rearrange("(fc p) c -> p fc c", p=128)[:, :, 2 * D + vb * 256: 2 * D + (vb + 1) * 256],
                          deps=[pcs[-1]], writes=vbufs, dsem="wv")
                for tl in tl_range:
                    gt0 = Qs + tl * 128
                    m_rows = min(128, max(0, TH - gt0))
                    if m_rows == 0:
                        continue
                    vbank = pbk[0] if tl % 2 == 0 else pbk[1]
                    vbB = B["pb0"] if tl % 2 == 0 else B["pb1"]
                    vmm = None
                    for f in range(FC):
                        vmm = t_mm(vbank[0:m_rows, 0:256],
                                   U2b[:, f * 640 + tl * 128: f * 640 + tl * 128 + m_rows],
                                   vreg[:, f * 256: f * 256 + 256],
                                   f == 0, f == FC - 1,
                                   deps=[ev_u2, ldv], reads=(B["U2b"],) + vbufs,
                                   writes=(vbB,))
                    t_copy("VE", Vb[0:m_rows, tl * 1024 + vb * 256: tl * 1024 + vb * 256 + 256],
                           vbank[0:m_rows, 0:256], deps=[vmm], reads=(vbB,), writes=(bV[tl],))
            q_lo, q_hi = 3, 515
            phi_evs = []
            for h in range(HEADS):
                g = h // 2
                rbase = (h % 2) * 64
                for which in ("q", "k"):
                    if which == "q":
                        ranges = [(q_lo, q_hi)]
                    else:
                        mid = min(new0 + 512, hi_local)
                        ranges = [(new0, mid)]
                        if mid < hi_local:
                            ranges.append((mid, hi_local))
                    slotn = h % 2
                    wB = B[f"WQKVb{slotn}"]
                    base = 0 if which == "q" else D
                    qi = h * 2 + (0 if which == "q" else 1)
                    rbank, rbB = (pbk[1], B["pb1"]) if qi % 2 == 0 else (pbk[3], B["pb3"])
                    phbank, phB = (pbk[2], B["pb2"]) if qi % 2 == 0 else (pbk[4], B["pb4"])
                    ldw = dma("SY", WQKVb[:, slotn * 1024: slotn * 1024 + 1024],
                              wqkv_bf.rearrange("(fc p) c -> p fc c", p=128)[:, :, base + h * 128: base + h * 128 + 128],
                              deps=[pcs[-1]], writes=(wB,), dsem=f"qk{slotn}")
                    for clo, chi in ranges:
                        cw = chi - clo
                        rmm = None
                        for f in range(FC):
                            rmm = t_mm(rbank[:, 0:cw],
                                       WQKVb[:, slotn * 1024 + f * 128: slotn * 1024 + f * 128 + 128],
                                       U2b[:, f * 640 + clo: f * 640 + chi], f == 0, f == FC - 1,
                                       deps=[ldw, ev_u2], reads=(wB, B["U2b"]), writes=(rbB,))
                        rb = QRb[:, (0 if which == "q" else 1) * 640 + clo:
                                 (0 if which == "q" else 1) * 640 + chi]
                        ev_r = t_copy("VE", rb, rbank[:, 0:cw], deps=[rmm],
                                      reads=(rbB,), writes=(B["QRb"],))
                        ph = t_mm(phbank[0:FEAT, 0:cw], proj_b[:, :], rb, True, True,
                                  deps=[ev_r], reads=(B["QRb"], B["consts"]), writes=(phB,))
                        e1 = t_ts("VE", PHt[0:FEAT, 0:cw], phbank[0:FEAT, 0:cw], 0.0, None,
                                  ALU.min, deps=[ph], reads=(phB,), writes=(B["PHt"],))
                        e2 = t_act(PHt2[0:FEAT, 0:cw], PHt[0:FEAT, 0:cw], AF.Exp,
                                   deps=[e1], reads=(B["PHt"],), writes=(B["PHt2"],))
                        e3 = t_ts("VE", PHt[0:FEAT, 0:cw], phbank[0:FEAT, 0:cw], 0.0, None,
                                  ALU.max, deps=[ph], reads=(phB,), writes=(B["PHt"],))
                        dstb, dB = (QPb, B["QPb"]) if which == "q" else (KPb, B["KPb"])
                        if rbase == 0:
                            dst = dstb[0:64, g * 640 + clo: g * 640 + chi]
                            mk = maskk_b[0:64, Qs + clo:Qs + chi]
                        else:
                            dst = dstb[64:128, g * 640 + clo: g * 640 + chi]
                            mk = maskk_b[0:64, Qs + clo:Qs + chi]
                        if which == "q":
                            e4 = t_tt("VE", dst, PHt2[0:FEAT, 0:cw], PHt[0:FEAT, 0:cw], ALU.add,
                                      deps=[e2, e3], reads=(B["PHt"], B["PHt2"]), writes=(dB,),
                                      risky=True)
                            phi_evs.append(e4)
                        else:
                            e4 = t_tt("VE", PHt[0:FEAT, 0:cw], PHt2[0:FEAT, 0:cw],
                                      PHt[0:FEAT, 0:cw], ALU.add, deps=[e2, e3],
                                      reads=(B["PHt"], B["PHt2"]), writes=(B["PHt"],), risky=True)
                            e5 = t_tt("VE", dst, PHt[0:FEAT, 0:cw], mk, ALU.mult,
                                      reads=(B["PHt"], B["consts"]), writes=(dB,), risky=True)
                            phi_evs.append(e5)
            if hi_local < 640:
                for g in range(4):
                    phi_evs.append(t_memset("VE", KPb[:, g * 640 + hi_local:(g + 1) * 640],
                                            0.0, writes=(B["KPb"],), risky=True))
            ev_phi = phi_evs[-1]
            for g in range(4):
                kp = KPb[:, g * 640:(g + 1) * 640]
                kw = KWb[:, 0:520]
                a1 = t_tt("VE", kw, kp[:, 0:520], kp[:, 1:521], ALU.add, deps=[ev_phi],
                          reads=(B["KPb"],), writes=(B["KWb"],), risky=True)
                a2 = t_tt("VE", PHt[:, 0:520], kp[:, 2:522], kp[:, 3:523], ALU.add,
                          deps=[ev_phi], reads=(B["KPb"],), writes=(B["PHt"],), risky=True)
                a3 = t_tt("VE", PHt2[:, 0:520], kp[:, 4:524], kp[:, 5:525], ALU.add,
                          deps=[ev_phi], reads=(B["KPb"],), writes=(B["PHt2"],), risky=True)
                t_tt("VE", kw, kw, PHt[:, 0:520], ALU.add,
                     reads=(B["KWb"], B["PHt"]), writes=(B["KWb"],), risky=True)
                t_tt("VE", kw, kw, PHt2[:, 0:520], ALU.add,
                     reads=(B["KWb"], B["PHt2"]), writes=(B["KWb"],), risky=True)
                a6 = t_tt("VE", kw, kw, kp[:, 6:526], ALU.add,
                          reads=(B["KWb"], B["KPb"]), writes=(B["KWb"],), risky=True)
                pp = t_tt("VE", QRb[:, 0:512], QPb[:, g * 640 + q_lo: g * 640 + q_hi],
                          kw[:, 0:512], ALU.mult, deps=[a6, ev_phi],
                          reads=(B["QPb"], B["KWb"]), writes=(B["QRb"],), risky=True)
                zmm = []
                for b_ in range(4):
                    mm = t_mm(pbk[3][:, b_ * 2: b_ * 2 + 2], QRb[:, b_ * 128:(b_ + 1) * 128],
                              pairmask_b[:, :], True, True, deps=[pp],
                              reads=(B["QRb"], B["consts"]), writes=(B["pb3"],))
                    zmm.append(mm)
                t_ts("VE", zq[:, 0:8], pbk[3][:, 0:8], 1e-8, None, ALU.add,
                     deps=[zmm[-1]], reads=(B["pb3"],), writes=(B["zq"],), risky=True)
                t_recip("VE", zq[:, 8:16], zq[:, 0:8], reads=(B["zq"],), writes=(B["zq"],),
                        risky=True)
                sc_evs = []
                for h2 in range(2):
                    for b_ in range(4):
                        col = 8 + b_ * 2 + h2
                        tr = t_tr(pbk[3][0:1, b_ * 128:128 + b_ * 128],
                                  zq[:, col:col + 1],
                                  reads=(B["zq"], B["consts"]), writes=(B["pb3"],))
                    ev = t_copy("VE", rowbU[0:1, 0:512], pbk[3][0:1, 0:512],
                                reads=(B["pb3"],), writes=(B["rowbU"],), risky=True)
                    mm = t_mm(pbk[3][:, 0:512] if False else pbk[2][:, 0:512],
                              ones_row_b[0:1, :], rowbU[0:1, 0:512], True, True,
                              deps=[ev], reads=(B["rowbU"], B["consts"]), writes=(B["pb2"],))
                    ev2 = t_copy("VE", ZBb[:, 0:512], pbk[2][:, 0:512], deps=[mm],
                                 reads=(B["pb2"],), writes=(B["ZBb"],), risky=True)
                    rb_ = h2 * 64
                    sc_evs.append(t_tt("VE", QPb[rb_:rb_ + 64, g * 640 + q_lo: g * 640 + q_hi],
                                       QPb[rb_:rb_ + 64, g * 640 + q_lo: g * 640 + q_hi],
                                       ZBb[rb_:rb_ + 64, 0:512], ALU.mult, deps=[ev2],
                                       reads=(B["QPb"], B["ZBb"]), writes=(B["QPb"],),
                                       risky=True))
                sc_ = sc_evs[-1]
                for h2 in range(2):
                    rbase = h2 * 64
                    h = 2 * g + h2
                    stj = {}
                    for j in range(5):
                        if j < 4:
                            qs = j * 128 + 3
                            sbank = pbk[6] if j % 2 == 0 else pbk[0]
                            sbB = B["pb6"] if j % 2 == 0 else B["pb0"]
                            sA = SBA[:, 0:128] if j % 2 == 0 else KWb[:, 0:128]
                            sAB = B["SBA"] if j % 2 == 0 else B["KWb"]
                            sB2 = SBB[0:8, 0:128] if j % 2 == 0 else KWb[0:8, 128:256]
                            sBB = B["SBB"] if j % 2 == 0 else B["KWb"]
                            sa = t_mm(sbank[:, 0:128],
                                      KPb[rbase:rbase + 64, g * 640 + j * 128: g * 640 + (j + 1) * 128],
                                      QPb[rbase:rbase + 64, g * 640 + qs: g * 640 + qs + 128],
                                      True, True, deps=[sc_], reads=(B["KPb"], B["QPb"]),
                                      writes=(sbB,))
                            sb_ = t_mm(sbank[0:8, 128:256],
                                       KPb[rbase:rbase + 64, g * 640 + (j + 1) * 128: g * 640 + (j + 1) * 128 + 8],
                                       QPb[rbase:rbase + 64, g * 640 + qs: g * 640 + qs + 128],
                                       True, True, deps=[sc_], reads=(B["KPb"], B["QPb"]),
                                       writes=(sbB,))
                            maA = t_tt("VE", sA, sbank[:, 0:128], maskA_b[:, :],
                                       ALU.mult, deps=[sa], reads=(sbB, B["consts"]),
                                       writes=(sAB,))
                            maB = t_tt("VE", sB2, sbank[0:8, 128:256], maskB_b[:, :],
                                       ALU.mult, deps=[sb_], reads=(sbB, B["consts"]),
                                       writes=(sBB,))
                            stj[j] = (sA, sAB, sB2, sBB, maA, maB)
                        if j >= 1:
                            jj = j - 1
                            sA, sAB, sB2, sBB, maA, maB = stj.pop(jj)
                            abank = pbk[5] if jj % 2 == 0 else pbk[1]
                            abB = B["pb5"] if jj % 2 == 0 else B["pb1"]
                            am1 = t_mm(abank[:, 256:384],
                                       Vb[:, jj * 1024 + h * 128: jj * 1024 + (h + 1) * 128],
                                       sA, True, False, deps=[maA],
                                       reads=(bV[jj], sAB), writes=(abB,))
                            am2 = t_mm(abank[:, 256:384],
                                       Vb[0:6, (jj + 1) * 1024 + h * 128: (jj + 1) * 1024 + (h + 1) * 128],
                                       sB2[0:6, 0:128], False, True, deps=[maB],
                                       reads=(bV[jj + 1], sBB), writes=(abB,))
                            t_copy("VE", AFMb[:, h * 512 + jj * 128: h * 512 + (jj + 1) * 128],
                                   abank[:, 256:384], deps=[am2], reads=(abB,),
                                   writes=(B["AFMb"],))
            g0 = Qs + 3
            for oc in range(8):
                wsl = oc % 2
                wtile = WOb[:, 0:1024] if wsl == 0 else WQKVb[:, 0:1024]
                wtB = B["WOb"] if wsl == 0 else B["WQKVb0"]
                ldo = dma("SY", wtile,
                          wout_bf.rearrange("(fc p) c -> p fc c", p=128)[:, :, oc * 128:(oc + 1) * 128],
                          deps=[pcs[-1]], writes=(wtB,), dsem=f"wo{wsl}")
                obank = pbk[4] if wsl == 0 else pbk[5]
                obB = B["pb4"] if wsl == 0 else B["pb5"]
                omm = None
                for d_ in range(FC):
                    omm = t_mm(obank[:, 0:512],
                               wtile[:, d_ * 128:(d_ + 1) * 128],
                               AFMb[:, d_ * 512:(d_ + 1) * 512], d_ == 0, d_ == FC - 1,
                               deps=[ldo], reads=(wtB, B["AFMb"]), writes=(obB,))
                t_stt("VE", xb_t[:, oc * TH + g0: oc * TH + g0 + 512], obank[:, 0:512],
                      bout_b[:, oc:oc + 1], xb_t[:, oc * TH + g0: oc * TH + g0 + 512],
                      ALU.add, ALU.add, deps=[omm],
                      reads=(obB, B["consts"]) + tuple(xbufs(g0, g0 + 512)),
                      writes=tuple(xbufs(g0, g0 + 512)))

    def tap(name):
        if not DEBUG_TAPS:
            return
        td = nc.dram_tensor(name, [D, TH], F32, kind="ExternalOutput").ap()
        for fc in range(FC):
            dma("SY", td[fc * 128:(fc + 1) * 128, :], xb_t[:, fc * TH: (fc + 1) * TH],
                reads=tuple(XR))

    dm1, dr1 = ln_prologue(FFN1_CH, R_M1, R_RSTD1)
    quant_p1(wgT_d, D, INNER, 0)
    qg.extend(quant_p2(wgT_d, wg_s, D, INNER, 0, "qgS", WGb[:, 0:2048]))
    quant_p1(wuT_d, D, INNER, 1)
    qu.extend(quant_p2(wuT_d, wu_s, D, INNER, 1, "quS", WUb[:, 0:2048]))
    quant_p1(wdT_d, INNER, D, 2)
    wsud = t_tt("VE", wsrow[0:1, 3:4], wsrow[0:1, 1:2], wsrow[0:1, 2:3], ALU.mult,
                reads=(B["wsrow"],), writes=(B["wsrow"],), risky=True)
    P.fence(PROSET, CHUNKSET)
    ffn_call(FFN1_CH, R_M1, dm1, dr1, dbg=True, emit_extra=[
        lambda: qd.extend(quant_p2(wdT_d, wd_s, INNER, D, 2, "qdS", WDb[:, 0:2048])),
        lambda: precast(wqkv_d, wqkv_bf, 0, 512, 3 * D, pcs),
        lambda: precast(wqkv_d, wqkv_bf, 512, D, 3 * D, pcs),
        lambda: precast(wout_d, wout_bf, 0, D, D, pcs)])
    tap("tap_x1")
    if DEBUG_TAPS:
        trows = nc.dram_tensor("tap_rows", [12, TH], F32, kind="ExternalOutput").ap()
        dma("SY", trows, rows_s, reads=tuple(B[f"rows{i}"] for i in range(12)))
        twg = nc.dram_tensor("tap_wg", [D, INNER], I8, kind="ExternalOutput").ap()
        dma("SY", twg, wg_s, reads=(B["wg_s"],))
    P.fence(CHUNKSET, PROSET)
    dm2, dr2 = ln_prologue(FFN1_CH, R_M2, R_RSTD2)
    P.fence(PROSET + CHUNKSET, ATTNSET)
    attn_phase(dm2, dr2)
    tap("tap_x2")
    P.fence(ATTNSET, PROSET)
    dm3, dr3 = ln_prologue(FFN2_CH, R_M3, R_RSTD3)
    P.fence(PROSET, CHUNKSET)
    ffn_call(FFN2_CH, R_M3, dm3, dr3)

    outs = []
    for fc in range(FC):
        st = dma("SY", y_d[fc * 128:(fc + 1) * 128, :], xb_t[:, fc * TH + 3: fc * TH + 3 + T],
                 reads=tuple(XR), dsem="out")
        outs.append(st)
    P.op("SY", lambda e: e.nop(), deps=[outs[-1]])

    dsem_names = set()
    for r in P.ops:
        if r.dma:
            dsem_names.add(r.dsem if r.dsem is not None else ("d" + r.eng))
    with ExitStack() as st:
        sems = {n: st.enter_context(nc.semaphore(n))
                for n in ["sTE", "sVE", "sSE", "sGP"] + sorted(dsem_names)}
        block = st.enter_context(nc.Block())
        P.finalize(block, sems)
    return nc


_NC_CACHE = {}


def make_inmaps(L0_attn_x, L0_attn_w_qkv, L0_attn_proj, L0_attn_w_out, L0_attn_b_out,
                L0_ffn_w_gate, L0_ffn_w_up, L0_ffn_w_down):
    x = np.asarray(L0_attn_x, dtype=np.float32)
    wqkv = np.ascontiguousarray(np.asarray(L0_attn_w_qkv, dtype=np.float32))
    proj = np.ascontiguousarray(np.asarray(L0_attn_proj, dtype=np.float32))
    wout = np.ascontiguousarray(np.asarray(L0_attn_w_out, dtype=np.float32))
    bout = np.ascontiguousarray(np.asarray(L0_attn_b_out, dtype=np.float32))
    wgT = np.ascontiguousarray(np.asarray(L0_ffn_w_gate, dtype=np.float32).T)
    wuT = np.ascontiguousarray(np.asarray(L0_ffn_w_up, dtype=np.float32).T)
    wdT = np.ascontiguousarray(np.asarray(L0_ffn_w_down, dtype=np.float32).T)

    Bsz, N, _ = x.shape
    eye = np.eye(128, dtype=np.float32)
    kp = np.arange(128)[:, None]
    jj = np.arange(128)[None, :]
    maskA = ((kp - jj >= 0) & (kp - jj <= 6)).astype(np.float32)
    kp8 = np.arange(8)[:, None]
    maskB = ((jj - kp8 >= 122) & (kp8 < 6)).astype(np.float32)
    ones_col = np.full((128, 1), 1.0 / 1024, dtype=ml_dtypes.bfloat16)
    ones_row = np.ones((1, 128), dtype=np.float32)
    pairmask = np.zeros((128, 2), dtype=ml_dtypes.bfloat16)
    pairmask[0:64, 0] = 1
    pairmask[64:128, 1] = 1

    in_maps = []
    for cix in range(NCORE):
        b = cix // 4
        s = 2048 * (cix % 4)
        lo = s - 3
        shard = np.zeros((TH, D), dtype=np.float32)
        a0, a1 = max(lo, 0), min(s + 2051, N)
        shard[a0 - lo:a1 - lo] = x[b, a0:a1]
        x_fm = np.ascontiguousarray(shard.T)
        mask = np.zeros(TQG, dtype=np.float32)
        tt = np.arange(TH)
        valid = (tt + lo >= 0) & (tt + lo < N)
        mask[0:TH][valid] = 1.0
        maskk = np.ascontiguousarray(np.broadcast_to(mask, (128, TQG))).astype(ml_dtypes.bfloat16)
        in_maps.append(dict(
            x_fm=x_fm, wqkv=wqkv, proj=proj, wout=wout, bout=bout,
            wgT=wgT, wuT=wuT, wdT=wdT, maskk=maskk, maskA=maskA, maskB=maskB,
            eye=eye, ones_col=ones_col, ones_row=ones_row, pairmask=pairmask))

    return in_maps, Bsz, N


def kernel(**inputs):
    in_maps, Bsz, N = make_inmaps(**inputs)
    if "nc" not in _NC_CACHE:
        _NC_CACHE["nc"] = build_nc()
    nc = _NC_CACHE["nc"]
    res = run_bass_kernel_spmd(nc, in_maps, core_ids=list(range(NCORE)))
    out = np.empty((Bsz, N, D), dtype=np.float32)
    for cix in range(NCORE):
        b = cix // 4
        s = 2048 * (cix % 4)
        out[b, s:s + 2048] = res.results[cix]["y_fm"].T
    return out


# version 24
# speedup vs baseline: 1.2274x; 1.2274x over previous
import numpy as np
import ml_dtypes
import concourse.bass as bass
import concourse.mybir as mybir
from concourse.bass_utils import run_bass_kernel_spmd
from contextlib import ExitStack

F32 = mybir.dt.float32
BF16 = mybir.dt.bfloat16
I8 = mybir.dt.int8
AF = mybir.ActivationFunctionType
ALU = mybir.AluOpType
AX = mybir.AxisListType

C_RND = float(np.float32(1.5 * 2.0**23))
D = 1024
INNER = 4096
HEADS = 8
FEAT = 64
NCORE = 8
T = 2048
TH = 2054
TQG = 2176
FC = 8
EPS = 1e-5
SW = 264

FFN1_CH = [(0, 259)] + [(259 + 256 * i, 515 + 256 * i) for i in range(6)] + [(1795, 2054)]
FFN2_CH = [(3 + 256 * i, 259 + 256 * i) for i in range(8)]


class Rec:
    __slots__ = ("idx", "eng", "emit", "deps", "dma", "risky", "needs_inc", "sem", "val",
                 "dsem")

    def __init__(self, idx, eng, emit, deps, dma, risky):
        self.idx, self.eng, self.emit = idx, eng, emit
        self.deps, self.dma, self.risky = deps, dma, risky
        self.needs_inc = False
        self.sem = None
        self.val = None


class Buf:
    def __init__(self, name):
        self.name = name
        self.last_write = None
        self.readers = []


class Prog:
    ENGS = ("TE", "VE", "SE", "GP", "SY")

    def __init__(self):
        self.ops = []

    def op(self, eng, emit, deps=(), reads=(), writes=(), dma=False, risky=False,
           dsem=None):
        dl = [d for d in deps if d is not None]
        for b in reads:
            if b.last_write is not None:
                dl.append(b.last_write)
        for b in writes:
            if b.last_write is not None:
                dl.append(b.last_write)
            dl.extend(b.readers)
        r = Rec(len(self.ops), eng, emit, dl, dma, risky)
        r.dsem = dsem
        self.ops.append(r)
        for b in reads:
            b.readers.append(r)
        for b in writes:
            b.last_write = r
            b.readers = []
        return r

    def fence(self, from_bufs, to_bufs):
        deps = []
        for b in from_bufs:
            if b.last_write is not None:
                deps.append(b.last_write)
            deps.extend(b.readers)
        rec = self.op("VE", lambda e: e.engine_nop(), deps=deps, risky=True)
        for b in to_bufs:
            b.last_write = rec
            b.readers = []
        return rec

    def finalize(self, block, sems):
        for r in self.ops:
            for d in r.deps:
                if d.dma:
                    continue
                if d.eng == r.eng and not d.risky:
                    continue
                d.needs_inc = True
        cnt = {e: 0 for e in self.ENGS}
        dcnt = {}
        for r in self.ops:
            if r.dma:
                nm = r.dsem if r.dsem is not None else ("d" + r.eng)
                dcnt[nm] = dcnt.get(nm, 0) + 16
                r.sem = nm
                r.val = dcnt[nm]
            elif r.needs_inc:
                cnt[r.eng] += 1
                r.sem = "s" + r.eng
                r.val = cnt[r.eng]
        self.dma_sems = sorted(dcnt.keys())
        for eng_name, starter in (("TE", block.tensor), ("VE", block.vector),
                                  ("SE", block.scalar), ("GP", block.gpsimd),
                                  ("SY", block.sync)):
            my = [r for r in self.ops if r.eng == eng_name]
            if not my:
                continue

            def body(e, my=my):
                last_wait = {}
                for r in my:
                    need = {}
                    for d in r.deps:
                        if d.sem is None:
                            continue
                        if d.eng == r.eng and not d.dma and not d.risky:
                            continue
                        if last_wait.get(d.sem, -1) >= d.val:
                            continue
                        if need.get(d.sem, -1) < d.val:
                            need[d.sem] = d.val
                    for sm, v in need.items():
                        e.wait_ge(sems[sm], v)
                        last_wait[sm] = v
                    inst = r.emit(e)
                    if r.sem is not None:
                        inst.then_inc(sems[r.sem], 16 if r.dma else 1)
            starter(body)


def secs(width, step=128):
    out, off = [], 0
    while off < width:
        w = min(step, width - off)
        out.append((off, w))
        off += w
    return out


DEBUG_TAPS = False


def build_nc():
    nc = bass.Bass("TRN2", target_bir_lowering=False)
    P = Prog()

    din = lambda n, s, dt=F32: nc.dram_tensor(n, s, dt, kind="ExternalInput").ap()
    x_d = din("x_fm", [D, TH])
    wqkv_d = din("wqkv", [D, 3 * D])
    proj_d = din("proj", [128, FEAT])
    wout_d = din("wout", [D, D])
    bout_d = din("bout", [D])
    wgT_d = din("wgT", [D, INNER])
    wuT_d = din("wuT", [D, INNER])
    wdT_d = din("wdT", [INNER, D])
    maskk_d = din("maskk", [128, TQG], BF16)
    maskA_d = din("maskA", [128, 128])
    maskB_d = din("maskB", [8, 128])
    eye_d = din("eye", [128, 128])
    ones_col_d = din("ones_col", [128, 1], BF16)
    ones_row_d = din("ones_row", [1, 128])
    pairmask_d = din("pairmask", [128, 2], BF16)
    y_d = nc.dram_tensor("y_fm", [D, T], F32, kind="ExternalOutput").ap()

    scr = lambda n, s, dt: nc.dram_tensor(n, s, dt, kind="Internal").ap()
    wg_s = scr("wg_s", [D, INNER], BF16)
    wu_s = scr("wu_s", [D, INNER], BF16)
    wd_s = scr("wd_s", [INNER, D], BF16)
    wqkv_bf = scr("wqkv_bf", [D, 3 * D], BF16)
    wout_bf = scr("wout_bf", [D, D], BF16)
    rows_s = scr("rows_s", [12, TH], F32)
    zrow_s = scr("zrow_s", [2, 512], BF16)
    R_M1, R_RSTD1, R_M2, R_RSTD2, R_M3, R_RSTD3, R_SU, R_FG, R_FDU, R_FD, R_SH = range(11)

    xb_t = nc.alloc_sbuf_tensor("xb", [128, FC * TH], F32)
    PF = nc.alloc_sbuf_tensor("PF", [128, 13440], F32)
    pcstage = nc.alloc_sbuf_tensor("pcstage", [128, 1024], BF16)
    PB = nc.alloc_sbuf_tensor("PB", [128, 29056], BF16)
    w8stage = nc.alloc_sbuf_tensor("w8stage", [128, 2048], I8)
    u8roll = nc.alloc_sbuf_tensor("u8roll", [128, 2 * SW], I8)
    h8roll = nc.alloc_sbuf_tensor("h8roll", [128, 2 * SW], I8)
    rstdb = nc.alloc_sbuf_tensor("rstdb", [1, TH], F32)
    BCb = nc.alloc_sbuf_tensor("BCb", [128, 5 * SW], F32)
    rowb = nc.alloc_sbuf_tensor("rowb", [1, 8 * SW], F32)
    colb = nc.alloc_sbuf_tensor("colb", [128, 72], F32)
    maskk_b = nc.alloc_sbuf_tensor("maskk_b", [128, TQG], BF16)
    maskA_b = nc.alloc_sbuf_tensor("maskA_b", [128, 128], F32)
    maskB_b = nc.alloc_sbuf_tensor("maskB_b", [8, 128], F32)
    eye_b = nc.alloc_sbuf_tensor("eye_b", [128, 128], F32)
    ones_col_b = nc.alloc_sbuf_tensor("ones_col_b", [128, 1], BF16)
    ones_row_b = nc.alloc_sbuf_tensor("ones_row_b", [1, 128], F32)
    pairmask_b = nc.alloc_sbuf_tensor("pairmask_b", [128, 2], BF16)
    bout_b = nc.alloc_sbuf_tensor("bout_b", [128, 8], F32)
    proj_b = nc.alloc_sbuf_tensor("proj_b", [128, FEAT], BF16)
    zq = nc.alloc_sbuf_tensor("zq", [128, 16], F32)
    wsc = nc.alloc_sbuf_tensor("wsc", [128, 4], F32)
    wsrow = nc.alloc_sbuf_tensor("wsrow", [1, 8], F32)
    sumc = nc.alloc_sbuf_tensor("sumc", [128, 8], F32)
    p1scr = nc.alloc_sbuf_tensor("p1scr", [128, 64], F32)
    print('SBUF remaining after allocs:', nc.sbuf_bytes_remaining)

    Hb = PF[:, 0:8448]
    vrow = PF[0:1, 0:TH]
    mrow = PF[0:1, TH:2 * TH]
    esqrow = PF[0:1, 2 * TH:3 * TH]
    Cb = PF[:, 8448:10560]
    GTb = PF[:, 10560:11616]
    MbU = PF[:, 11616:11880]
    WSTG = PF[:, 11880:13416]
    MbH = PF[:, 10560 + 3 * SW:10560 + 4 * SW]
    colbU = colb[:, 0:36]
    colbH = colb[:, 36:72]
    rowbU = rowb[0:1, 0:6 * SW]
    rowbH = rowb[0:1, 6 * SW:8 * SW]
    PHt = PF[:, 0:1280]
    PHt2 = PF[:, 1280:2560]
    zrowb = PF[0:8, 2560:3072]

    HIb = PB[:, 0:8448]
    XBFb = PB[:, 0:2112]
    X2Bb = PB[:, 2112:4224]
    UIb = PB[:, 8448:12672]
    wbfstage = PB[:, 8448:10496]
    WGb = PB[:, 12672:16768]
    WUb = PB[:, 16768:20864]
    WDb = PB[:, 20864:29056]
    U2b = PB[:, 0:5120]
    stashb = PB[:, 5120:6144]
    Vb = PB[:, 6144:11264]
    QPb = PB[:, 11264:13824]
    KPb = PB[:, 13824:16384]
    KWb = PB[:, 16384:17024]
    QRb = PB[:, 17024:18304]
    ZBb = PB[:, 18304:18848]
    AFMb = PB[:, 18848:22944]
    WQKVb = PB[:, 22944:24992]
    WVb = PB[:, 24992:27040]
    WOb = PB[:, 27040:28064]
    SBA = PB[:, 28064:28192]
    SBB = PB[0:8, 28192:28320]

    pbk = [nc.alloc_psum_tensor(f"pb{i}", [128, 512], F32) for i in range(8)]

    names = (["Cb", "GTb", "MbU", "MbH", "WSTG", "rowbU", "rowbH", "colbU", "colbH", "colb", "rstdb", "vrow", "mrow",
              "esqrow", "wsc", "wsrow", "sumc", "p1V", "p1G", "p1q0", "p1q1", "p1q2", "p1q3", "consts", "w8stage", "w8stage1", "wbf0", "wbf1", "WSTG0", "WSTG1", "WSTG2", "WSTG3", "u8a", "u8b", "h8a", "h8b",
              "wqb0", "wqb1",
              "Hb", "HIb", "UIb0", "UIb1", "WGb0", "WGb1", "WUb0", "WUb1", "WDb0", "WDb1",
              "XBFb", "X2Bb", "PHt", "PHt2", "zrowb",
              "U2b", "stash", "Vb0", "Vb1", "Vb2", "Vb3", "Vb4", "QPb", "KPb",
              "KWb", "QRb", "ZBb", "AFMb", "WQKVb0", "WQKVb1", "WVb0", "WVb1",
              "WOb", "SBA", "SBB", "zq", "zrow_s",
              "wg_s", "wu_s", "wd_s"]
             + [f"BC{i}" for i in range(5)]
             + [f"pb{i}" for i in range(8)]
             + [f"rows{i}" for i in range(12)])
    B = {n: Buf(n) for n in names}
    XR = [Buf(f"x{i}") for i in range(17)]

    def xbufs(a, b):
        return [XR[i] for i in range(a // 128, min((b + 127) // 128, 17))]

    CHUNKSET = [B[n] for n in ["Hb", "HIb", "UIb0", "UIb1", "WGb0", "WGb1", "WUb0",
                               "WUb1", "WDb0", "WDb1", "Cb", "GTb", "MbU", "MbH",
                               "colbU", "colbH", "rowbU", "rowbH"]]
    PROSET = [B[n] for n in ["XBFb", "X2Bb", "vrow", "mrow", "esqrow", "colb"]]
    ATTNSET = [B[n] for n in ["U2b", "stash", "Vb0", "Vb1", "Vb2", "Vb3", "Vb4",
                              "QPb", "KPb", "KWb", "QRb", "ZBb", "AFMb",
                              "WQKVb0", "WQKVb1", "WVb0", "WVb1", "WOb",
                              "SBA", "SBB", "PHt", "PHt2", "zrowb", "WSTG"]]

    def dma(eng, out, in_, reads=(), writes=(), deps=(), dsem=None):
        return P.op(eng, lambda e, o=out, i=in_: e.dma_start(out=o, in_=i),
                    deps=deps, reads=reads, writes=writes, dma=True, dsem=dsem)

    def t_copy(eng, out, in_, **kw):
        if eng == "SE":
            return P.op(eng, lambda e, o=out, i=in_: e.copy(out=o, in_=i), **kw)
        return P.op(eng, lambda e, o=out, i=in_: e.tensor_copy(out=o, in_=i), **kw)

    def t_tt(eng, out, in0, in1, op, **kw):
        return P.op(eng, lambda e, o=out, a=in0, b=in1, p=op:
                    e.tensor_tensor(out=o, in0=a, in1=b, op=p), **kw)

    def t_ts(eng, out, in0, s1, s2, op0, op1=None, **kw):
        if op1 is None:
            return P.op(eng, lambda e, o=out, i=in0, a=s1, p0=op0:
                        e.tensor_single_scalar(out=o, in_=i, scalar=a, op=p0), **kw)
        return P.op(eng, lambda e, o=out, i=in0, a=s1, b=s2, p0=op0, p1=op1:
                    e.tensor_scalar(out=o, in0=i, scalar1=a, scalar2=b, op0=p0, op1=p1), **kw)

    def t_red(eng, out, in_, op, absval=False, **kw):
        return P.op(eng, lambda e, o=out, i=in_, p=op, a=absval:
                    e.tensor_reduce(out=o, in_=i, axis=AX.X, op=p, apply_absolute_value=a), **kw)

    def t_act(out, in_, func, bias=0.0, scale=1.0, **kw):
        return P.op("SE", lambda e, o=out, i=in_, f=func, b=bias, s=scale:
                    e.activation(out=o, in_=i, func=f, bias=b, scale=s), **kw)

    def t_mm(out, lhsT, rhs, start, stop, **kw):
        return P.op("TE", lambda e, o=out, l=lhsT, r=rhs, st=start, sp=stop:
                    e.matmul(o, l, r, start=st, stop=sp), **kw)

    def t_tr(out, in_, **kw):
        pd = in_.partition_size
        pd = pd() if callable(pd) else pd
        return P.op("TE", lambda e, o=out, i=in_, p=pd:
                    e.transpose(o, i, eye_b[0:p, 0:p]), **kw)

    def t_stt(eng, out, in0, scalar, in1, op0, op1, **kw):
        return P.op(eng, lambda e, o=out, a=in0, s=scalar, b=in1, p0=op0, p1=op1:
                    e.scalar_tensor_tensor(out=o, in0=a, scalar=s, in1=b, op0=p0, op1=p1), **kw)

    def t_recip(eng, out, in_, **kw):
        return P.op(eng, lambda e, o=out, i=in_: e.reciprocal(out=o, in_=i), **kw)

    def t_memset(eng, ap, val, **kw):
        return P.op(eng, lambda e, a=ap, v=val: e.memset(a, v), **kw)

    def bcast(dram_row_slice, dst_tile, writes, deps=(), dsem=None):
        return dma("SE", dst_tile, dram_row_slice.partition_broadcast(128),
                   deps=deps, writes=writes, dsem=dsem)

    def rowslice(slot, a, b):
        return rows_s[slot, a:b].rearrange("(one t) -> one t", one=1)

    c = B["consts"]
    for dst, src in [(eye_b[:, :], eye_d), (maskA_b[:, :], maskA_d),
                     (maskB_b[:, :], maskB_d), (maskk_b[:, :], maskk_d),
                     (ones_col_b[:, :], ones_col_d), (ones_row_b[:, :], ones_row_d),
                     (pairmask_b[:, :], pairmask_d)]:
        dma("SY", dst, src, writes=(c,), dsem="cst")
    dma("GP", proj_b[:, :], proj_d, writes=(c,), dsem="cst2")
    for oc in range(8):
        dma("SY", bout_b[:, oc:oc + 1],
            bout_d[oc * 128:(oc + 1) * 128].rearrange("(p one) -> p one", one=1),
            writes=(c,), dsem="cst")
    dma("SY", xb_t[:, :].rearrange("p (fc t) -> p fc t", fc=FC),
        x_d.rearrange("(fc p) t -> p fc t", p=128), writes=tuple(XR), dsem="xld")

    def quant_p1(w_d, nrows, ncols, ws_idx):
        hw = 512
        tiles = [(r, h) for r in range(nrows // 128) for h in range(ncols // hw)]
        nt = len(tiles)
        NS = 3
        for i, (r, h) in enumerate(tiles):
            sl0 = (i % NS) * hw
            wB = B[f"WSTG{i % NS}"]
            src = w_d[r * 128:(r + 1) * 128, h * hw:(h + 1) * hw]
            dma("SY", WSTG[:, sl0:sl0 + hw], src, writes=(wB,), dsem=f"stg{i % NS}")
            t_red("VE", p1scr[:, i:i + 1], WSTG[:, sl0:sl0 + hw], ALU.add, absval=True,
                  reads=(wB,), writes=(B[f"p1q{i % 4}"],), risky=True)
        cb = t_red("VE", sumc[:, ws_idx:ws_idx + 1], p1scr[:, 0:nt], ALU.add,
                   reads=(B["p1q0"], B["p1q1"], B["p1q2"], B["p1q3"]),
                   writes=(B["sumc"],), risky=True)
        tr = t_tr(pbk[7][0:1, 0:128], sumc[:, ws_idx:ws_idx + 1],
                  deps=[cb], reads=(B["sumc"], B["consts"]), writes=(B["pb7"],))
        t_red("VE", wsrow[0:1, 4:5], pbk[7][0:1, 0:128], ALU.add,
              deps=[tr], reads=(B["pb7"],), writes=(B["wsrow"],), risky=True)
        t_ts("VE", wsrow[0:1, ws_idx:ws_idx + 1], wsrow[0:1, 4:5], 1.0 / (nrows * ncols),
             1e-5, ALU.mult, ALU.max, reads=(B["wsrow"],), writes=(B["wsrow"],), risky=True)
        t_recip("VE", wsrow[0:1, 5:6], wsrow[0:1, ws_idx:ws_idx + 1],
                reads=(B["wsrow"],), writes=(B["wsrow"],), risky=True)
        bc = t_mm(pbk[7][:, 128:129], ones_row_b[0:1, :], wsrow[0:1, 5:6], True, True,
                  reads=(B["wsrow"], B["consts"]), writes=(B["pb7"],))
        t_copy("VE", wsc[:, ws_idx:ws_idx + 1], pbk[7][:, 128:129],
               deps=[bc], reads=(B["pb7"],), writes=(B["wsc"],), risky=True)

    def quant_p2(w_d, w_scr, nrows, ncols, ws_idx, ssem, stage):
        hw = 512
        tiles = [(r, h) for r in range(nrows // 128) for h in range(ncols // hw)]
        NS = 3
        stores = []
        for i, (r, h) in enumerate(tiles):
            sl0 = (i % NS) * hw
            wB = B[f"WSTG{i % NS}"]
            w8B = B["w8stage"] if i % 2 == 0 else B["w8stage1"]
            s8 = (i % 2) * 1024
            eng = "VE"
            src = w_d[r * 128:(r + 1) * 128, h * hw:(h + 1) * hw]
            dma("SE", WSTG[:, sl0:sl0 + hw], src, writes=(wB,), dsem=f"sg2{i % NS}")
            t_ts(eng, WSTG[:, sl0:sl0 + hw], WSTG[:, sl0:sl0 + hw], wsc[:, ws_idx:ws_idx + 1],
                 1.0, ALU.mult, ALU.min, reads=(wB, B["wsc"]), writes=(wB,))
            t_ts(eng, WSTG[:, sl0:sl0 + hw], WSTG[:, sl0:sl0 + hw], -1.0, None,
                 ALU.max, reads=(wB,), writes=(wB,))
            t_copy("VE", w8stage[:, s8:s8 + hw], WSTG[:, sl0:sl0 + hw],
                   reads=(wB,), writes=(w8B,))
            wbfB = B["wbf0"] if i % 2 == 0 else B["wbf1"]
            t_copy("SE", stage[:, s8:s8 + hw], w8stage[:, s8:s8 + hw],
                   reads=(w8B,), writes=(wbfB,))
            st = dma("SE", w_scr[r * 128:(r + 1) * 128, h * hw:(h + 1) * hw],
                     stage[:, s8:s8 + hw], reads=(wbfB,), dsem=ssem)
            stores.append(st)
        return stores

    def precast(w_d, w_bf, rlo, rhi, ncols, stores):
        hw = 1024
        for r in range(rlo // 128, rhi // 128):
            for h in range(ncols // hw):
                dma("GP", pcstage[:, :],
                    w_d[r * 128:(r + 1) * 128, h * hw:(h + 1) * hw],
                    writes=(B["wqb0"],), dsem="pcL")
                st = dma("GP", w_bf[r * 128:(r + 1) * 128, h * hw:(h + 1) * hw],
                         pcstage[:, :], reads=(B["wqb0"],), dsem="pcS")
                stores.append(st)

    qd, qg, qu, pcs = [], [], [], []

    def ln_prologue(chunks, slot_m, slot_rstd):
        lo, hi = chunks[0][0], chunks[-1][1]
        for (c0, c1) in chunks:
            w = c1 - c0
            cps, sqs = [], []
            for fc in range(FC):
                cp = t_copy("VE", XBFb[:, fc * SW: fc * SW + w],
                            xb_t[:, fc * TH + c0: fc * TH + c0 + w],
                            reads=tuple(xbufs(c0, c1)), writes=(B["XBFb"],))
                cps.append(cp)
            for fc in range(FC):
                sq = t_act(X2Bb[:, fc * SW: fc * SW + w], XBFb[:, fc * SW: fc * SW + w],
                           AF.Square, deps=[cps[fc]], reads=(B["XBFb"],), writes=(B["X2Bb"],))
                sqs.append(sq)
            mm = None
            for fc in range(FC):
                mm = t_mm(pbk[5][0:1, 0:w], ones_col_b[:, 0:1], XBFb[:, fc * SW: fc * SW + w],
                          fc == 0, fc == FC - 1, deps=[cps[fc]],
                          reads=(B["XBFb"], B["consts"]), writes=(B["pb5"],))
            t_copy("VE", mrow[0:1, c0:c1], pbk[5][0:1, 0:w], deps=[mm],
                   reads=(B["pb5"],), writes=(B["mrow"],), risky=True)
            mm = None
            for fc in range(FC):
                mm = t_mm(pbk[6][0:1, 0:w], ones_col_b[:, 0:1], X2Bb[:, fc * SW: fc * SW + w],
                          fc == 0, fc == FC - 1, deps=[sqs[fc]],
                          reads=(B["X2Bb"], B["consts"]), writes=(B["pb6"],))
            t_copy("VE", esqrow[0:1, c0:c1], pbk[6][0:1, 0:w], deps=[mm],
                   reads=(B["pb6"],), writes=(B["esqrow"],), risky=True)
        t_tt("VE", vrow[0:1, lo:hi], mrow[0:1, lo:hi], mrow[0:1, lo:hi], ALU.mult,
             reads=(B["mrow"],), writes=(B["vrow"],), risky=True)
        t_tt("VE", vrow[0:1, lo:hi], esqrow[0:1, lo:hi], vrow[0:1, lo:hi], ALU.subtract,
             reads=(B["esqrow"], B["vrow"]), writes=(B["vrow"],), risky=True)
        sl = secs(hi - lo)
        ns = len(sl)
        assert ns <= 17
        for si, (off, w) in enumerate(sl):
            tr = t_tr(pbk[7][0:w, si:si + 1], vrow[0:1, lo + off:lo + off + w],
                      reads=(B["vrow"], B["consts"]), writes=(B["pb7"],))
        ev = t_copy("VE", colb[:, 0:ns], pbk[7][:, 0:ns],
                    reads=(B["pb7"],), writes=(B["colb"],), risky=True)
        t_ts("VE", colb[:, 0:ns], colb[:, 0:ns], EPS, None, ALU.add,
             reads=(B["colb"],), writes=(B["colb"],), risky=True)
        t_recip("VE", colb[:, 17:17 + ns], colb[:, 0:ns],
                reads=(B["colb"],), writes=(B["colb"],), risky=True)
        t_act(colb[:, 34:34 + ns], colb[:, 17:17 + ns], AF.Sqrt,
              reads=(B["colb"],), writes=(B["colb"],), risky=True)
        t_tt("VE", colb[:, 51:51 + ns], colb[:, 34:34 + ns], colb[:, 34:34 + ns], ALU.mult,
             reads=(B["colb"],), writes=(B["colb"],), risky=True)
        t_tt("VE", colb[:, 51:51 + ns], colb[:, 51:51 + ns], colb[:, 0:ns], ALU.mult,
             reads=(B["colb"],), writes=(B["colb"],), risky=True)
        t_ts("VE", colb[:, 51:51 + ns], colb[:, 51:51 + ns], -0.5, 1.5, ALU.mult, ALU.add,
             reads=(B["colb"],), writes=(B["colb"],), risky=True)
        t_tt("VE", colb[:, 34:34 + ns], colb[:, 34:34 + ns], colb[:, 51:51 + ns], ALU.mult,
             reads=(B["colb"],), writes=(B["colb"],), risky=True)
        for si, (off, w) in enumerate(sl):
            tr = t_tr(pbk[5][0:1, 0:w], colb[0:w, 34 + si:35 + si],
                      reads=(B["colb"], B["consts"]), writes=(B["pb5"],))
            t_copy("VE", rstdb[0:1, lo + off:lo + off + w], pbk[5][0:1, 0:w],
                   deps=[tr], reads=(B["pb5"],), writes=(B["rstdb"],), risky=True)
        dm = dma("SE", rowslice(slot_m, lo, hi), mrow[0:1, lo:hi],
                 reads=(B["mrow"],), writes=(B[f"rows{slot_m}"],), dsem=f"rs{slot_m}")
        dr = dma("SE", rowslice(slot_rstd, lo, hi), rstdb[0:1, lo:hi],
                 reads=(B["rstdb"],), writes=(B[f"rows{slot_rstd}"],), dsem=f"rs{slot_rstd}")
        return dm, dr

    def mx_chain(w, src_red, src_bufs, pband, pbB, colap, colB, rowap, rowB, mbap, mbB,
                 bdst, bdstB, pre_m1=None):
        if pre_m1 is None:
            ap, n_inner = src_red
            m1 = t_red("VE", mbap[:, 0:w], ap, ALU.max, absval=True,
                       reads=src_bufs, writes=(mbB,), risky=True)
        else:
            m1 = pre_m1
        sl = secs(w)
        ns = len(sl)
        for si, (off, sw_) in enumerate(sl):
            reg = 128 + (si % 2) * 128
            tr = t_tr(pband[0:sw_, reg:reg + 128], mbap[:, off:off + sw_],
                      deps=[m1], reads=(mbB, B["consts"]), writes=(pbB,))
            t_red("VE", colap[0:sw_, si:si + 1], pband[0:sw_, reg:reg + 128],
                  ALU.max, deps=[tr], reads=(pbB,), writes=(colB,), risky=True)
        t_ts("VE", colap[:, 3:3 + ns], colap[:, 0:ns], EPS, None,
             ALU.max, reads=(colB,), writes=(colB,), risky=True)
        t_recip("VE", colap[:, 6:6 + ns], colap[:, 3:3 + ns],
                reads=(colB,), writes=(colB,), risky=True)
        t_ts("VE", colap[:, 9:9 + ns], colap[:, 6:6 + ns],
             127.0, None, ALU.mult, reads=(colB,), writes=(colB,), risky=True)
        for si, (off, sw_) in enumerate(sl):
            tr = t_tr(pband[0:1, 0:sw_], colap[0:sw_, 9 + si:10 + si],
                      reads=(colB, B["consts"]), writes=(pbB,))
            t_copy("VE", rowap[0:1, off:off + sw_], pband[0:1, 0:sw_],
                   deps=[tr], reads=(pbB,), writes=(rowB,), risky=True)
            tr2 = t_tr(pband[0:1, 0:sw_], colap[0:sw_, 3 + si:4 + si],
                       reads=(colB, B["consts"]), writes=(pbB,))
            t_copy("VE", rowap[0:1, SW + off:SW + off + sw_], pband[0:1, 0:sw_],
                   deps=[tr2], reads=(pbB,), writes=(rowB,), risky=True)
        return mmbcast(rowap[0:1, 0:w], rowB, w, pband, pbB, bdst, bdstB)

    def mmbcast(row_ap, rowB, w, pband, pbB, dst, dstB):
        mm = t_mm(pband[:, 0:w], ones_row_b[0:1, :], row_ap, True, True,
                  reads=(rowB, B["consts"]), writes=(pbB,))
        return t_copy("VE", dst, pband[:, 0:w], deps=[mm],
                      reads=(pbB,), writes=(dstB,), risky=True)

    def ffn_call(chunks, slot_m, ev_m, ev_r, dbg=False, emit_extra=None):
        pend = []

        def emit_down(st_):
            c0, c1, w, rnd, bfd = st_["c0"], st_["c1"], st_["w"], st_["rnd"], st_["bfd"]
            for hc in range(8):
                ws_ = hc % 2
                wdB = B[f"WDb{ws_}"]
                wdt = WDb[:, ws_ * 4096:(ws_ + 1) * 4096]
                dma("SY", wdt,
                    wd_s.rearrange("(ic p) h -> p ic h", p=128)[:, :, hc * 128:(hc + 1) * 128],
                    deps=[qd[-1]], writes=(wdB,), dsem=f"wd{ws_}")
                bank = pbk[4 + hc % 2]
                bB = B[f"pb{4 + hc % 2}"]
                dmm = None
                for ic in range(32):
                    dmm = t_mm(bank[:, 0:w], wdt[:, ic * 128:(ic + 1) * 128],
                               HIb[:, ic * SW: ic * SW + w], ic == 0, ic == 31,
                               deps=[rnd[ic]], reads=(wdB, B["HIb"]), writes=(bB,))
                tt = t_tt("VE", GTb[:, 2 * SW:2 * SW + w], bank[:, 0:w],
                          BCb[:, 4 * SW:4 * SW + w], ALU.mult, deps=[dmm, bfd],
                          reads=(bB, B["BC4"]), writes=(B["GTb"],))
                t_tt("VE", xb_t[:, hc * TH + c0: hc * TH + c0 + w],
                     xb_t[:, hc * TH + c0: hc * TH + c0 + w], GTb[:, 2 * SW:2 * SW + w],
                     ALU.add, deps=[tt], reads=(B["GTb"],) + tuple(xbufs(c0, c1)),
                     writes=tuple(xbufs(c0, c1)))

        def dtap(name, src, srcbuf, shape, dt):
            if not (DEBUG_TAPS and dbg):
                return
            td = nc.dram_tensor(name, shape, dt, kind="ExternalOutput").ap()
            dma("SY", td, src, reads=(srcbuf,))
        qstate = {}

        def emit_quant(ci):
            c0, c1 = chunks[ci]
            w = c1 - c0
            ui = ci % 2
            uB = B[f"UIb{ui}"]
            uiT = UIb[:, ui * FC * SW:(ui + 1) * FC * SW]
            mb = bcast(rowslice(slot_m, c0, c1), BCb[:, 0:w], writes=(B["BC0"],),
                       deps=[ev_m], dsem="bc0")
            for fc in range(FC):
                t_tt("GP", Cb[:, fc * SW: fc * SW + w], xb_t[:, fc * TH + c0: fc * TH + c0 + w],
                     BCb[:, 0:w], ALU.subtract, deps=[mb],
                     reads=tuple(xbufs(c0, c1)) + (B["BC0"],), writes=(B["Cb"],))
            bsu = mx_chain(w, (Cb[:, 0:FC * SW].rearrange("p (fc t) -> p t fc", fc=FC)[:, 0:w, :], FC),
                           (B["Cb"],), pbk[6], B["pb6"], colbU, B["colbU"], rowbU, B["rowbU"],
                           MbU, B["MbU"], BCb[:, SW:SW + w], B["BC1"])
            t0 = t_tt("VE", rowbU[0:1, 2 * SW:2 * SW + w], rowbU[0:1, SW:SW + w],
                      rstdb[0:1, c0:c1], ALU.mult, deps=[ev_r],
                      reads=(B["rowbU"], B["rstdb"]), writes=(B["rowbU"],), risky=True)
            fg = t_ts("VE", rowbU[0:1, 3 * SW:3 * SW + w], rowbU[0:1, 2 * SW:2 * SW + w],
                      wsrow[0:1, 0:1], 1.0 / 127, ALU.mult, ALU.mult, deps=[wsud],
                      reads=(B["rowbU"], B["wsrow"]), writes=(B["rowbU"],), risky=True)
            fsl = 4 * SW + (ci % 2) * SW
            fdu = t_ts("VE", rowbU[0:1, fsl:fsl + w], rowbU[0:1, 2 * SW:2 * SW + w],
                       wsrow[0:1, 3:4], 0.5 / (127.0 * 127.0), ALU.mult, ALU.mult,
                       reads=(B["rowbU"], B["wsrow"]), writes=(B["rowbU"],), risky=True)
            bfg = mmbcast(rowbU[0:1, 3 * SW:3 * SW + w], B["rowbU"], w, pbk[6], B["pb6"],
                          BCb[:, 2 * SW:2 * SW + w], B["BC2"])
            for fc in range(FC):
                sl8 = (fc % 2) * SW
                u8B = B["u8a"] if fc % 2 == 0 else B["u8b"]
                t_tt("VE", Cb[:, fc * SW: fc * SW + w], Cb[:, fc * SW: fc * SW + w],
                     BCb[:, SW:SW + w], ALU.mult, deps=[bsu],
                     reads=(B["Cb"], B["BC1"]), writes=(B["Cb"],))
                t_copy("VE", u8roll[:, sl8:sl8 + w], Cb[:, fc * SW: fc * SW + w],
                       reads=(B["Cb"],), writes=(u8B,), risky=True)
                t_copy("SE", uiT[:, fc * SW: fc * SW + w], u8roll[:, sl8:sl8 + w],
                       reads=(u8B,), writes=(uB,), risky=True)
            ev_ui = P.ops[-1]
            qstate[ci] = dict(uiT=uiT, uB=uB, ev_ui=ev_ui, bfg=bfg, w=w, c0=c0, c1=c1)

        emit_quant(0)
        extras = list(emit_extra or [])
        for ci, (c0, c1) in enumerate(chunks):
            st0 = qstate[ci]
            uiT, uB, ev_ui, bfg, w = st0["uiT"], st0["uB"], st0["ev_ui"], st0["bfg"], st0["w"]
            for oc in range(32):
                pr = oc // 2
                j = oc % 2
                ws_ = pr % 2
                wgB, wuB = B[f"WGb{ws_}"], B[f"WUb{ws_}"]
                wgt = WGb[:, ws_ * 2048:(ws_ + 1) * 2048]
                wut = WUb[:, ws_ * 2048:(ws_ + 1) * 2048]
                if j == 0:
                    dma("SY", wgt,
                        wg_s.rearrange("(fc p) o -> p fc o", p=128)[:, :, pr * 256:(pr + 1) * 256],
                        deps=[qg[-1]], writes=(wgB,), dsem=f"wg{ws_}")
                    dma("SY", wut,
                        wu_s.rearrange("(fc p) o -> p fc o", p=128)[:, :, pr * 256:(pr + 1) * 256],
                        deps=[qu[-1]], writes=(wuB,), dsem=f"wu{ws_}")
                gbank = pbk[oc % 3]
                gB = B[f"pb{oc % 3}"]
                ubank = pbk[3 + oc % 2]
                uB2 = B[f"pb{3 + oc % 2}"]
                gmm = None
                for f in range(FC):
                    gmm = t_mm(gbank[:, 0:w], wgt[:, f * 256 + j * 128: f * 256 + j * 128 + 128],
                               uiT[:, f * SW: f * SW + w], f == 0, f == FC - 1,
                               deps=[ev_ui], reads=(wgB, uB), writes=(gB,))
                umm = None
                for f in range(FC):
                    umm = t_mm(ubank[:, 0:w], wut[:, f * 256 + j * 128: f * 256 + j * 128 + 128],
                               uiT[:, f * SW: f * SW + w], f == 0, f == FC - 1,
                               deps=[ev_ui], reads=(wuB, uB), writes=(uB2,))
                gt = GTb[:, (oc % 2) * SW:(oc % 2) * SW + w]
                g1 = t_tt("VE", gt, gbank[:, 0:w], BCb[:, 2 * SW:2 * SW + w], ALU.mult,
                          deps=[gmm, bfg], reads=(gB, B["BC2"]), writes=(B["GTb"],))
                sl_ = t_act(gt, gt, AF.Silu, deps=[g1], reads=(B["GTb"],), writes=(B["GTb"],))
                hh = t_tt("VE", Hb[:, oc * SW: oc * SW + w], ubank[:, 0:w], gt, ALU.mult,
                          deps=[umm, sl_], reads=(uB2, B["GTb"]), writes=(B["Hb"],))
                if oc == 15:
                    m2a = t_red("VE", MbH[:, 0:w],
                                Hb[:, 0:16 * SW].rearrange("p (oc t) -> p t oc", oc=16)[:, 0:w, :],
                                ALU.max, absval=True, deps=[hh], reads=(B["Hb"],),
                                writes=(B["MbH"],), risky=True)
                if oc == 31:
                    m2h = t_red("VE", MbU[:, 0:w],
                                Hb[:, 16 * SW:32 * SW].rearrange("p (oc t) -> p t oc", oc=16)[:, 0:w, :],
                                ALU.max, absval=True, deps=[hh], reads=(B["Hb"],),
                                writes=(B["MbU"],), risky=True)
                    m2b = t_tt("VE", MbH[:, 0:w], MbH[:, 0:w], MbU[:, 0:w], ALU.max,
                               deps=[m2a, m2h], reads=(B["MbH"], B["MbU"]),
                               writes=(B["MbH"],), risky=True)
            if ci == 0:
                dtap("tap_h", Hb[:, :], B["Hb"], [128, 32 * SW], F32)
            if ci + 1 < len(chunks):
                emit_quant(ci + 1)
            if pend:
                emit_down(pend.pop(0))
            if extras:
                extras.pop(0)()
            bsh = mx_chain(w, None, (B["Hb"],), pbk[7], B["pb7"], colbH, B["colbH"],
                           rowbH, B["rowbH"], MbH, B["MbH"],
                           BCb[:, 3 * SW:3 * SW + w], B["BC3"], pre_m1=m2b)
            fsl2 = 4 * SW + (ci % 2) * SW
            fd = t_tt("VE", rowbH[0:1, SW:SW + w], rowbH[0:1, SW:SW + w],
                      rowbU[0:1, fsl2:fsl2 + w], ALU.mult,
                      reads=(B["rowbH"], B["rowbU"]), writes=(B["rowbH"],), risky=True)
            bfd = mmbcast(rowbH[0:1, SW:SW + w], B["rowbH"], w, pbk[7], B["pb7"],
                          BCb[:, 4 * SW:4 * SW + w], B["BC4"])
            rnd = []
            for oc in range(32):
                sl8 = (oc % 2) * SW
                h8B = B["h8a"] if oc % 2 == 0 else B["h8b"]
                t_tt("VE", Hb[:, oc * SW: oc * SW + w], Hb[:, oc * SW: oc * SW + w],
                     BCb[:, 3 * SW:3 * SW + w], ALU.mult, deps=[bsh],
                     reads=(B["Hb"], B["BC3"]), writes=(B["Hb"],))
                t_copy("VE", h8roll[:, sl8:sl8 + w], Hb[:, oc * SW: oc * SW + w],
                       reads=(B["Hb"],), writes=(h8B,), risky=True)
                rn = t_copy("SE", HIb[:, oc * SW: oc * SW + w], h8roll[:, sl8:sl8 + w],
                            reads=(h8B,), writes=(B["HIb"],), risky=True)
                rnd.append(rn)
            if ci == 0:
                dtap("tap_hi", HIb[:, :], B["HIb"], [128, 32 * SW], BF16)
            pend.append(dict(c0=c0, c1=c1, w=w, rnd=rnd, bfd=bfd))
        emit_down(pend.pop(0))

    def attn_phase(ev_m2, ev_r2):
        bV = [B[f"Vb{i}"] for i in range(5)]
        for q in range(4):
            Qs = 512 * q
            new0 = 0 if q == 0 else 128
            hi_local = min(640, TH - Qs)
            if q > 0:
                for fc in range(FC):
                    t_copy("SE", U2b[:, fc * 640: fc * 640 + 128],
                           stashb[:, fc * 128:(fc + 1) * 128],
                           reads=(B["stash"],), writes=(B["U2b"],), risky=True)
                t_copy("SE", Vb[:, 0:1024], Vb[:, 4 * 1024:5 * 1024],
                       reads=(bV[4],), writes=(bV[0],), risky=True)
                for g in range(4):
                    t_copy("SE", KPb[:, g * 640:g * 640 + 128],
                           KPb[:, g * 640 + 512:g * 640 + 640],
                           reads=(B["KPb"],), writes=(B["KPb"],), risky=True)
            g0, g1_ = Qs + new0, Qs + hi_local
            nw = g1_ - g0
            bm2 = bcast(rows_s[R_M2, g0:g1_].rearrange("(one t) -> one t", one=1),
                        PHt2[:, 0:nw], writes=(B["PHt2"],), deps=[ev_m2], dsem="ph2")
            br2 = bcast(rows_s[R_RSTD2, g0:g1_].rearrange("(one t) -> one t", one=1),
                        PHt2[:, 640:640 + nw], writes=(B["PHt2"],), deps=[ev_r2], dsem="ph2")
            u2evs = []
            for fc in range(FC):
                s1 = t_tt("GP", PHt[:, 0:nw], xb_t[:, fc * TH + g0: fc * TH + g1_],
                          PHt2[:, 0:nw], ALU.subtract, deps=[bm2],
                          reads=tuple(xbufs(g0, g1_)) + (B["PHt2"],), writes=(B["PHt"],))
                s2 = t_tt("GP", U2b[:, fc * 640 + new0: fc * 640 + hi_local], PHt[:, 0:nw],
                          PHt2[:, 640:640 + nw], ALU.mult, deps=[br2, s1],
                          reads=(B["PHt"], B["PHt2"]), writes=(B["U2b"],))
                u2evs.append(s2)
            if hi_local < 640:
                for fc in range(FC):
                    u2evs.append(t_memset("VE", U2b[:, fc * 640 + hi_local: fc * 640 + 640],
                                          0.0, writes=(B["U2b"],), risky=True))
            ev_u2 = u2evs[-1]
            if q < 3:
                for fc in range(FC):
                    t_copy("SE", stashb[:, fc * 128:(fc + 1) * 128],
                           U2b[:, fc * 640 + 512: fc * 640 + 640],
                           deps=[ev_u2], reads=(B["U2b"],), writes=(B["stash"],), risky=True)
            tl_range = list(range(0 if q == 0 else 1, 5))
            for tl in tl_range:
                if min(128, max(0, TH - (Qs + tl * 128))) < 128:
                    t_memset("VE", Vb[:, tl * 1024:(tl + 1) * 1024], 0.0,
                             writes=(bV[tl],), risky=True)
            for vb in range(4):
                if vb % 2 == 0:
                    vreg, vbufs = WVb[:, 0:2048], (B["WVb0"], B["WVb1"])
                else:
                    vreg, vbufs = WQKVb[:, 0:2048], (B["WQKVb0"], B["WQKVb1"])
                ldv = dma("SY", vreg,
                          wqkv_bf.rearrange("(fc p) c -> p fc c", p=128)[:, :, 2 * D + vb * 256: 2 * D + (vb + 1) * 256],
                          deps=[pcs[-1]], writes=vbufs, dsem="wv")
                for tl in tl_range:
                    gt0 = Qs + tl * 128
                    m_rows = min(128, max(0, TH - gt0))
                    if m_rows == 0:
                        continue
                    vbank = pbk[0] if tl % 2 == 0 else pbk[1]
                    vbB = B["pb0"] if tl % 2 == 0 else B["pb1"]
                    vmm = None
                    for f in range(FC):
                        vmm = t_mm(vbank[0:m_rows, 0:256],
                                   U2b[:, f * 640 + tl * 128: f * 640 + tl * 128 + m_rows],
                                   vreg[:, f * 256: f * 256 + 256],
                                   f == 0, f == FC - 1,
                                   deps=[ev_u2, ldv], reads=(B["U2b"],) + vbufs,
                                   writes=(vbB,))
                    t_copy("VE", Vb[0:m_rows, tl * 1024 + vb * 256: tl * 1024 + vb * 256 + 256],
                           vbank[0:m_rows, 0:256], deps=[vmm], reads=(vbB,), writes=(bV[tl],))
            q_lo, q_hi = 3, 515
            phi_evs = []
            for h in range(HEADS):
                g = h // 2
                rbase = (h % 2) * 64
                for which in ("q", "k"):
                    if which == "q":
                        ranges = [(q_lo, q_hi)]
                    else:
                        mid = min(new0 + 512, hi_local)
                        ranges = [(new0, mid)]
                        if mid < hi_local:
                            ranges.append((mid, hi_local))
                    slotn = h % 2
                    wB = B[f"WQKVb{slotn}"]
                    base = 0 if which == "q" else D
                    qi = h * 2 + (0 if which == "q" else 1)
                    rbank, rbB = (pbk[1], B["pb1"]) if qi % 2 == 0 else (pbk[3], B["pb3"])
                    phbank, phB = (pbk[2], B["pb2"]) if qi % 2 == 0 else (pbk[4], B["pb4"])
                    ldw = dma("SY", WQKVb[:, slotn * 1024: slotn * 1024 + 1024],
                              wqkv_bf.rearrange("(fc p) c -> p fc c", p=128)[:, :, base + h * 128: base + h * 128 + 128],
                              deps=[pcs[-1]], writes=(wB,), dsem=f"qk{slotn}")
                    for clo, chi in ranges:
                        cw = chi - clo
                        rmm = None
                        for f in range(FC):
                            rmm = t_mm(rbank[:, 0:cw],
                                       WQKVb[:, slotn * 1024 + f * 128: slotn * 1024 + f * 128 + 128],
                                       U2b[:, f * 640 + clo: f * 640 + chi], f == 0, f == FC - 1,
                                       deps=[ldw, ev_u2], reads=(wB, B["U2b"]), writes=(rbB,))
                        rb = QRb[:, (0 if which == "q" else 1) * 640 + clo:
                                 (0 if which == "q" else 1) * 640 + chi]
                        ev_r = t_copy("VE", rb, rbank[:, 0:cw], deps=[rmm],
                                      reads=(rbB,), writes=(B["QRb"],))
                        ph = t_mm(phbank[0:FEAT, 0:cw], proj_b[:, :], rb, True, True,
                                  deps=[ev_r], reads=(B["QRb"], B["consts"]), writes=(phB,))
                        e1 = t_ts("VE", PHt[0:FEAT, 0:cw], phbank[0:FEAT, 0:cw], 0.0, None,
                                  ALU.min, deps=[ph], reads=(phB,), writes=(B["PHt"],))
                        e2 = t_act(PHt2[0:FEAT, 0:cw], PHt[0:FEAT, 0:cw], AF.Exp,
                                   deps=[e1], reads=(B["PHt"],), writes=(B["PHt2"],))
                        e3 = t_ts("VE", PHt[0:FEAT, 0:cw], phbank[0:FEAT, 0:cw], 0.0, None,
                                  ALU.max, deps=[ph], reads=(phB,), writes=(B["PHt"],))
                        dstb, dB = (QPb, B["QPb"]) if which == "q" else (KPb, B["KPb"])
                        if rbase == 0:
                            dst = dstb[0:64, g * 640 + clo: g * 640 + chi]
                            mk = maskk_b[0:64, Qs + clo:Qs + chi]
                        else:
                            dst = dstb[64:128, g * 640 + clo: g * 640 + chi]
                            mk = maskk_b[0:64, Qs + clo:Qs + chi]
                        if which == "q":
                            e4 = t_tt("VE", dst, PHt2[0:FEAT, 0:cw], PHt[0:FEAT, 0:cw], ALU.add,
                                      deps=[e2, e3], reads=(B["PHt"], B["PHt2"]), writes=(dB,),
                                      risky=True)
                            phi_evs.append(e4)
                        else:
                            e4 = t_tt("VE", PHt[0:FEAT, 0:cw], PHt2[0:FEAT, 0:cw],
                                      PHt[0:FEAT, 0:cw], ALU.add, deps=[e2, e3],
                                      reads=(B["PHt"], B["PHt2"]), writes=(B["PHt"],), risky=True)
                            e5 = t_tt("VE", dst, PHt[0:FEAT, 0:cw], mk, ALU.mult,
                                      reads=(B["PHt"], B["consts"]), writes=(dB,), risky=True)
                            phi_evs.append(e5)
            if hi_local < 640:
                for g in range(4):
                    phi_evs.append(t_memset("VE", KPb[:, g * 640 + hi_local:(g + 1) * 640],
                                            0.0, writes=(B["KPb"],), risky=True))
            ev_phi = phi_evs[-1]
            for g in range(4):
                kp = KPb[:, g * 640:(g + 1) * 640]
                kw = KWb[:, 0:520]
                a1 = t_tt("VE", kw, kp[:, 0:520], kp[:, 1:521], ALU.add, deps=[ev_phi],
                          reads=(B["KPb"],), writes=(B["KWb"],), risky=True)
                a2 = t_tt("VE", PHt[:, 0:520], kp[:, 2:522], kp[:, 3:523], ALU.add,
                          deps=[ev_phi], reads=(B["KPb"],), writes=(B["PHt"],), risky=True)
                a3 = t_tt("VE", PHt2[:, 0:520], kp[:, 4:524], kp[:, 5:525], ALU.add,
                          deps=[ev_phi], reads=(B["KPb"],), writes=(B["PHt2"],), risky=True)
                t_tt("VE", kw, kw, PHt[:, 0:520], ALU.add,
                     reads=(B["KWb"], B["PHt"]), writes=(B["KWb"],), risky=True)
                t_tt("VE", kw, kw, PHt2[:, 0:520], ALU.add,
                     reads=(B["KWb"], B["PHt2"]), writes=(B["KWb"],), risky=True)
                a6 = t_tt("VE", kw, kw, kp[:, 6:526], ALU.add,
                          reads=(B["KWb"], B["KPb"]), writes=(B["KWb"],), risky=True)
                pp = t_tt("VE", QRb[:, 0:512], QPb[:, g * 640 + q_lo: g * 640 + q_hi],
                          kw[:, 0:512], ALU.mult, deps=[a6, ev_phi],
                          reads=(B["QPb"], B["KWb"]), writes=(B["QRb"],), risky=True)
                zmm = []
                for b_ in range(4):
                    mm = t_mm(pbk[3][:, b_ * 2: b_ * 2 + 2], QRb[:, b_ * 128:(b_ + 1) * 128],
                              pairmask_b[:, :], True, True, deps=[pp],
                              reads=(B["QRb"], B["consts"]), writes=(B["pb3"],))
                    zmm.append(mm)
                t_ts("VE", zq[:, 0:8], pbk[3][:, 0:8], 1e-8, None, ALU.add,
                     deps=[zmm[-1]], reads=(B["pb3"],), writes=(B["zq"],), risky=True)
                t_recip("VE", zq[:, 8:16], zq[:, 0:8], reads=(B["zq"],), writes=(B["zq"],),
                        risky=True)
                sc_evs = []
                for h2 in range(2):
                    for b_ in range(4):
                        col = 8 + b_ * 2 + h2
                        tr = t_tr(pbk[3][0:1, b_ * 128:128 + b_ * 128],
                                  zq[:, col:col + 1],
                                  reads=(B["zq"], B["consts"]), writes=(B["pb3"],))
                    ev = t_copy("VE", rowbU[0:1, 0:512], pbk[3][0:1, 0:512],
                                reads=(B["pb3"],), writes=(B["rowbU"],), risky=True)
                    mm = t_mm(pbk[3][:, 0:512] if False else pbk[2][:, 0:512],
                              ones_row_b[0:1, :], rowbU[0:1, 0:512], True, True,
                              deps=[ev], reads=(B["rowbU"], B["consts"]), writes=(B["pb2"],))
                    ev2 = t_copy("VE", ZBb[:, 0:512], pbk[2][:, 0:512], deps=[mm],
                                 reads=(B["pb2"],), writes=(B["ZBb"],), risky=True)
                    rb_ = h2 * 64
                    sc_evs.append(t_tt("VE", QPb[rb_:rb_ + 64, g * 640 + q_lo: g * 640 + q_hi],
                                       QPb[rb_:rb_ + 64, g * 640 + q_lo: g * 640 + q_hi],
                                       ZBb[rb_:rb_ + 64, 0:512], ALU.mult, deps=[ev2],
                                       reads=(B["QPb"], B["ZBb"]), writes=(B["QPb"],),
                                       risky=True))
                sc_ = sc_evs[-1]
                for h2 in range(2):
                    rbase = h2 * 64
                    h = 2 * g + h2
                    stj = {}
                    for j in range(5):
                        if j < 4:
                            qs = j * 128 + 3
                            sbank = pbk[6] if j % 2 == 0 else pbk[0]
                            sbB = B["pb6"] if j % 2 == 0 else B["pb0"]
                            sA = SBA[:, 0:128] if j % 2 == 0 else KWb[:, 0:128]
                            sAB = B["SBA"] if j % 2 == 0 else B["KWb"]
                            sB2 = SBB[0:8, 0:128] if j % 2 == 0 else KWb[0:8, 128:256]
                            sBB = B["SBB"] if j % 2 == 0 else B["KWb"]
                            sa = t_mm(sbank[:, 0:128],
                                      KPb[rbase:rbase + 64, g * 640 + j * 128: g * 640 + (j + 1) * 128],
                                      QPb[rbase:rbase + 64, g * 640 + qs: g * 640 + qs + 128],
                                      True, True, deps=[sc_], reads=(B["KPb"], B["QPb"]),
                                      writes=(sbB,))
                            sb_ = t_mm(sbank[0:8, 128:256],
                                       KPb[rbase:rbase + 64, g * 640 + (j + 1) * 128: g * 640 + (j + 1) * 128 + 8],
                                       QPb[rbase:rbase + 64, g * 640 + qs: g * 640 + qs + 128],
                                       True, True, deps=[sc_], reads=(B["KPb"], B["QPb"]),
                                       writes=(sbB,))
                            maA = t_tt("VE", sA, sbank[:, 0:128], maskA_b[:, :],
                                       ALU.mult, deps=[sa], reads=(sbB, B["consts"]),
                                       writes=(sAB,))
                            maB = t_tt("VE", sB2, sbank[0:8, 128:256], maskB_b[:, :],
                                       ALU.mult, deps=[sb_], reads=(sbB, B["consts"]),
                                       writes=(sBB,))
                            stj[j] = (sA, sAB, sB2, sBB, maA, maB)
                        if j >= 1:
                            jj = j - 1
                            sA, sAB, sB2, sBB, maA, maB = stj.pop(jj)
                            abank = pbk[5] if jj % 2 == 0 else pbk[1]
                            abB = B["pb5"] if jj % 2 == 0 else B["pb1"]
                            am1 = t_mm(abank[:, 256:384],
                                       Vb[:, jj * 1024 + h * 128: jj * 1024 + (h + 1) * 128],
                                       sA, True, False, deps=[maA],
                                       reads=(bV[jj], sAB), writes=(abB,))
                            am2 = t_mm(abank[:, 256:384],
                                       Vb[0:6, (jj + 1) * 1024 + h * 128: (jj + 1) * 1024 + (h + 1) * 128],
                                       sB2[0:6, 0:128], False, True, deps=[maB],
                                       reads=(bV[jj + 1], sBB), writes=(abB,))
                            t_copy("VE", AFMb[:, h * 512 + jj * 128: h * 512 + (jj + 1) * 128],
                                   abank[:, 256:384], deps=[am2], reads=(abB,),
                                   writes=(B["AFMb"],))
            g0 = Qs + 3
            for oc in range(8):
                wsl = oc % 2
                wtile = WOb[:, 0:1024] if wsl == 0 else WQKVb[:, 0:1024]
                wtB = B["WOb"] if wsl == 0 else B["WQKVb0"]
                ldo = dma("SY", wtile,
                          wout_bf.rearrange("(fc p) c -> p fc c", p=128)[:, :, oc * 128:(oc + 1) * 128],
                          deps=[pcs[-1]], writes=(wtB,), dsem=f"wo{wsl}")
                obank = pbk[4] if wsl == 0 else pbk[5]
                obB = B["pb4"] if wsl == 0 else B["pb5"]
                omm = None
                for d_ in range(FC):
                    omm = t_mm(obank[:, 0:512],
                               wtile[:, d_ * 128:(d_ + 1) * 128],
                               AFMb[:, d_ * 512:(d_ + 1) * 512], d_ == 0, d_ == FC - 1,
                               deps=[ldo], reads=(wtB, B["AFMb"]), writes=(obB,))
                t_stt("VE", xb_t[:, oc * TH + g0: oc * TH + g0 + 512], obank[:, 0:512],
                      bout_b[:, oc:oc + 1], xb_t[:, oc * TH + g0: oc * TH + g0 + 512],
                      ALU.add, ALU.add, deps=[omm],
                      reads=(obB, B["consts"]) + tuple(xbufs(g0, g0 + 512)),
                      writes=tuple(xbufs(g0, g0 + 512)))

    def tap(name):
        if not DEBUG_TAPS:
            return
        td = nc.dram_tensor(name, [D, TH], F32, kind="ExternalOutput").ap()
        for fc in range(FC):
            dma("SY", td[fc * 128:(fc + 1) * 128, :], xb_t[:, fc * TH: (fc + 1) * TH],
                reads=tuple(XR))

    dm1, dr1 = ln_prologue(FFN1_CH, R_M1, R_RSTD1)
    quant_p1(wgT_d, D, INNER, 0)
    qg.extend(quant_p2(wgT_d, wg_s, D, INNER, 0, "qgS", WGb[:, 0:2048]))
    quant_p1(wuT_d, D, INNER, 1)
    qu.extend(quant_p2(wuT_d, wu_s, D, INNER, 1, "quS", WUb[:, 0:2048]))
    quant_p1(wdT_d, INNER, D, 2)
    wsud = t_tt("VE", wsrow[0:1, 3:4], wsrow[0:1, 1:2], wsrow[0:1, 2:3], ALU.mult,
                reads=(B["wsrow"],), writes=(B["wsrow"],), risky=True)
    P.fence(PROSET, CHUNKSET)
    ffn_call(FFN1_CH, R_M1, dm1, dr1, dbg=True, emit_extra=[
        lambda: qd.extend(quant_p2(wdT_d, wd_s, INNER, D, 2, "qdS", WDb[:, 0:2048])),
        lambda: precast(wqkv_d, wqkv_bf, 0, 512, 3 * D, pcs),
        lambda: precast(wqkv_d, wqkv_bf, 512, D, 3 * D, pcs),
        lambda: precast(wout_d, wout_bf, 0, D, D, pcs)])
    tap("tap_x1")
    if DEBUG_TAPS:
        trows = nc.dram_tensor("tap_rows", [12, TH], F32, kind="ExternalOutput").ap()
        dma("SY", trows, rows_s, reads=tuple(B[f"rows{i}"] for i in range(12)))
        twg = nc.dram_tensor("tap_wg", [D, INNER], I8, kind="ExternalOutput").ap()
        dma("SY", twg, wg_s, reads=(B["wg_s"],))
    P.fence(CHUNKSET, PROSET)
    dm2, dr2 = ln_prologue(FFN1_CH, R_M2, R_RSTD2)
    P.fence(PROSET + CHUNKSET, ATTNSET)
    attn_phase(dm2, dr2)
    tap("tap_x2")
    P.fence(ATTNSET, PROSET)
    dm3, dr3 = ln_prologue(FFN2_CH, R_M3, R_RSTD3)
    P.fence(PROSET, CHUNKSET)
    ffn_call(FFN2_CH, R_M3, dm3, dr3)

    outs = []
    for fc in range(FC):
        st = dma("SY", y_d[fc * 128:(fc + 1) * 128, :], xb_t[:, fc * TH + 3: fc * TH + 3 + T],
                 reads=tuple(XR), dsem="out")
        outs.append(st)
    P.op("SY", lambda e: e.nop(), deps=[outs[-1]])

    dsem_names = set()
    for r in P.ops:
        if r.dma:
            dsem_names.add(r.dsem if r.dsem is not None else ("d" + r.eng))
    with ExitStack() as st:
        sems = {n: st.enter_context(nc.semaphore(n))
                for n in ["sTE", "sVE", "sSE", "sGP"] + sorted(dsem_names)}
        block = st.enter_context(nc.Block())
        P.finalize(block, sems)
    return nc


_NC_CACHE = {}


def make_inmaps(L0_attn_x, L0_attn_w_qkv, L0_attn_proj, L0_attn_w_out, L0_attn_b_out,
                L0_ffn_w_gate, L0_ffn_w_up, L0_ffn_w_down):
    x = np.asarray(L0_attn_x, dtype=np.float32)
    wqkv = np.ascontiguousarray(np.asarray(L0_attn_w_qkv, dtype=np.float32))
    proj = np.ascontiguousarray(np.asarray(L0_attn_proj, dtype=np.float32))
    wout = np.ascontiguousarray(np.asarray(L0_attn_w_out, dtype=np.float32))
    bout = np.ascontiguousarray(np.asarray(L0_attn_b_out, dtype=np.float32))
    wgT = np.ascontiguousarray(np.asarray(L0_ffn_w_gate, dtype=np.float32).T)
    wuT = np.ascontiguousarray(np.asarray(L0_ffn_w_up, dtype=np.float32).T)
    wdT = np.ascontiguousarray(np.asarray(L0_ffn_w_down, dtype=np.float32).T)

    Bsz, N, _ = x.shape
    eye = np.eye(128, dtype=np.float32)
    kp = np.arange(128)[:, None]
    jj = np.arange(128)[None, :]
    maskA = ((kp - jj >= 0) & (kp - jj <= 6)).astype(np.float32)
    kp8 = np.arange(8)[:, None]
    maskB = ((jj - kp8 >= 122) & (kp8 < 6)).astype(np.float32)
    ones_col = np.full((128, 1), 1.0 / 1024, dtype=ml_dtypes.bfloat16)
    ones_row = np.ones((1, 128), dtype=np.float32)
    pairmask = np.zeros((128, 2), dtype=ml_dtypes.bfloat16)
    pairmask[0:64, 0] = 1
    pairmask[64:128, 1] = 1

    in_maps = []
    for cix in range(NCORE):
        b = cix // 4
        s = 2048 * (cix % 4)
        lo = s - 3
        shard = np.zeros((TH, D), dtype=np.float32)
        a0, a1 = max(lo, 0), min(s + 2051, N)
        shard[a0 - lo:a1 - lo] = x[b, a0:a1]
        x_fm = np.ascontiguousarray(shard.T)
        mask = np.zeros(TQG, dtype=np.float32)
        tt = np.arange(TH)
        valid = (tt + lo >= 0) & (tt + lo < N)
        mask[0:TH][valid] = 1.0
        maskk = np.ascontiguousarray(np.broadcast_to(mask, (128, TQG))).astype(ml_dtypes.bfloat16)
        in_maps.append(dict(
            x_fm=x_fm, wqkv=wqkv, proj=proj, wout=wout, bout=bout,
            wgT=wgT, wuT=wuT, wdT=wdT, maskk=maskk, maskA=maskA, maskB=maskB,
            eye=eye, ones_col=ones_col, ones_row=ones_row, pairmask=pairmask))

    return in_maps, Bsz, N


def kernel(**inputs):
    in_maps, Bsz, N = make_inmaps(**inputs)
    if "nc" not in _NC_CACHE:
        _NC_CACHE["nc"] = build_nc()
    nc = _NC_CACHE["nc"]
    res = run_bass_kernel_spmd(nc, in_maps, core_ids=list(range(NCORE)))
    out = np.empty((Bsz, N, D), dtype=np.float32)
    for cix in range(NCORE):
        b = cix // 4
        s = 2048 * (cix % 4)
        out[b, s:s + 2048] = res.results[cix]["y_fm"].T
    return out


# version 25
# speedup vs baseline: 1.3163x; 1.0724x over previous
import numpy as np
import ml_dtypes
import concourse.bass as bass
import concourse.mybir as mybir
from concourse.bass_utils import run_bass_kernel_spmd
from contextlib import ExitStack

F32 = mybir.dt.float32
BF16 = mybir.dt.bfloat16
I8 = mybir.dt.int8
AF = mybir.ActivationFunctionType
ALU = mybir.AluOpType
AX = mybir.AxisListType

C_RND = float(np.float32(1.5 * 2.0**23))
D = 1024
INNER = 4096
HEADS = 8
FEAT = 64
NCORE = 8
T = 2048
TH = 2054
TQG = 2176
FC = 8
EPS = 1e-5
SW = 264

FFN1_CH = [(0, 259)] + [(259 + 256 * i, 515 + 256 * i) for i in range(6)] + [(1795, 2054)]
FFN2_CH = [(3 + 256 * i, 259 + 256 * i) for i in range(8)]


class Rec:
    __slots__ = ("idx", "eng", "emit", "deps", "dma", "risky", "needs_inc", "sem", "val",
                 "dsem")

    def __init__(self, idx, eng, emit, deps, dma, risky):
        self.idx, self.eng, self.emit = idx, eng, emit
        self.deps, self.dma, self.risky = deps, dma, risky
        self.needs_inc = False
        self.sem = None
        self.val = None


class Buf:
    def __init__(self, name):
        self.name = name
        self.last_write = None
        self.readers = []


class Prog:
    ENGS = ("TE", "VE", "SE", "GP", "SY")

    def __init__(self):
        self.ops = []

    def op(self, eng, emit, deps=(), reads=(), writes=(), dma=False, risky=False,
           dsem=None):
        dl = [d for d in deps if d is not None]
        for b in reads:
            if b.last_write is not None:
                dl.append(b.last_write)
        for b in writes:
            if b.last_write is not None:
                dl.append(b.last_write)
            dl.extend(b.readers)
        r = Rec(len(self.ops), eng, emit, dl, dma, risky)
        r.dsem = dsem
        self.ops.append(r)
        for b in reads:
            b.readers.append(r)
        for b in writes:
            b.last_write = r
            b.readers = []
        return r

    def fence(self, from_bufs, to_bufs):
        deps = []
        for b in from_bufs:
            if b.last_write is not None:
                deps.append(b.last_write)
            deps.extend(b.readers)
        rec = self.op("VE", lambda e: e.engine_nop(), deps=deps, risky=True)
        for b in to_bufs:
            b.last_write = rec
            b.readers = []
        return rec

    def finalize(self, block, sems):
        for r in self.ops:
            for d in r.deps:
                if d.dma:
                    continue
                if d.eng == r.eng and not d.risky:
                    continue
                d.needs_inc = True
        cnt = {e: 0 for e in self.ENGS}
        dcnt = {}
        for r in self.ops:
            if r.dma:
                nm = r.dsem if r.dsem is not None else ("d" + r.eng)
                dcnt[nm] = dcnt.get(nm, 0) + 16
                r.sem = nm
                r.val = dcnt[nm]
            elif r.needs_inc:
                cnt[r.eng] += 1
                r.sem = "s" + r.eng
                r.val = cnt[r.eng]
        self.dma_sems = sorted(dcnt.keys())
        for eng_name, starter in (("TE", block.tensor), ("VE", block.vector),
                                  ("SE", block.scalar), ("GP", block.gpsimd),
                                  ("SY", block.sync)):
            my = [r for r in self.ops if r.eng == eng_name]
            if not my:
                continue

            def body(e, my=my):
                last_wait = {}
                for r in my:
                    need = {}
                    for d in r.deps:
                        if d.sem is None:
                            continue
                        if d.eng == r.eng and not d.dma and not d.risky:
                            continue
                        if last_wait.get(d.sem, -1) >= d.val:
                            continue
                        if need.get(d.sem, -1) < d.val:
                            need[d.sem] = d.val
                    for sm, v in need.items():
                        e.wait_ge(sems[sm], v)
                        last_wait[sm] = v
                    inst = r.emit(e)
                    if r.sem is not None:
                        inst.then_inc(sems[r.sem], 16 if r.dma else 1)
            starter(body)


def secs(width, step=128):
    out, off = [], 0
    while off < width:
        w = min(step, width - off)
        out.append((off, w))
        off += w
    return out


DEBUG_TAPS = False


def build_nc():
    nc = bass.Bass("TRN2", target_bir_lowering=False)
    P = Prog()

    din = lambda n, s, dt=F32: nc.dram_tensor(n, s, dt, kind="ExternalInput").ap()
    x_d = din("x_fm", [D, TH])
    wqkv_d = din("wqkv", [D, 3 * D])
    proj_d = din("proj", [128, FEAT])
    wout_d = din("wout", [D, D])
    bout_d = din("bout", [D])
    wgT_d = din("wgT", [D, INNER])
    wuT_d = din("wuT", [D, INNER])
    wdT_d = din("wdT", [INNER, D])
    maskk_d = din("maskk", [128, TQG], BF16)
    maskA_d = din("maskA", [128, 128])
    maskB_d = din("maskB", [8, 128])
    eye_d = din("eye", [128, 128])
    ones_col_d = din("ones_col", [128, 1], BF16)
    ones_row_d = din("ones_row", [1, 128])
    pairmask_d = din("pairmask", [128, 2], BF16)
    y_d = nc.dram_tensor("y_fm", [D, T], F32, kind="ExternalOutput").ap()

    scr = lambda n, s, dt: nc.dram_tensor(n, s, dt, kind="Internal").ap()
    wg_s = scr("wg_s", [D, INNER], BF16)
    wu_s = scr("wu_s", [D, INNER], BF16)
    wd_s = scr("wd_s", [INNER, D], BF16)
    wqkv_bf = scr("wqkv_bf", [D, 3 * D], BF16)
    wout_bf = scr("wout_bf", [D, D], BF16)
    rows_s = scr("rows_s", [12, TH], F32)
    zrow_s = scr("zrow_s", [2, 512], BF16)
    R_M1, R_RSTD1, R_M2, R_RSTD2, R_M3, R_RSTD3, R_SU, R_FG, R_FDU, R_FD, R_SH = range(11)

    xb_t = nc.alloc_sbuf_tensor("xb", [128, FC * TH], F32)
    PF = nc.alloc_sbuf_tensor("PF", [128, 13440], F32)
    pcstage = nc.alloc_sbuf_tensor("pcstage", [128, 1024], BF16)
    PB = nc.alloc_sbuf_tensor("PB", [128, 29056], BF16)
    w8stage = nc.alloc_sbuf_tensor("w8stage", [128, 2048], I8)
    u8roll = nc.alloc_sbuf_tensor("u8roll", [128, 2 * SW], I8)
    h8roll = nc.alloc_sbuf_tensor("h8roll", [128, 2 * SW], I8)
    rstdb = nc.alloc_sbuf_tensor("rstdb", [1, TH], F32)
    BCb = nc.alloc_sbuf_tensor("BCb", [128, 5 * SW], F32)
    rowb = nc.alloc_sbuf_tensor("rowb", [1, 8 * SW], F32)
    colb = nc.alloc_sbuf_tensor("colb", [128, 72], F32)
    maskk_b = nc.alloc_sbuf_tensor("maskk_b", [128, TQG], BF16)
    maskA_b = nc.alloc_sbuf_tensor("maskA_b", [128, 128], F32)
    maskB_b = nc.alloc_sbuf_tensor("maskB_b", [8, 128], F32)
    eye_b = nc.alloc_sbuf_tensor("eye_b", [128, 128], F32)
    ones_col_b = nc.alloc_sbuf_tensor("ones_col_b", [128, 1], BF16)
    ones_row_b = nc.alloc_sbuf_tensor("ones_row_b", [1, 128], F32)
    pairmask_b = nc.alloc_sbuf_tensor("pairmask_b", [128, 2], BF16)
    bout_b = nc.alloc_sbuf_tensor("bout_b", [128, 8], F32)
    proj_b = nc.alloc_sbuf_tensor("proj_b", [128, FEAT], BF16)
    zq = nc.alloc_sbuf_tensor("zq", [128, 16], F32)
    wsc = nc.alloc_sbuf_tensor("wsc", [128, 4], F32)
    wsrow = nc.alloc_sbuf_tensor("wsrow", [1, 8], F32)
    sumc = nc.alloc_sbuf_tensor("sumc", [128, 8], F32)
    p1scr = nc.alloc_sbuf_tensor("p1scr", [128, 64], F32)
    print('SBUF remaining after allocs:', nc.sbuf_bytes_remaining)

    Hb = PF[:, 0:8448]
    vrow = PF[0:1, 0:TH]
    mrow = PF[0:1, TH:2 * TH]
    esqrow = PF[0:1, 2 * TH:3 * TH]
    Cb = PF[:, 8448:10560]
    GTb = PF[:, 10560:11616]
    MbU = PF[:, 11616:11880]
    WSTG = PF[:, 11880:13416]
    MbH = PF[:, 10560 + 3 * SW:10560 + 4 * SW]
    colbU = colb[:, 0:36]
    colbH = colb[:, 36:72]
    rowbU = rowb[0:1, 0:6 * SW]
    rowbH = rowb[0:1, 6 * SW:8 * SW]
    PHt = PF[:, 0:1280]
    PHt2 = PF[:, 1280:2560]
    zrowb = PF[0:8, 2560:3072]

    HIb = PB[:, 0:8448]
    XBFb = PB[:, 0:2112]
    X2Bb = PB[:, 2112:4224]
    UIb = PB[:, 8448:12672]
    wbfstage = PB[:, 8448:10496]
    WGb = PB[:, 12672:16768]
    WUb = PB[:, 16768:20864]
    WDb = PB[:, 20864:29056]
    U2b = PB[:, 0:5120]
    stashb = PB[:, 5120:6144]
    Vb = PB[:, 6144:11264]
    QPb = PB[:, 11264:13824]
    KPb = PB[:, 13824:16384]
    KWb = PB[:, 16384:17024]
    QRb = PB[:, 17024:18304]
    ZBb = PB[:, 18304:18848]
    AFMb = PB[:, 18848:22944]
    WQKVb = PB[:, 22944:24992]
    WVb = PB[:, 24992:27040]
    WOb = PB[:, 27040:28064]
    SBA = PB[:, 28064:28192]
    SBB = PB[0:8, 28192:28320]

    pbk = [nc.alloc_psum_tensor(f"pb{i}", [128, 512], F32) for i in range(8)]

    names = (["Cb", "GTb", "MbU", "MbH", "WSTG", "rowbU", "rowbH", "colbU", "colbH", "colb", "rstdb", "vrow", "mrow",
              "esqrow", "wsc", "wsrow", "sumc", "p1V", "p1G", "p1q0", "p1q1", "p1q2", "p1q3", "consts", "w8stage", "w8stage1", "wbf0", "wbf1", "WSTG0", "WSTG1", "WSTG2", "WSTG3", "u8a", "u8b", "h8a", "h8b",
              "wqb0", "wqb1",
              "Hb", "HIb", "UIb0", "UIb1", "WGb0", "WGb1", "WUb0", "WUb1", "WDb0", "WDb1",
              "XBFb", "X2Bb", "PHt", "PHt2", "zrowb",
              "U2b", "stash", "Vb0", "Vb1", "Vb2", "Vb3", "Vb4", "QPb", "KPb",
              "KWb", "QRb", "ZBb", "AFMb", "WQKVb0", "WQKVb1", "WVb0", "WVb1",
              "WOb", "SBA", "SBB", "zq", "zrow_s",
              "wg_s", "wu_s", "wd_s"]
             + [f"BC{i}" for i in range(5)]
             + [f"pb{i}" for i in range(8)]
             + [f"rows{i}" for i in range(12)])
    B = {n: Buf(n) for n in names}
    XR = [Buf(f"x{i}") for i in range(17)]

    def xbufs(a, b):
        return [XR[i] for i in range(a // 128, min((b + 127) // 128, 17))]

    CHUNKSET = [B[n] for n in ["Hb", "HIb", "UIb0", "UIb1", "WGb0", "WGb1", "WUb0",
                               "WUb1", "WDb0", "WDb1", "Cb", "GTb", "MbU", "MbH",
                               "colbU", "colbH", "rowbU", "rowbH"]]
    PROSET = [B[n] for n in ["XBFb", "X2Bb", "vrow", "mrow", "esqrow", "colb"]]
    ATTNSET = [B[n] for n in ["U2b", "stash", "Vb0", "Vb1", "Vb2", "Vb3", "Vb4",
                              "QPb", "KPb", "KWb", "QRb", "ZBb", "AFMb",
                              "WQKVb0", "WQKVb1", "WVb0", "WVb1", "WOb",
                              "SBA", "SBB", "PHt", "PHt2", "zrowb", "WSTG"]]

    def dma(eng, out, in_, reads=(), writes=(), deps=(), dsem=None):
        return P.op(eng, lambda e, o=out, i=in_: e.dma_start(out=o, in_=i),
                    deps=deps, reads=reads, writes=writes, dma=True, dsem=dsem)

    def t_copy(eng, out, in_, **kw):
        if eng == "SE":
            return P.op(eng, lambda e, o=out, i=in_: e.copy(out=o, in_=i), **kw)
        return P.op(eng, lambda e, o=out, i=in_: e.tensor_copy(out=o, in_=i), **kw)

    def t_tt(eng, out, in0, in1, op, **kw):
        return P.op(eng, lambda e, o=out, a=in0, b=in1, p=op:
                    e.tensor_tensor(out=o, in0=a, in1=b, op=p), **kw)

    def t_ts(eng, out, in0, s1, s2, op0, op1=None, **kw):
        if op1 is None:
            return P.op(eng, lambda e, o=out, i=in0, a=s1, p0=op0:
                        e.tensor_single_scalar(out=o, in_=i, scalar=a, op=p0), **kw)
        return P.op(eng, lambda e, o=out, i=in0, a=s1, b=s2, p0=op0, p1=op1:
                    e.tensor_scalar(out=o, in0=i, scalar1=a, scalar2=b, op0=p0, op1=p1), **kw)

    def t_red(eng, out, in_, op, absval=False, **kw):
        return P.op(eng, lambda e, o=out, i=in_, p=op, a=absval:
                    e.tensor_reduce(out=o, in_=i, axis=AX.X, op=p, apply_absolute_value=a), **kw)

    def t_act(out, in_, func, bias=0.0, scale=1.0, **kw):
        return P.op("SE", lambda e, o=out, i=in_, f=func, b=bias, s=scale:
                    e.activation(out=o, in_=i, func=f, bias=b, scale=s), **kw)

    def t_mm(out, lhsT, rhs, start, stop, **kw):
        return P.op("TE", lambda e, o=out, l=lhsT, r=rhs, st=start, sp=stop:
                    e.matmul(o, l, r, start=st, stop=sp), **kw)

    def t_tr(out, in_, **kw):
        pd = in_.partition_size
        pd = pd() if callable(pd) else pd
        return P.op("TE", lambda e, o=out, i=in_, p=pd:
                    e.transpose(o, i, eye_b[0:p, 0:p]), **kw)

    def t_stt(eng, out, in0, scalar, in1, op0, op1, **kw):
        return P.op(eng, lambda e, o=out, a=in0, s=scalar, b=in1, p0=op0, p1=op1:
                    e.scalar_tensor_tensor(out=o, in0=a, scalar=s, in1=b, op0=p0, op1=p1), **kw)

    def t_recip(eng, out, in_, **kw):
        return P.op(eng, lambda e, o=out, i=in_: e.reciprocal(out=o, in_=i), **kw)

    def t_memset(eng, ap, val, **kw):
        return P.op(eng, lambda e, a=ap, v=val: e.memset(a, v), **kw)

    def bcast(dram_row_slice, dst_tile, writes, deps=(), dsem=None):
        return dma("SE", dst_tile, dram_row_slice.partition_broadcast(128),
                   deps=deps, writes=writes, dsem=dsem)

    def rowslice(slot, a, b):
        return rows_s[slot, a:b].rearrange("(one t) -> one t", one=1)

    c = B["consts"]
    for dst, src in [(eye_b[:, :], eye_d), (maskA_b[:, :], maskA_d),
                     (maskB_b[:, :], maskB_d), (maskk_b[:, :], maskk_d),
                     (ones_col_b[:, :], ones_col_d), (ones_row_b[:, :], ones_row_d),
                     (pairmask_b[:, :], pairmask_d)]:
        dma("SY", dst, src, writes=(c,), dsem="cst")
    dma("GP", proj_b[:, :], proj_d, writes=(c,), dsem="cst2")
    for oc in range(8):
        dma("SY", bout_b[:, oc:oc + 1],
            bout_d[oc * 128:(oc + 1) * 128].rearrange("(p one) -> p one", one=1),
            writes=(c,), dsem="cst")
    dma("SY", xb_t[:, :].rearrange("p (fc t) -> p fc t", fc=FC),
        x_d.rearrange("(fc p) t -> p fc t", p=128), writes=tuple(XR), dsem="xld")

    def quant_p1(w_d, nrows, ncols, ws_idx):
        hw = 512
        tiles = [(r, h) for r in range(nrows // 128) for h in range(ncols // hw)]
        nt = len(tiles)
        NS = 3
        for i, (r, h) in enumerate(tiles):
            sl0 = (i % NS) * hw
            wB = B[f"WSTG{i % NS}"]
            src = w_d[r * 128:(r + 1) * 128, h * hw:(h + 1) * hw]
            dma("SY", WSTG[:, sl0:sl0 + hw], src, writes=(wB,), dsem=f"stg{i % NS}")
            t_red("VE", p1scr[:, i:i + 1], WSTG[:, sl0:sl0 + hw], ALU.add, absval=True,
                  reads=(wB,), writes=(B[f"p1q{i % 4}"],), risky=True)
        cb = t_red("VE", sumc[:, ws_idx:ws_idx + 1], p1scr[:, 0:nt], ALU.add,
                   reads=(B["p1q0"], B["p1q1"], B["p1q2"], B["p1q3"]),
                   writes=(B["sumc"],), risky=True)
        tr = t_tr(pbk[7][0:1, 0:128], sumc[:, ws_idx:ws_idx + 1],
                  deps=[cb], reads=(B["sumc"], B["consts"]), writes=(B["pb7"],))
        t_red("VE", wsrow[0:1, 4:5], pbk[7][0:1, 0:128], ALU.add,
              deps=[tr], reads=(B["pb7"],), writes=(B["wsrow"],), risky=True)
        t_ts("VE", wsrow[0:1, ws_idx:ws_idx + 1], wsrow[0:1, 4:5], 1.0 / (nrows * ncols),
             1e-5, ALU.mult, ALU.max, reads=(B["wsrow"],), writes=(B["wsrow"],), risky=True)
        t_recip("VE", wsrow[0:1, 5:6], wsrow[0:1, ws_idx:ws_idx + 1],
                reads=(B["wsrow"],), writes=(B["wsrow"],), risky=True)
        bc = t_mm(pbk[7][:, 128:129], ones_row_b[0:1, :], wsrow[0:1, 5:6], True, True,
                  reads=(B["wsrow"], B["consts"]), writes=(B["pb7"],))
        t_copy("VE", wsc[:, ws_idx:ws_idx + 1], pbk[7][:, 128:129],
               deps=[bc], reads=(B["pb7"],), writes=(B["wsc"],), risky=True)

    def quant_p2(w_d, w_scr, nrows, ncols, ws_idx, ssem, stage, q_load="SY"):
        hw = 512
        tiles = [(r, h) for r in range(nrows // 128) for h in range(ncols // hw)]
        NS = 3
        stores = []
        for i, (r, h) in enumerate(tiles):
            sl0 = (i % NS) * hw
            wB = B[f"WSTG{i % NS}"]
            w8B = B["w8stage"] if i % 2 == 0 else B["w8stage1"]
            s8 = (i % 2) * 1024
            eng = "VE"
            src = w_d[r * 128:(r + 1) * 128, h * hw:(h + 1) * hw]
            ldsem = f"stg{i % NS}" if q_load == "SY" else f"sg2{i % NS}"
            dma(q_load, WSTG[:, sl0:sl0 + hw], src, writes=(wB,), dsem=ldsem)
            t_ts(eng, WSTG[:, sl0:sl0 + hw], WSTG[:, sl0:sl0 + hw], wsc[:, ws_idx:ws_idx + 1],
                 1.0, ALU.mult, ALU.min, reads=(wB, B["wsc"]), writes=(wB,))
            t_ts(eng, WSTG[:, sl0:sl0 + hw], WSTG[:, sl0:sl0 + hw], -1.0, None,
                 ALU.max, reads=(wB,), writes=(wB,))
            t_copy("VE", w8stage[:, s8:s8 + hw], WSTG[:, sl0:sl0 + hw],
                   reads=(wB,), writes=(w8B,))
            wbfB = B["wbf0"] if i % 2 == 0 else B["wbf1"]
            t_copy("SE", stage[:, s8:s8 + hw], w8stage[:, s8:s8 + hw],
                   reads=(w8B,), writes=(wbfB,))
            st = dma("SE", w_scr[r * 128:(r + 1) * 128, h * hw:(h + 1) * hw],
                     stage[:, s8:s8 + hw], reads=(wbfB,), dsem=ssem)
            stores.append(st)
        return stores

    def precast(w_d, w_bf, rlo, rhi, ncols, stores):
        hw = 1024
        for r in range(rlo // 128, rhi // 128):
            for h in range(ncols // hw):
                dma("GP", pcstage[:, :],
                    w_d[r * 128:(r + 1) * 128, h * hw:(h + 1) * hw],
                    writes=(B["wqb0"],), dsem="pcL")
                st = dma("GP", w_bf[r * 128:(r + 1) * 128, h * hw:(h + 1) * hw],
                         pcstage[:, :], reads=(B["wqb0"],), dsem="pcS")
                stores.append(st)

    qd, qg, qu, pcs = [], [], [], []

    def ln_prologue(chunks, slot_m, slot_rstd):
        lo, hi = chunks[0][0], chunks[-1][1]
        for (c0, c1) in chunks:
            w = c1 - c0
            cps, sqs = [], []
            for fc in range(FC):
                cp = t_copy("VE", XBFb[:, fc * SW: fc * SW + w],
                            xb_t[:, fc * TH + c0: fc * TH + c0 + w],
                            reads=tuple(xbufs(c0, c1)), writes=(B["XBFb"],))
                cps.append(cp)
            for fc in range(FC):
                sq = t_act(X2Bb[:, fc * SW: fc * SW + w], XBFb[:, fc * SW: fc * SW + w],
                           AF.Square, deps=[cps[fc]], reads=(B["XBFb"],), writes=(B["X2Bb"],))
                sqs.append(sq)
            mm = None
            for fc in range(FC):
                mm = t_mm(pbk[5][0:1, 0:w], ones_col_b[:, 0:1], XBFb[:, fc * SW: fc * SW + w],
                          fc == 0, fc == FC - 1, deps=[cps[fc]],
                          reads=(B["XBFb"], B["consts"]), writes=(B["pb5"],))
            t_copy("VE", mrow[0:1, c0:c1], pbk[5][0:1, 0:w], deps=[mm],
                   reads=(B["pb5"],), writes=(B["mrow"],), risky=True)
            mm = None
            for fc in range(FC):
                mm = t_mm(pbk[6][0:1, 0:w], ones_col_b[:, 0:1], X2Bb[:, fc * SW: fc * SW + w],
                          fc == 0, fc == FC - 1, deps=[sqs[fc]],
                          reads=(B["X2Bb"], B["consts"]), writes=(B["pb6"],))
            t_copy("VE", esqrow[0:1, c0:c1], pbk[6][0:1, 0:w], deps=[mm],
                   reads=(B["pb6"],), writes=(B["esqrow"],), risky=True)
        t_tt("VE", vrow[0:1, lo:hi], mrow[0:1, lo:hi], mrow[0:1, lo:hi], ALU.mult,
             reads=(B["mrow"],), writes=(B["vrow"],), risky=True)
        t_tt("VE", vrow[0:1, lo:hi], esqrow[0:1, lo:hi], vrow[0:1, lo:hi], ALU.subtract,
             reads=(B["esqrow"], B["vrow"]), writes=(B["vrow"],), risky=True)
        sl = secs(hi - lo)
        ns = len(sl)
        assert ns <= 17
        for si, (off, w) in enumerate(sl):
            tr = t_tr(pbk[7][0:w, si:si + 1], vrow[0:1, lo + off:lo + off + w],
                      reads=(B["vrow"], B["consts"]), writes=(B["pb7"],))
        ev = t_copy("VE", colb[:, 0:ns], pbk[7][:, 0:ns],
                    reads=(B["pb7"],), writes=(B["colb"],), risky=True)
        t_ts("VE", colb[:, 0:ns], colb[:, 0:ns], EPS, None, ALU.add,
             reads=(B["colb"],), writes=(B["colb"],), risky=True)
        t_recip("VE", colb[:, 17:17 + ns], colb[:, 0:ns],
                reads=(B["colb"],), writes=(B["colb"],), risky=True)
        t_act(colb[:, 34:34 + ns], colb[:, 17:17 + ns], AF.Sqrt,
              reads=(B["colb"],), writes=(B["colb"],), risky=True)
        t_tt("VE", colb[:, 51:51 + ns], colb[:, 34:34 + ns], colb[:, 34:34 + ns], ALU.mult,
             reads=(B["colb"],), writes=(B["colb"],), risky=True)
        t_tt("VE", colb[:, 51:51 + ns], colb[:, 51:51 + ns], colb[:, 0:ns], ALU.mult,
             reads=(B["colb"],), writes=(B["colb"],), risky=True)
        t_ts("VE", colb[:, 51:51 + ns], colb[:, 51:51 + ns], -0.5, 1.5, ALU.mult, ALU.add,
             reads=(B["colb"],), writes=(B["colb"],), risky=True)
        t_tt("VE", colb[:, 34:34 + ns], colb[:, 34:34 + ns], colb[:, 51:51 + ns], ALU.mult,
             reads=(B["colb"],), writes=(B["colb"],), risky=True)
        for si, (off, w) in enumerate(sl):
            tr = t_tr(pbk[5][0:1, 0:w], colb[0:w, 34 + si:35 + si],
                      reads=(B["colb"], B["consts"]), writes=(B["pb5"],))
            t_copy("VE", rstdb[0:1, lo + off:lo + off + w], pbk[5][0:1, 0:w],
                   deps=[tr], reads=(B["pb5"],), writes=(B["rstdb"],), risky=True)
        dm = dma("SE", rowslice(slot_m, lo, hi), mrow[0:1, lo:hi],
                 reads=(B["mrow"],), writes=(B[f"rows{slot_m}"],), dsem=f"rs{slot_m}")
        dr = dma("SE", rowslice(slot_rstd, lo, hi), rstdb[0:1, lo:hi],
                 reads=(B["rstdb"],), writes=(B[f"rows{slot_rstd}"],), dsem=f"rs{slot_rstd}")
        return dm, dr

    def mx_chain(w, src_red, src_bufs, pband, pbB, colap, colB, rowap, rowB, mbap, mbB,
                 bdst, bdstB, pre_m1=None):
        if pre_m1 is None:
            ap, n_inner = src_red
            m1 = t_red("VE", mbap[:, 0:w], ap, ALU.max, absval=True,
                       reads=src_bufs, writes=(mbB,), risky=True)
        else:
            m1 = pre_m1
        sl = secs(w)
        ns = len(sl)
        for si, (off, sw_) in enumerate(sl):
            reg = 128 + (si % 2) * 128
            tr = t_tr(pband[0:sw_, reg:reg + 128], mbap[:, off:off + sw_],
                      deps=[m1], reads=(mbB, B["consts"]), writes=(pbB,))
            t_red("VE", colap[0:sw_, si:si + 1], pband[0:sw_, reg:reg + 128],
                  ALU.max, deps=[tr], reads=(pbB,), writes=(colB,), risky=True)
        t_ts("VE", colap[:, 3:3 + ns], colap[:, 0:ns], EPS, None,
             ALU.max, reads=(colB,), writes=(colB,), risky=True)
        t_recip("VE", colap[:, 6:6 + ns], colap[:, 3:3 + ns],
                reads=(colB,), writes=(colB,), risky=True)
        t_ts("VE", colap[:, 9:9 + ns], colap[:, 6:6 + ns],
             127.0, None, ALU.mult, reads=(colB,), writes=(colB,), risky=True)
        for si, (off, sw_) in enumerate(sl):
            tr = t_tr(pband[0:1, 0:sw_], colap[0:sw_, 9 + si:10 + si],
                      reads=(colB, B["consts"]), writes=(pbB,))
            t_copy("VE", rowap[0:1, off:off + sw_], pband[0:1, 0:sw_],
                   deps=[tr], reads=(pbB,), writes=(rowB,), risky=True)
            tr2 = t_tr(pband[0:1, 0:sw_], colap[0:sw_, 3 + si:4 + si],
                       reads=(colB, B["consts"]), writes=(pbB,))
            t_copy("VE", rowap[0:1, SW + off:SW + off + sw_], pband[0:1, 0:sw_],
                   deps=[tr2], reads=(pbB,), writes=(rowB,), risky=True)
        return mmbcast(rowap[0:1, 0:w], rowB, w, pband, pbB, bdst, bdstB)

    def mmbcast(row_ap, rowB, w, pband, pbB, dst, dstB):
        mm = t_mm(pband[:, 0:w], ones_row_b[0:1, :], row_ap, True, True,
                  reads=(rowB, B["consts"]), writes=(pbB,))
        return t_copy("VE", dst, pband[:, 0:w], deps=[mm],
                      reads=(pbB,), writes=(dstB,), risky=True)

    def ffn_call(chunks, slot_m, ev_m, ev_r, dbg=False, emit_extra=None):
        pend = []

        def emit_down(st_):
            c0, c1, w, rnd, bfd = st_["c0"], st_["c1"], st_["w"], st_["rnd"], st_["bfd"]
            for hc in range(8):
                ws_ = hc % 2
                wdB = B[f"WDb{ws_}"]
                wdt = WDb[:, ws_ * 4096:(ws_ + 1) * 4096]
                dma("SY", wdt,
                    wd_s.rearrange("(ic p) h -> p ic h", p=128)[:, :, hc * 128:(hc + 1) * 128],
                    deps=[qd[-1]], writes=(wdB,), dsem=f"wd{ws_}")
                bank = pbk[4 + hc % 2]
                bB = B[f"pb{4 + hc % 2}"]
                dmm = None
                for ic in range(32):
                    dmm = t_mm(bank[:, 0:w], wdt[:, ic * 128:(ic + 1) * 128],
                               HIb[:, ic * SW: ic * SW + w], ic == 0, ic == 31,
                               deps=[rnd[ic]], reads=(wdB, B["HIb"]), writes=(bB,))
                tt = t_tt("VE", GTb[:, 2 * SW:2 * SW + w], bank[:, 0:w],
                          BCb[:, 4 * SW:4 * SW + w], ALU.mult, deps=[dmm, bfd],
                          reads=(bB, B["BC4"]), writes=(B["GTb"],))
                t_tt("VE", xb_t[:, hc * TH + c0: hc * TH + c0 + w],
                     xb_t[:, hc * TH + c0: hc * TH + c0 + w], GTb[:, 2 * SW:2 * SW + w],
                     ALU.add, deps=[tt], reads=(B["GTb"],) + tuple(xbufs(c0, c1)),
                     writes=tuple(xbufs(c0, c1)))

        def dtap(name, src, srcbuf, shape, dt):
            if not (DEBUG_TAPS and dbg):
                return
            td = nc.dram_tensor(name, shape, dt, kind="ExternalOutput").ap()
            dma("SY", td, src, reads=(srcbuf,))
        qstate = {}

        def emit_quant(ci):
            c0, c1 = chunks[ci]
            w = c1 - c0
            ui = ci % 2
            uB = B[f"UIb{ui}"]
            uiT = UIb[:, ui * FC * SW:(ui + 1) * FC * SW]
            mb = bcast(rowslice(slot_m, c0, c1), BCb[:, 0:w], writes=(B["BC0"],),
                       deps=[ev_m], dsem="bc0")
            for fc in range(FC):
                t_tt("GP", Cb[:, fc * SW: fc * SW + w], xb_t[:, fc * TH + c0: fc * TH + c0 + w],
                     BCb[:, 0:w], ALU.subtract, deps=[mb],
                     reads=tuple(xbufs(c0, c1)) + (B["BC0"],), writes=(B["Cb"],))
            bsu = mx_chain(w, (Cb[:, 0:FC * SW].rearrange("p (fc t) -> p t fc", fc=FC)[:, 0:w, :], FC),
                           (B["Cb"],), pbk[6], B["pb6"], colbU, B["colbU"], rowbU, B["rowbU"],
                           MbU, B["MbU"], BCb[:, SW:SW + w], B["BC1"])
            t0 = t_tt("VE", rowbU[0:1, 2 * SW:2 * SW + w], rowbU[0:1, SW:SW + w],
                      rstdb[0:1, c0:c1], ALU.mult, deps=[ev_r],
                      reads=(B["rowbU"], B["rstdb"]), writes=(B["rowbU"],), risky=True)
            fg = t_ts("VE", rowbU[0:1, 3 * SW:3 * SW + w], rowbU[0:1, 2 * SW:2 * SW + w],
                      wsrow[0:1, 0:1], 1.0 / 127, ALU.mult, ALU.mult, deps=[wsud],
                      reads=(B["rowbU"], B["wsrow"]), writes=(B["rowbU"],), risky=True)
            fsl = 4 * SW + (ci % 2) * SW
            fdu = t_ts("VE", rowbU[0:1, fsl:fsl + w], rowbU[0:1, 2 * SW:2 * SW + w],
                       wsrow[0:1, 3:4], 0.5 / (127.0 * 127.0), ALU.mult, ALU.mult,
                       reads=(B["rowbU"], B["wsrow"]), writes=(B["rowbU"],), risky=True)
            bfg = mmbcast(rowbU[0:1, 3 * SW:3 * SW + w], B["rowbU"], w, pbk[6], B["pb6"],
                          BCb[:, 2 * SW:2 * SW + w], B["BC2"])
            for fc in range(FC):
                sl8 = (fc % 2) * SW
                u8B = B["u8a"] if fc % 2 == 0 else B["u8b"]
                t_tt("VE", Cb[:, fc * SW: fc * SW + w], Cb[:, fc * SW: fc * SW + w],
                     BCb[:, SW:SW + w], ALU.mult, deps=[bsu],
                     reads=(B["Cb"], B["BC1"]), writes=(B["Cb"],))
                t_copy("VE", u8roll[:, sl8:sl8 + w], Cb[:, fc * SW: fc * SW + w],
                       reads=(B["Cb"],), writes=(u8B,), risky=True)
                t_copy("SE", uiT[:, fc * SW: fc * SW + w], u8roll[:, sl8:sl8 + w],
                       reads=(u8B,), writes=(uB,), risky=True)
            ev_ui = P.ops[-1]
            qstate[ci] = dict(uiT=uiT, uB=uB, ev_ui=ev_ui, bfg=bfg, w=w, c0=c0, c1=c1)

        emit_quant(0)
        extras = list(emit_extra or [])
        for ci, (c0, c1) in enumerate(chunks):
            st0 = qstate[ci]
            uiT, uB, ev_ui, bfg, w = st0["uiT"], st0["uB"], st0["ev_ui"], st0["bfg"], st0["w"]
            for oc in range(32):
                pr = oc // 2
                j = oc % 2
                ws_ = pr % 2
                wgB, wuB = B[f"WGb{ws_}"], B[f"WUb{ws_}"]
                wgt = WGb[:, ws_ * 2048:(ws_ + 1) * 2048]
                wut = WUb[:, ws_ * 2048:(ws_ + 1) * 2048]
                if j == 0:
                    dma("SY", wgt,
                        wg_s.rearrange("(fc p) o -> p fc o", p=128)[:, :, pr * 256:(pr + 1) * 256],
                        deps=[qg[-1]], writes=(wgB,), dsem=f"wg{ws_}")
                    dma("SY", wut,
                        wu_s.rearrange("(fc p) o -> p fc o", p=128)[:, :, pr * 256:(pr + 1) * 256],
                        deps=[qu[-1]], writes=(wuB,), dsem=f"wu{ws_}")
                gbank = pbk[oc % 3]
                gB = B[f"pb{oc % 3}"]
                ubank = pbk[3 + oc % 2]
                uB2 = B[f"pb{3 + oc % 2}"]
                gmm = None
                for f in range(FC):
                    gmm = t_mm(gbank[:, 0:w], wgt[:, f * 256 + j * 128: f * 256 + j * 128 + 128],
                               uiT[:, f * SW: f * SW + w], f == 0, f == FC - 1,
                               deps=[ev_ui], reads=(wgB, uB), writes=(gB,))
                umm = None
                for f in range(FC):
                    umm = t_mm(ubank[:, 0:w], wut[:, f * 256 + j * 128: f * 256 + j * 128 + 128],
                               uiT[:, f * SW: f * SW + w], f == 0, f == FC - 1,
                               deps=[ev_ui], reads=(wuB, uB), writes=(uB2,))
                gt = GTb[:, (oc % 2) * SW:(oc % 2) * SW + w]
                g1 = t_tt("VE", gt, gbank[:, 0:w], BCb[:, 2 * SW:2 * SW + w], ALU.mult,
                          deps=[gmm, bfg], reads=(gB, B["BC2"]), writes=(B["GTb"],))
                sl_ = t_act(gt, gt, AF.Silu, deps=[g1], reads=(B["GTb"],), writes=(B["GTb"],))
                hh = t_tt("VE", Hb[:, oc * SW: oc * SW + w], ubank[:, 0:w], gt, ALU.mult,
                          deps=[umm, sl_], reads=(uB2, B["GTb"]), writes=(B["Hb"],))
                if oc == 15:
                    m2a = t_red("VE", MbH[:, 0:w],
                                Hb[:, 0:16 * SW].rearrange("p (oc t) -> p t oc", oc=16)[:, 0:w, :],
                                ALU.max, absval=True, deps=[hh], reads=(B["Hb"],),
                                writes=(B["MbH"],), risky=True)
                if oc == 31:
                    m2h = t_red("VE", MbU[:, 0:w],
                                Hb[:, 16 * SW:32 * SW].rearrange("p (oc t) -> p t oc", oc=16)[:, 0:w, :],
                                ALU.max, absval=True, deps=[hh], reads=(B["Hb"],),
                                writes=(B["MbU"],), risky=True)
                    m2b = t_tt("VE", MbH[:, 0:w], MbH[:, 0:w], MbU[:, 0:w], ALU.max,
                               deps=[m2a, m2h], reads=(B["MbH"], B["MbU"]),
                               writes=(B["MbH"],), risky=True)
            if ci == 0:
                dtap("tap_h", Hb[:, :], B["Hb"], [128, 32 * SW], F32)
            if ci + 1 < len(chunks):
                emit_quant(ci + 1)
            if pend:
                emit_down(pend.pop(0))
            if extras:
                extras.pop(0)()
            bsh = mx_chain(w, None, (B["Hb"],), pbk[7], B["pb7"], colbH, B["colbH"],
                           rowbH, B["rowbH"], MbH, B["MbH"],
                           BCb[:, 3 * SW:3 * SW + w], B["BC3"], pre_m1=m2b)
            fsl2 = 4 * SW + (ci % 2) * SW
            fd = t_tt("VE", rowbH[0:1, SW:SW + w], rowbH[0:1, SW:SW + w],
                      rowbU[0:1, fsl2:fsl2 + w], ALU.mult,
                      reads=(B["rowbH"], B["rowbU"]), writes=(B["rowbH"],), risky=True)
            bfd = mmbcast(rowbH[0:1, SW:SW + w], B["rowbH"], w, pbk[7], B["pb7"],
                          BCb[:, 4 * SW:4 * SW + w], B["BC4"])
            rnd = []
            for oc in range(32):
                sl8 = (oc % 2) * SW
                h8B = B["h8a"] if oc % 2 == 0 else B["h8b"]
                t_tt("VE", Hb[:, oc * SW: oc * SW + w], Hb[:, oc * SW: oc * SW + w],
                     BCb[:, 3 * SW:3 * SW + w], ALU.mult, deps=[bsh],
                     reads=(B["Hb"], B["BC3"]), writes=(B["Hb"],))
                t_copy("VE", h8roll[:, sl8:sl8 + w], Hb[:, oc * SW: oc * SW + w],
                       reads=(B["Hb"],), writes=(h8B,), risky=True)
                rn = t_copy("SE", HIb[:, oc * SW: oc * SW + w], h8roll[:, sl8:sl8 + w],
                            reads=(h8B,), writes=(B["HIb"],), risky=True)
                rnd.append(rn)
            if ci == 0:
                dtap("tap_hi", HIb[:, :], B["HIb"], [128, 32 * SW], BF16)
            pend.append(dict(c0=c0, c1=c1, w=w, rnd=rnd, bfd=bfd))
        emit_down(pend.pop(0))

    def attn_phase(ev_m2, ev_r2):
        bV = [B[f"Vb{i}"] for i in range(5)]
        for q in range(4):
            Qs = 512 * q
            new0 = 0 if q == 0 else 128
            hi_local = min(640, TH - Qs)
            if q > 0:
                for fc in range(FC):
                    t_copy("SE", U2b[:, fc * 640: fc * 640 + 128],
                           stashb[:, fc * 128:(fc + 1) * 128],
                           reads=(B["stash"],), writes=(B["U2b"],), risky=True)
                t_copy("SE", Vb[:, 0:1024], Vb[:, 4 * 1024:5 * 1024],
                       reads=(bV[4],), writes=(bV[0],), risky=True)
                for g in range(4):
                    t_copy("SE", KPb[:, g * 640:g * 640 + 128],
                           KPb[:, g * 640 + 512:g * 640 + 640],
                           reads=(B["KPb"],), writes=(B["KPb"],), risky=True)
            g0, g1_ = Qs + new0, Qs + hi_local
            nw = g1_ - g0
            bm2 = bcast(rows_s[R_M2, g0:g1_].rearrange("(one t) -> one t", one=1),
                        PHt2[:, 0:nw], writes=(B["PHt2"],), deps=[ev_m2], dsem="ph2")
            br2 = bcast(rows_s[R_RSTD2, g0:g1_].rearrange("(one t) -> one t", one=1),
                        PHt2[:, 640:640 + nw], writes=(B["PHt2"],), deps=[ev_r2], dsem="ph2")
            u2evs = []
            for fc in range(FC):
                s1 = t_tt("GP", PHt[:, 0:nw], xb_t[:, fc * TH + g0: fc * TH + g1_],
                          PHt2[:, 0:nw], ALU.subtract, deps=[bm2],
                          reads=tuple(xbufs(g0, g1_)) + (B["PHt2"],), writes=(B["PHt"],))
                s2 = t_tt("GP", U2b[:, fc * 640 + new0: fc * 640 + hi_local], PHt[:, 0:nw],
                          PHt2[:, 640:640 + nw], ALU.mult, deps=[br2, s1],
                          reads=(B["PHt"], B["PHt2"]), writes=(B["U2b"],))
                u2evs.append(s2)
            if hi_local < 640:
                for fc in range(FC):
                    u2evs.append(t_memset("VE", U2b[:, fc * 640 + hi_local: fc * 640 + 640],
                                          0.0, writes=(B["U2b"],), risky=True))
            ev_u2 = u2evs[-1]
            if q < 3:
                for fc in range(FC):
                    t_copy("SE", stashb[:, fc * 128:(fc + 1) * 128],
                           U2b[:, fc * 640 + 512: fc * 640 + 640],
                           deps=[ev_u2], reads=(B["U2b"],), writes=(B["stash"],), risky=True)
            tl_range = list(range(0 if q == 0 else 1, 5))
            for tl in tl_range:
                if min(128, max(0, TH - (Qs + tl * 128))) < 128:
                    t_memset("VE", Vb[:, tl * 1024:(tl + 1) * 1024], 0.0,
                             writes=(bV[tl],), risky=True)
            for vb in range(4):
                if vb % 2 == 0:
                    vreg, vbufs = WVb[:, 0:2048], (B["WVb0"], B["WVb1"])
                else:
                    vreg, vbufs = WQKVb[:, 0:2048], (B["WQKVb0"], B["WQKVb1"])
                ldv = dma("SY", vreg,
                          wqkv_bf.rearrange("(fc p) c -> p fc c", p=128)[:, :, 2 * D + vb * 256: 2 * D + (vb + 1) * 256],
                          deps=[pcs[-1]], writes=vbufs, dsem="wv")
                for tl in tl_range:
                    gt0 = Qs + tl * 128
                    m_rows = min(128, max(0, TH - gt0))
                    if m_rows == 0:
                        continue
                    vbank = pbk[0] if tl % 2 == 0 else pbk[1]
                    vbB = B["pb0"] if tl % 2 == 0 else B["pb1"]
                    vmm = None
                    for f in range(FC):
                        vmm = t_mm(vbank[0:m_rows, 0:256],
                                   U2b[:, f * 640 + tl * 128: f * 640 + tl * 128 + m_rows],
                                   vreg[:, f * 256: f * 256 + 256],
                                   f == 0, f == FC - 1,
                                   deps=[ev_u2, ldv], reads=(B["U2b"],) + vbufs,
                                   writes=(vbB,))
                    t_copy("VE", Vb[0:m_rows, tl * 1024 + vb * 256: tl * 1024 + vb * 256 + 256],
                           vbank[0:m_rows, 0:256], deps=[vmm], reads=(vbB,), writes=(bV[tl],))
            q_lo, q_hi = 3, 515
            phi_evs = []
            for h in range(HEADS):
                g = h // 2
                rbase = (h % 2) * 64
                for which in ("q", "k"):
                    if which == "q":
                        ranges = [(q_lo, q_hi)]
                    else:
                        mid = min(new0 + 512, hi_local)
                        ranges = [(new0, mid)]
                        if mid < hi_local:
                            ranges.append((mid, hi_local))
                    slotn = h % 2
                    wB = B[f"WQKVb{slotn}"]
                    base = 0 if which == "q" else D
                    qi = h * 2 + (0 if which == "q" else 1)
                    rbank, rbB = (pbk[1], B["pb1"]) if qi % 2 == 0 else (pbk[3], B["pb3"])
                    phbank, phB = (pbk[2], B["pb2"]) if qi % 2 == 0 else (pbk[4], B["pb4"])
                    ldw = dma("SY", WQKVb[:, slotn * 1024: slotn * 1024 + 1024],
                              wqkv_bf.rearrange("(fc p) c -> p fc c", p=128)[:, :, base + h * 128: base + h * 128 + 128],
                              deps=[pcs[-1]], writes=(wB,), dsem=f"qk{slotn}")
                    for clo, chi in ranges:
                        cw = chi - clo
                        rmm = None
                        for f in range(FC):
                            rmm = t_mm(rbank[:, 0:cw],
                                       WQKVb[:, slotn * 1024 + f * 128: slotn * 1024 + f * 128 + 128],
                                       U2b[:, f * 640 + clo: f * 640 + chi], f == 0, f == FC - 1,
                                       deps=[ldw, ev_u2], reads=(wB, B["U2b"]), writes=(rbB,))
                        rb = QRb[:, (0 if which == "q" else 1) * 640 + clo:
                                 (0 if which == "q" else 1) * 640 + chi]
                        ev_r = t_copy("VE", rb, rbank[:, 0:cw], deps=[rmm],
                                      reads=(rbB,), writes=(B["QRb"],))
                        ph = t_mm(phbank[0:FEAT, 0:cw], proj_b[:, :], rb, True, True,
                                  deps=[ev_r], reads=(B["QRb"], B["consts"]), writes=(phB,))
                        e1 = t_ts("VE", PHt[0:FEAT, 0:cw], phbank[0:FEAT, 0:cw], 0.0, None,
                                  ALU.min, deps=[ph], reads=(phB,), writes=(B["PHt"],))
                        e2 = t_act(PHt2[0:FEAT, 0:cw], PHt[0:FEAT, 0:cw], AF.Exp,
                                   deps=[e1], reads=(B["PHt"],), writes=(B["PHt2"],))
                        e3 = t_ts("VE", PHt[0:FEAT, 0:cw], phbank[0:FEAT, 0:cw], 0.0, None,
                                  ALU.max, deps=[ph], reads=(phB,), writes=(B["PHt"],))
                        dstb, dB = (QPb, B["QPb"]) if which == "q" else (KPb, B["KPb"])
                        if rbase == 0:
                            dst = dstb[0:64, g * 640 + clo: g * 640 + chi]
                            mk = maskk_b[0:64, Qs + clo:Qs + chi]
                        else:
                            dst = dstb[64:128, g * 640 + clo: g * 640 + chi]
                            mk = maskk_b[0:64, Qs + clo:Qs + chi]
                        if which == "q":
                            e4 = t_tt("VE", dst, PHt2[0:FEAT, 0:cw], PHt[0:FEAT, 0:cw], ALU.add,
                                      deps=[e2, e3], reads=(B["PHt"], B["PHt2"]), writes=(dB,),
                                      risky=True)
                            phi_evs.append(e4)
                        else:
                            e4 = t_tt("VE", PHt[0:FEAT, 0:cw], PHt2[0:FEAT, 0:cw],
                                      PHt[0:FEAT, 0:cw], ALU.add, deps=[e2, e3],
                                      reads=(B["PHt"], B["PHt2"]), writes=(B["PHt"],), risky=True)
                            e5 = t_tt("VE", dst, PHt[0:FEAT, 0:cw], mk, ALU.mult,
                                      reads=(B["PHt"], B["consts"]), writes=(dB,), risky=True)
                            phi_evs.append(e5)
            if hi_local < 640:
                for g in range(4):
                    phi_evs.append(t_memset("VE", KPb[:, g * 640 + hi_local:(g + 1) * 640],
                                            0.0, writes=(B["KPb"],), risky=True))
            ev_phi = phi_evs[-1]
            for g in range(4):
                kp = KPb[:, g * 640:(g + 1) * 640]
                kw = KWb[:, 0:520]
                a1 = t_tt("VE", kw, kp[:, 0:520], kp[:, 1:521], ALU.add, deps=[ev_phi],
                          reads=(B["KPb"],), writes=(B["KWb"],), risky=True)
                a2 = t_tt("VE", PHt[:, 0:520], kp[:, 2:522], kp[:, 3:523], ALU.add,
                          deps=[ev_phi], reads=(B["KPb"],), writes=(B["PHt"],), risky=True)
                a3 = t_tt("VE", PHt2[:, 0:520], kp[:, 4:524], kp[:, 5:525], ALU.add,
                          deps=[ev_phi], reads=(B["KPb"],), writes=(B["PHt2"],), risky=True)
                t_tt("VE", kw, kw, PHt[:, 0:520], ALU.add,
                     reads=(B["KWb"], B["PHt"]), writes=(B["KWb"],), risky=True)
                t_tt("VE", kw, kw, PHt2[:, 0:520], ALU.add,
                     reads=(B["KWb"], B["PHt2"]), writes=(B["KWb"],), risky=True)
                a6 = t_tt("VE", kw, kw, kp[:, 6:526], ALU.add,
                          reads=(B["KWb"], B["KPb"]), writes=(B["KWb"],), risky=True)
                pp = t_tt("VE", QRb[:, 0:512], QPb[:, g * 640 + q_lo: g * 640 + q_hi],
                          kw[:, 0:512], ALU.mult, deps=[a6, ev_phi],
                          reads=(B["QPb"], B["KWb"]), writes=(B["QRb"],), risky=True)
                zmm = []
                for b_ in range(4):
                    mm = t_mm(pbk[3][:, b_ * 2: b_ * 2 + 2], QRb[:, b_ * 128:(b_ + 1) * 128],
                              pairmask_b[:, :], True, True, deps=[pp],
                              reads=(B["QRb"], B["consts"]), writes=(B["pb3"],))
                    zmm.append(mm)
                t_ts("VE", zq[:, 0:8], pbk[3][:, 0:8], 1e-8, None, ALU.add,
                     deps=[zmm[-1]], reads=(B["pb3"],), writes=(B["zq"],), risky=True)
                t_recip("VE", zq[:, 8:16], zq[:, 0:8], reads=(B["zq"],), writes=(B["zq"],),
                        risky=True)
                sc_evs = []
                for h2 in range(2):
                    for b_ in range(4):
                        col = 8 + b_ * 2 + h2
                        tr = t_tr(pbk[3][0:1, b_ * 128:128 + b_ * 128],
                                  zq[:, col:col + 1],
                                  reads=(B["zq"], B["consts"]), writes=(B["pb3"],))
                    ev = t_copy("VE", rowbU[0:1, 0:512], pbk[3][0:1, 0:512],
                                reads=(B["pb3"],), writes=(B["rowbU"],), risky=True)
                    mm = t_mm(pbk[3][:, 0:512] if False else pbk[2][:, 0:512],
                              ones_row_b[0:1, :], rowbU[0:1, 0:512], True, True,
                              deps=[ev], reads=(B["rowbU"], B["consts"]), writes=(B["pb2"],))
                    ev2 = t_copy("VE", ZBb[:, 0:512], pbk[2][:, 0:512], deps=[mm],
                                 reads=(B["pb2"],), writes=(B["ZBb"],), risky=True)
                    rb_ = h2 * 64
                    sc_evs.append(t_tt("VE", QPb[rb_:rb_ + 64, g * 640 + q_lo: g * 640 + q_hi],
                                       QPb[rb_:rb_ + 64, g * 640 + q_lo: g * 640 + q_hi],
                                       ZBb[rb_:rb_ + 64, 0:512], ALU.mult, deps=[ev2],
                                       reads=(B["QPb"], B["ZBb"]), writes=(B["QPb"],),
                                       risky=True))
                sc_ = sc_evs[-1]
                for h2 in range(2):
                    rbase = h2 * 64
                    h = 2 * g + h2
                    stj = {}
                    for j in range(5):
                        if j < 4:
                            qs = j * 128 + 3
                            sbank = pbk[6] if j % 2 == 0 else pbk[0]
                            sbB = B["pb6"] if j % 2 == 0 else B["pb0"]
                            sA = SBA[:, 0:128] if j % 2 == 0 else KWb[:, 0:128]
                            sAB = B["SBA"] if j % 2 == 0 else B["KWb"]
                            sB2 = SBB[0:8, 0:128] if j % 2 == 0 else KWb[0:8, 128:256]
                            sBB = B["SBB"] if j % 2 == 0 else B["KWb"]
                            sa = t_mm(sbank[:, 0:128],
                                      KPb[rbase:rbase + 64, g * 640 + j * 128: g * 640 + (j + 1) * 128],
                                      QPb[rbase:rbase + 64, g * 640 + qs: g * 640 + qs + 128],
                                      True, True, deps=[sc_], reads=(B["KPb"], B["QPb"]),
                                      writes=(sbB,))
                            sb_ = t_mm(sbank[0:8, 128:256],
                                       KPb[rbase:rbase + 64, g * 640 + (j + 1) * 128: g * 640 + (j + 1) * 128 + 8],
                                       QPb[rbase:rbase + 64, g * 640 + qs: g * 640 + qs + 128],
                                       True, True, deps=[sc_], reads=(B["KPb"], B["QPb"]),
                                       writes=(sbB,))
                            maA = t_tt("VE", sA, sbank[:, 0:128], maskA_b[:, :],
                                       ALU.mult, deps=[sa], reads=(sbB, B["consts"]),
                                       writes=(sAB,))
                            maB = t_tt("VE", sB2, sbank[0:8, 128:256], maskB_b[:, :],
                                       ALU.mult, deps=[sb_], reads=(sbB, B["consts"]),
                                       writes=(sBB,))
                            stj[j] = (sA, sAB, sB2, sBB, maA, maB)
                        if j >= 1:
                            jj = j - 1
                            sA, sAB, sB2, sBB, maA, maB = stj.pop(jj)
                            abank = pbk[5] if jj % 2 == 0 else pbk[1]
                            abB = B["pb5"] if jj % 2 == 0 else B["pb1"]
                            am1 = t_mm(abank[:, 256:384],
                                       Vb[:, jj * 1024 + h * 128: jj * 1024 + (h + 1) * 128],
                                       sA, True, False, deps=[maA],
                                       reads=(bV[jj], sAB), writes=(abB,))
                            am2 = t_mm(abank[:, 256:384],
                                       Vb[0:6, (jj + 1) * 1024 + h * 128: (jj + 1) * 1024 + (h + 1) * 128],
                                       sB2[0:6, 0:128], False, True, deps=[maB],
                                       reads=(bV[jj + 1], sBB), writes=(abB,))
                            t_copy("VE", AFMb[:, h * 512 + jj * 128: h * 512 + (jj + 1) * 128],
                                   abank[:, 256:384], deps=[am2], reads=(abB,),
                                   writes=(B["AFMb"],))
            g0 = Qs + 3
            for oc in range(8):
                wsl = oc % 2
                wtile = WOb[:, 0:1024] if wsl == 0 else WQKVb[:, 0:1024]
                wtB = B["WOb"] if wsl == 0 else B["WQKVb0"]
                ldo = dma("SY", wtile,
                          wout_bf.rearrange("(fc p) c -> p fc c", p=128)[:, :, oc * 128:(oc + 1) * 128],
                          deps=[pcs[-1]], writes=(wtB,), dsem=f"wo{wsl}")
                obank = pbk[4] if wsl == 0 else pbk[5]
                obB = B["pb4"] if wsl == 0 else B["pb5"]
                omm = None
                for d_ in range(FC):
                    omm = t_mm(obank[:, 0:512],
                               wtile[:, d_ * 128:(d_ + 1) * 128],
                               AFMb[:, d_ * 512:(d_ + 1) * 512], d_ == 0, d_ == FC - 1,
                               deps=[ldo], reads=(wtB, B["AFMb"]), writes=(obB,))
                t_stt("VE", xb_t[:, oc * TH + g0: oc * TH + g0 + 512], obank[:, 0:512],
                      bout_b[:, oc:oc + 1], xb_t[:, oc * TH + g0: oc * TH + g0 + 512],
                      ALU.add, ALU.add, deps=[omm],
                      reads=(obB, B["consts"]) + tuple(xbufs(g0, g0 + 512)),
                      writes=tuple(xbufs(g0, g0 + 512)))

    def tap(name):
        if not DEBUG_TAPS:
            return
        td = nc.dram_tensor(name, [D, TH], F32, kind="ExternalOutput").ap()
        for fc in range(FC):
            dma("SY", td[fc * 128:(fc + 1) * 128, :], xb_t[:, fc * TH: (fc + 1) * TH],
                reads=tuple(XR))

    dm1, dr1 = ln_prologue(FFN1_CH, R_M1, R_RSTD1)
    quant_p1(wgT_d, D, INNER, 0)
    qg.extend(quant_p2(wgT_d, wg_s, D, INNER, 0, "qgS", WGb[:, 0:2048]))
    quant_p1(wuT_d, D, INNER, 1)
    qu.extend(quant_p2(wuT_d, wu_s, D, INNER, 1, "quS", WUb[:, 0:2048]))
    quant_p1(wdT_d, INNER, D, 2)
    wsud = t_tt("VE", wsrow[0:1, 3:4], wsrow[0:1, 1:2], wsrow[0:1, 2:3], ALU.mult,
                reads=(B["wsrow"],), writes=(B["wsrow"],), risky=True)
    P.fence(PROSET, CHUNKSET)
    ffn_call(FFN1_CH, R_M1, dm1, dr1, dbg=True, emit_extra=[
        lambda: qd.extend(quant_p2(wdT_d, wd_s, INNER, D, 2, "qdS", WDb[:, 0:2048],
                                   q_load="SE")),
        lambda: precast(wqkv_d, wqkv_bf, 0, 512, 3 * D, pcs),
        lambda: precast(wqkv_d, wqkv_bf, 512, D, 3 * D, pcs),
        lambda: precast(wout_d, wout_bf, 0, D, D, pcs)])
    tap("tap_x1")
    if DEBUG_TAPS:
        trows = nc.dram_tensor("tap_rows", [12, TH], F32, kind="ExternalOutput").ap()
        dma("SY", trows, rows_s, reads=tuple(B[f"rows{i}"] for i in range(12)))
        twg = nc.dram_tensor("tap_wg", [D, INNER], I8, kind="ExternalOutput").ap()
        dma("SY", twg, wg_s, reads=(B["wg_s"],))
    P.fence(CHUNKSET, PROSET)
    dm2, dr2 = ln_prologue(FFN1_CH, R_M2, R_RSTD2)
    P.fence(PROSET + CHUNKSET, ATTNSET)
    attn_phase(dm2, dr2)
    tap("tap_x2")
    P.fence(ATTNSET, PROSET)
    dm3, dr3 = ln_prologue(FFN2_CH, R_M3, R_RSTD3)
    P.fence(PROSET, CHUNKSET)
    ffn_call(FFN2_CH, R_M3, dm3, dr3)

    outs = []
    for fc in range(FC):
        st = dma("SY", y_d[fc * 128:(fc + 1) * 128, :], xb_t[:, fc * TH + 3: fc * TH + 3 + T],
                 reads=tuple(XR), dsem="out")
        outs.append(st)
    P.op("SY", lambda e: e.nop(), deps=[outs[-1]])

    dsem_names = set()
    for r in P.ops:
        if r.dma:
            dsem_names.add(r.dsem if r.dsem is not None else ("d" + r.eng))
    with ExitStack() as st:
        sems = {n: st.enter_context(nc.semaphore(n))
                for n in ["sTE", "sVE", "sSE", "sGP"] + sorted(dsem_names)}
        block = st.enter_context(nc.Block())
        P.finalize(block, sems)
    return nc


_NC_CACHE = {}


def make_inmaps(L0_attn_x, L0_attn_w_qkv, L0_attn_proj, L0_attn_w_out, L0_attn_b_out,
                L0_ffn_w_gate, L0_ffn_w_up, L0_ffn_w_down):
    x = np.asarray(L0_attn_x, dtype=np.float32)
    wqkv = np.ascontiguousarray(np.asarray(L0_attn_w_qkv, dtype=np.float32))
    proj = np.ascontiguousarray(np.asarray(L0_attn_proj, dtype=np.float32))
    wout = np.ascontiguousarray(np.asarray(L0_attn_w_out, dtype=np.float32))
    bout = np.ascontiguousarray(np.asarray(L0_attn_b_out, dtype=np.float32))
    wgT = np.ascontiguousarray(np.asarray(L0_ffn_w_gate, dtype=np.float32).T)
    wuT = np.ascontiguousarray(np.asarray(L0_ffn_w_up, dtype=np.float32).T)
    wdT = np.ascontiguousarray(np.asarray(L0_ffn_w_down, dtype=np.float32).T)

    Bsz, N, _ = x.shape
    eye = np.eye(128, dtype=np.float32)
    kp = np.arange(128)[:, None]
    jj = np.arange(128)[None, :]
    maskA = ((kp - jj >= 0) & (kp - jj <= 6)).astype(np.float32)
    kp8 = np.arange(8)[:, None]
    maskB = ((jj - kp8 >= 122) & (kp8 < 6)).astype(np.float32)
    ones_col = np.full((128, 1), 1.0 / 1024, dtype=ml_dtypes.bfloat16)
    ones_row = np.ones((1, 128), dtype=np.float32)
    pairmask = np.zeros((128, 2), dtype=ml_dtypes.bfloat16)
    pairmask[0:64, 0] = 1
    pairmask[64:128, 1] = 1

    in_maps = []
    for cix in range(NCORE):
        b = cix // 4
        s = 2048 * (cix % 4)
        lo = s - 3
        shard = np.zeros((TH, D), dtype=np.float32)
        a0, a1 = max(lo, 0), min(s + 2051, N)
        shard[a0 - lo:a1 - lo] = x[b, a0:a1]
        x_fm = np.ascontiguousarray(shard.T)
        mask = np.zeros(TQG, dtype=np.float32)
        tt = np.arange(TH)
        valid = (tt + lo >= 0) & (tt + lo < N)
        mask[0:TH][valid] = 1.0
        maskk = np.ascontiguousarray(np.broadcast_to(mask, (128, TQG))).astype(ml_dtypes.bfloat16)
        in_maps.append(dict(
            x_fm=x_fm, wqkv=wqkv, proj=proj, wout=wout, bout=bout,
            wgT=wgT, wuT=wuT, wdT=wdT, maskk=maskk, maskA=maskA, maskB=maskB,
            eye=eye, ones_col=ones_col, ones_row=ones_row, pairmask=pairmask))

    return in_maps, Bsz, N


def kernel(**inputs):
    in_maps, Bsz, N = make_inmaps(**inputs)
    if "nc" not in _NC_CACHE:
        _NC_CACHE["nc"] = build_nc()
    nc = _NC_CACHE["nc"]
    res = run_bass_kernel_spmd(nc, in_maps, core_ids=list(range(NCORE)))
    out = np.empty((Bsz, N, D), dtype=np.float32)
    for cix in range(NCORE):
        b = cix // 4
        s = 2048 * (cix % 4)
        out[b, s:s + 2048] = res.results[cix]["y_fm"].T
    return out


# version 26
# speedup vs baseline: 1.3439x; 1.0210x over previous
import numpy as np
import ml_dtypes
import concourse.bass as bass
import concourse.mybir as mybir
from concourse.bass_utils import run_bass_kernel_spmd
from contextlib import ExitStack

F32 = mybir.dt.float32
BF16 = mybir.dt.bfloat16
I8 = mybir.dt.int8
AF = mybir.ActivationFunctionType
ALU = mybir.AluOpType
AX = mybir.AxisListType

C_RND = float(np.float32(1.5 * 2.0**23))
D = 1024
INNER = 4096
HEADS = 8
FEAT = 64
NCORE = 8
T = 2048
TH = 2054
TQG = 2176
FC = 8
EPS = 1e-5
SW = 264

FFN1_CH = [(0, 259)] + [(259 + 256 * i, 515 + 256 * i) for i in range(6)] + [(1795, 2054)]
FFN2_CH = [(3 + 256 * i, 259 + 256 * i) for i in range(8)]


class Rec:
    __slots__ = ("idx", "eng", "emit", "deps", "dma", "risky", "needs_inc", "sem", "val",
                 "dsem")

    def __init__(self, idx, eng, emit, deps, dma, risky):
        self.idx, self.eng, self.emit = idx, eng, emit
        self.deps, self.dma, self.risky = deps, dma, risky
        self.needs_inc = False
        self.sem = None
        self.val = None


class Buf:
    def __init__(self, name):
        self.name = name
        self.last_write = None
        self.readers = []


class Prog:
    ENGS = ("TE", "VE", "SE", "GP", "SY")

    def __init__(self):
        self.ops = []

    def op(self, eng, emit, deps=(), reads=(), writes=(), dma=False, risky=False,
           dsem=None):
        dl = [d for d in deps if d is not None]
        for b in reads:
            if b.last_write is not None:
                dl.append(b.last_write)
        for b in writes:
            if b.last_write is not None:
                dl.append(b.last_write)
            dl.extend(b.readers)
        r = Rec(len(self.ops), eng, emit, dl, dma, risky)
        r.dsem = dsem
        self.ops.append(r)
        for b in reads:
            b.readers.append(r)
        for b in writes:
            b.last_write = r
            b.readers = []
        return r

    def fence(self, from_bufs, to_bufs):
        deps = []
        for b in from_bufs:
            if b.last_write is not None:
                deps.append(b.last_write)
            deps.extend(b.readers)
        rec = self.op("VE", lambda e: e.engine_nop(), deps=deps, risky=True)
        for b in to_bufs:
            b.last_write = rec
            b.readers = []
        return rec

    def finalize(self, block, sems):
        for r in self.ops:
            for d in r.deps:
                if d.dma:
                    continue
                if d.eng == r.eng and not d.risky:
                    continue
                d.needs_inc = True
        cnt = {e: 0 for e in self.ENGS}
        dcnt = {}
        for r in self.ops:
            if r.dma:
                nm = r.dsem if r.dsem is not None else ("d" + r.eng)
                dcnt[nm] = dcnt.get(nm, 0) + 16
                r.sem = nm
                r.val = dcnt[nm]
            elif r.needs_inc:
                cnt[r.eng] += 1
                r.sem = "s" + r.eng
                r.val = cnt[r.eng]
        self.dma_sems = sorted(dcnt.keys())
        for eng_name, starter in (("TE", block.tensor), ("VE", block.vector),
                                  ("SE", block.scalar), ("GP", block.gpsimd),
                                  ("SY", block.sync)):
            my = [r for r in self.ops if r.eng == eng_name]
            if not my:
                continue

            def body(e, my=my):
                last_wait = {}
                for r in my:
                    need = {}
                    for d in r.deps:
                        if d.sem is None:
                            continue
                        if d.eng == r.eng and not d.dma and not d.risky:
                            continue
                        if last_wait.get(d.sem, -1) >= d.val:
                            continue
                        if need.get(d.sem, -1) < d.val:
                            need[d.sem] = d.val
                    for sm, v in need.items():
                        e.wait_ge(sems[sm], v)
                        last_wait[sm] = v
                    inst = r.emit(e)
                    if r.sem is not None:
                        inst.then_inc(sems[r.sem], 16 if r.dma else 1)
            starter(body)


def secs(width, step=128):
    out, off = [], 0
    while off < width:
        w = min(step, width - off)
        out.append((off, w))
        off += w
    return out


DEBUG_TAPS = False


def build_nc():
    nc = bass.Bass("TRN2", target_bir_lowering=False)
    P = Prog()

    din = lambda n, s, dt=F32: nc.dram_tensor(n, s, dt, kind="ExternalInput").ap()
    x_d = din("x_fm", [D, TH])
    wqkv_d = din("wqkv", [D, 3 * D])
    proj_d = din("proj", [128, FEAT])
    wout_d = din("wout", [D, D])
    bout_d = din("bout", [D])
    wgT_d = din("wgT", [D, INNER])
    wuT_d = din("wuT", [D, INNER])
    wdT_d = din("wdT", [INNER, D])
    maskk_d = din("maskk", [128, TQG], BF16)
    maskA_d = din("maskA", [128, 128])
    maskB_d = din("maskB", [8, 128])
    eye_d = din("eye", [128, 128])
    ones_col_d = din("ones_col", [128, 1], BF16)
    ones_row_d = din("ones_row", [1, 128])
    pairmask_d = din("pairmask", [128, 2], BF16)
    y_d = nc.dram_tensor("y_fm", [D, T], F32, kind="ExternalOutput").ap()

    scr = lambda n, s, dt: nc.dram_tensor(n, s, dt, kind="Internal").ap()
    wg_s = scr("wg_s", [D, INNER], BF16)
    wu_s = scr("wu_s", [D, INNER], BF16)
    wd_s = scr("wd_s", [INNER, D], BF16)
    wqkv_bf = scr("wqkv_bf", [D, 3 * D], BF16)
    wout_bf = scr("wout_bf", [D, D], BF16)
    rows_s = scr("rows_s", [12, TH], F32)
    zrow_s = scr("zrow_s", [2, 512], BF16)
    R_M1, R_RSTD1, R_M2, R_RSTD2, R_M3, R_RSTD3, R_SU, R_FG, R_FDU, R_FD, R_SH = range(11)

    xb_t = nc.alloc_sbuf_tensor("xb", [128, FC * TH], F32)
    PF = nc.alloc_sbuf_tensor("PF", [128, 13440], F32)
    pcstage = nc.alloc_sbuf_tensor("pcstage", [128, 1024], BF16)
    PB = nc.alloc_sbuf_tensor("PB", [128, 29056], BF16)
    w8stage = nc.alloc_sbuf_tensor("w8stage", [128, 2048], I8)
    u8roll = nc.alloc_sbuf_tensor("u8roll", [128, 2 * SW], I8)
    h8roll = nc.alloc_sbuf_tensor("h8roll", [128, 2 * SW], I8)
    rstdb = nc.alloc_sbuf_tensor("rstdb", [1, TH], F32)
    BCb = nc.alloc_sbuf_tensor("BCb", [128, 5 * SW], F32)
    rowb = nc.alloc_sbuf_tensor("rowb", [1, 8 * SW], F32)
    colb = nc.alloc_sbuf_tensor("colb", [128, 72], F32)
    maskk_b = nc.alloc_sbuf_tensor("maskk_b", [128, TQG], BF16)
    maskA_b = nc.alloc_sbuf_tensor("maskA_b", [128, 128], F32)
    maskB_b = nc.alloc_sbuf_tensor("maskB_b", [8, 128], F32)
    eye_b = nc.alloc_sbuf_tensor("eye_b", [128, 128], F32)
    ones_col_b = nc.alloc_sbuf_tensor("ones_col_b", [128, 1], BF16)
    ones_row_b = nc.alloc_sbuf_tensor("ones_row_b", [1, 128], F32)
    pairmask_b = nc.alloc_sbuf_tensor("pairmask_b", [128, 2], BF16)
    bout_b = nc.alloc_sbuf_tensor("bout_b", [128, 8], F32)
    proj_b = nc.alloc_sbuf_tensor("proj_b", [128, FEAT], BF16)
    zq = nc.alloc_sbuf_tensor("zq", [128, 16], F32)
    wsc = nc.alloc_sbuf_tensor("wsc", [128, 4], F32)
    wsrow = nc.alloc_sbuf_tensor("wsrow", [1, 8], F32)
    sumc = nc.alloc_sbuf_tensor("sumc", [128, 8], F32)
    p1scr = nc.alloc_sbuf_tensor("p1scr", [128, 64], F32)
    print('SBUF remaining after allocs:', nc.sbuf_bytes_remaining)

    Hb = PF[:, 0:8448]
    vrow = PF[0:1, 0:TH]
    mrow = PF[0:1, TH:2 * TH]
    esqrow = PF[0:1, 2 * TH:3 * TH]
    Cb = PF[:, 8448:10560]
    GTb = PF[:, 10560:11616]
    MbU = PF[:, 11616:11880]
    WSTG = PF[:, 11880:13416]
    MbH = PF[:, 10560 + 3 * SW:10560 + 4 * SW]
    colbU = colb[:, 0:36]
    colbH = colb[:, 36:72]
    rowbU = rowb[0:1, 0:6 * SW]
    rowbH = rowb[0:1, 6 * SW:8 * SW]
    PHt = PF[:, 0:1280]
    PHt2 = PF[:, 1280:2560]
    zrowb = PF[0:8, 2560:3072]

    HIb = PB[:, 0:8448]
    XBFb = PB[:, 0:2112]
    X2Bb = PB[:, 2112:4224]
    UIb = PB[:, 8448:12672]
    wbfstage = PB[:, 8448:10496]
    WGb = PB[:, 12672:16768]
    WUb = PB[:, 16768:20864]
    WDb = PB[:, 20864:29056]
    U2b = PB[:, 0:5120]
    stashb = PB[:, 5120:6144]
    Vb = PB[:, 6144:11264]
    QPb = PB[:, 11264:13824]
    KPb = PB[:, 13824:16384]
    KWb = PB[:, 16384:17024]
    QRb = PB[:, 17024:18304]
    ZBb = PB[:, 18304:18848]
    AFMb = PB[:, 18848:22944]
    WQKVb = PB[:, 22944:24992]
    WVb = PB[:, 24992:27040]
    WOb = PB[:, 27040:28064]
    SBA = PB[:, 28064:28192]
    SBB = PB[0:8, 28192:28320]

    pbk = [nc.alloc_psum_tensor(f"pb{i}", [128, 512], F32) for i in range(8)]

    names = (["Cb", "GTb", "MbU", "MbH", "WSTG", "rowbU", "rowbH", "colbU", "colbH", "colb", "rstdb", "vrow", "mrow",
              "esqrow", "wsc", "wsrow", "sumc", "p1V", "p1G", "p1q0", "p1q1", "p1q2", "p1q3", "consts", "w8stage", "w8stage1", "wbf0", "wbf1", "WSTG0", "WSTG1", "WSTG2", "WSTG3", "u8a", "u8b", "h8a", "h8b",
              "wqb0", "wqb1",
              "Hb", "HIb", "UIb0", "UIb1", "WGb0", "WGb1", "WUb0", "WUb1", "WDb0", "WDb1",
              "XBFb", "X2Bb", "PHt", "PHt2", "zrowb",
              "U2b", "stash", "Vb0", "Vb1", "Vb2", "Vb3", "Vb4", "QPb", "KPb",
              "KWb", "QRb", "ZBb", "AFMb", "WQKVb0", "WQKVb1", "WVb0", "WVb1",
              "WOb", "SBA", "SBB", "zq", "zrow_s",
              "wg_s", "wu_s", "wd_s"]
             + [f"BC{i}" for i in range(5)]
             + [f"pb{i}" for i in range(8)]
             + [f"rows{i}" for i in range(12)])
    B = {n: Buf(n) for n in names}
    XR = [Buf(f"x{i}") for i in range(17)]

    def xbufs(a, b):
        return [XR[i] for i in range(a // 128, min((b + 127) // 128, 17))]

    CHUNKSET = [B[n] for n in ["Hb", "HIb", "UIb0", "UIb1", "WGb0", "WGb1", "WUb0",
                               "WUb1", "WDb0", "WDb1", "Cb", "GTb", "MbU", "MbH",
                               "colbU", "colbH", "rowbU", "rowbH"]]
    PROSET = [B[n] for n in ["XBFb", "X2Bb", "vrow", "mrow", "esqrow", "colb"]]
    ATTNSET = [B[n] for n in ["U2b", "stash", "Vb0", "Vb1", "Vb2", "Vb3", "Vb4",
                              "QPb", "KPb", "KWb", "QRb", "ZBb", "AFMb",
                              "WQKVb0", "WQKVb1", "WVb0", "WVb1", "WOb",
                              "SBA", "SBB", "PHt", "PHt2", "zrowb", "WSTG"]]

    def dma(eng, out, in_, reads=(), writes=(), deps=(), dsem=None):
        return P.op(eng, lambda e, o=out, i=in_: e.dma_start(out=o, in_=i),
                    deps=deps, reads=reads, writes=writes, dma=True, dsem=dsem)

    def t_copy(eng, out, in_, **kw):
        if eng == "SE":
            return P.op(eng, lambda e, o=out, i=in_: e.copy(out=o, in_=i), **kw)
        return P.op(eng, lambda e, o=out, i=in_: e.tensor_copy(out=o, in_=i), **kw)

    def t_tt(eng, out, in0, in1, op, **kw):
        return P.op(eng, lambda e, o=out, a=in0, b=in1, p=op:
                    e.tensor_tensor(out=o, in0=a, in1=b, op=p), **kw)

    def t_ts(eng, out, in0, s1, s2, op0, op1=None, **kw):
        if op1 is None:
            return P.op(eng, lambda e, o=out, i=in0, a=s1, p0=op0:
                        e.tensor_single_scalar(out=o, in_=i, scalar=a, op=p0), **kw)
        return P.op(eng, lambda e, o=out, i=in0, a=s1, b=s2, p0=op0, p1=op1:
                    e.tensor_scalar(out=o, in0=i, scalar1=a, scalar2=b, op0=p0, op1=p1), **kw)

    def t_red(eng, out, in_, op, absval=False, **kw):
        return P.op(eng, lambda e, o=out, i=in_, p=op, a=absval:
                    e.tensor_reduce(out=o, in_=i, axis=AX.X, op=p, apply_absolute_value=a), **kw)

    def t_act(out, in_, func, bias=0.0, scale=1.0, **kw):
        return P.op("SE", lambda e, o=out, i=in_, f=func, b=bias, s=scale:
                    e.activation(out=o, in_=i, func=f, bias=b, scale=s), **kw)

    def t_mm(out, lhsT, rhs, start, stop, **kw):
        return P.op("TE", lambda e, o=out, l=lhsT, r=rhs, st=start, sp=stop:
                    e.matmul(o, l, r, start=st, stop=sp), **kw)

    def t_tr(out, in_, **kw):
        pd = in_.partition_size
        pd = pd() if callable(pd) else pd
        return P.op("TE", lambda e, o=out, i=in_, p=pd:
                    e.transpose(o, i, eye_b[0:p, 0:p]), **kw)

    def t_stt(eng, out, in0, scalar, in1, op0, op1, **kw):
        return P.op(eng, lambda e, o=out, a=in0, s=scalar, b=in1, p0=op0, p1=op1:
                    e.scalar_tensor_tensor(out=o, in0=a, scalar=s, in1=b, op0=p0, op1=p1), **kw)

    def t_recip(eng, out, in_, **kw):
        return P.op(eng, lambda e, o=out, i=in_: e.reciprocal(out=o, in_=i), **kw)

    def t_memset(eng, ap, val, **kw):
        return P.op(eng, lambda e, a=ap, v=val: e.memset(a, v), **kw)

    def bcast(dram_row_slice, dst_tile, writes, deps=(), dsem=None):
        return dma("SE", dst_tile, dram_row_slice.partition_broadcast(128),
                   deps=deps, writes=writes, dsem=dsem)

    def rowslice(slot, a, b):
        return rows_s[slot, a:b].rearrange("(one t) -> one t", one=1)

    c = B["consts"]
    for dst, src in [(eye_b[:, :], eye_d), (maskA_b[:, :], maskA_d),
                     (maskB_b[:, :], maskB_d), (maskk_b[:, :], maskk_d),
                     (ones_col_b[:, :], ones_col_d), (ones_row_b[:, :], ones_row_d),
                     (pairmask_b[:, :], pairmask_d)]:
        dma("SY", dst, src, writes=(c,), dsem="cst")
    dma("GP", proj_b[:, :], proj_d, writes=(c,), dsem="cst2")
    for oc in range(8):
        dma("SY", bout_b[:, oc:oc + 1],
            bout_d[oc * 128:(oc + 1) * 128].rearrange("(p one) -> p one", one=1),
            writes=(c,), dsem="cst")
    dma("SY", xb_t[:, :].rearrange("p (fc t) -> p fc t", fc=FC),
        x_d.rearrange("(fc p) t -> p fc t", p=128), writes=tuple(XR), dsem="xld")

    def quant_p1(w_d, nrows, ncols, ws_idx):
        hw = 512
        tiles = [(r, h) for r in range(nrows // 128) for h in range(ncols // hw)]
        nt = len(tiles)
        NS = 3
        for i, (r, h) in enumerate(tiles):
            sl0 = (i % NS) * hw
            wB = B[f"WSTG{i % NS}"]
            src = w_d[r * 128:(r + 1) * 128, h * hw:(h + 1) * hw]
            dma("SY", WSTG[:, sl0:sl0 + hw], src, writes=(wB,), dsem=f"stg{i % NS}")
            t_red("VE", p1scr[:, i:i + 1], WSTG[:, sl0:sl0 + hw], ALU.add, absval=True,
                  reads=(wB,), writes=(B[f"p1q{i % 4}"],), risky=True)
        cb = t_red("VE", sumc[:, ws_idx:ws_idx + 1], p1scr[:, 0:nt], ALU.add,
                   reads=(B["p1q0"], B["p1q1"], B["p1q2"], B["p1q3"]),
                   writes=(B["sumc"],), risky=True)
        tr = t_tr(pbk[7][0:1, 0:128], sumc[:, ws_idx:ws_idx + 1],
                  deps=[cb], reads=(B["sumc"], B["consts"]), writes=(B["pb7"],))
        t_red("VE", wsrow[0:1, 4:5], pbk[7][0:1, 0:128], ALU.add,
              deps=[tr], reads=(B["pb7"],), writes=(B["wsrow"],), risky=True)
        t_ts("VE", wsrow[0:1, ws_idx:ws_idx + 1], wsrow[0:1, 4:5], 1.0 / (nrows * ncols),
             1e-5, ALU.mult, ALU.max, reads=(B["wsrow"],), writes=(B["wsrow"],), risky=True)
        t_recip("VE", wsrow[0:1, 5:6], wsrow[0:1, ws_idx:ws_idx + 1],
                reads=(B["wsrow"],), writes=(B["wsrow"],), risky=True)
        bc = t_mm(pbk[7][:, 128:129], ones_row_b[0:1, :], wsrow[0:1, 5:6], True, True,
                  reads=(B["wsrow"], B["consts"]), writes=(B["pb7"],))
        t_copy("VE", wsc[:, ws_idx:ws_idx + 1], pbk[7][:, 128:129],
               deps=[bc], reads=(B["pb7"],), writes=(B["wsc"],), risky=True)

    def quant_p2(w_d, w_scr, nrows, ncols, ws_idx, ssem, stage, q_load="SY"):
        hw = 512
        tiles = [(r, h) for r in range(nrows // 128) for h in range(ncols // hw)]
        NS = 3
        stores = []
        for i, (r, h) in enumerate(tiles):
            sl0 = (i % NS) * hw
            wB = B[f"WSTG{i % NS}"]
            w8B = B["w8stage"] if i % 2 == 0 else B["w8stage1"]
            s8 = (i % 2) * 1024
            eng = "VE"
            src = w_d[r * 128:(r + 1) * 128, h * hw:(h + 1) * hw]
            ldsem = f"stg{i % NS}" if q_load == "SY" else f"sg2{i % NS}"
            dma(q_load, WSTG[:, sl0:sl0 + hw], src, writes=(wB,), dsem=ldsem)
            t_ts(eng, WSTG[:, sl0:sl0 + hw], WSTG[:, sl0:sl0 + hw], wsc[:, ws_idx:ws_idx + 1],
                 1.0, ALU.mult, ALU.min, reads=(wB, B["wsc"]), writes=(wB,))
            t_ts(eng, WSTG[:, sl0:sl0 + hw], WSTG[:, sl0:sl0 + hw], -1.0, None,
                 ALU.max, reads=(wB,), writes=(wB,))
            t_copy("VE", w8stage[:, s8:s8 + hw], WSTG[:, sl0:sl0 + hw],
                   reads=(wB,), writes=(w8B,))
            wbfB = B["wbf0"] if i % 2 == 0 else B["wbf1"]
            t_copy("SE", stage[:, s8:s8 + hw], w8stage[:, s8:s8 + hw],
                   reads=(w8B,), writes=(wbfB,))
            st = dma("SE", w_scr[r * 128:(r + 1) * 128, h * hw:(h + 1) * hw],
                     stage[:, s8:s8 + hw], reads=(wbfB,), dsem=ssem)
            stores.append(st)
        return stores

    def precast(w_d, w_bf, rlo, rhi, ncols, stores):
        hw = 1024
        for r in range(rlo // 128, rhi // 128):
            for h in range(ncols // hw):
                dma("GP", pcstage[:, :],
                    w_d[r * 128:(r + 1) * 128, h * hw:(h + 1) * hw],
                    writes=(B["wqb0"],), dsem="pcL")
                st = dma("GP", w_bf[r * 128:(r + 1) * 128, h * hw:(h + 1) * hw],
                         pcstage[:, :], reads=(B["wqb0"],), dsem="pcS")
                stores.append(st)

    qd, qg, qu, pcs = [], [], [], []

    def ln_prologue(chunks, slot_m, slot_rstd):
        lo, hi = chunks[0][0], chunks[-1][1]
        for (c0, c1) in chunks:
            w = c1 - c0
            cps, sqs = [], []
            for fc in range(FC):
                cp = t_copy("VE", XBFb[:, fc * SW: fc * SW + w],
                            xb_t[:, fc * TH + c0: fc * TH + c0 + w],
                            reads=tuple(xbufs(c0, c1)), writes=(B["XBFb"],))
                cps.append(cp)
            for fc in range(FC):
                sq = t_act(X2Bb[:, fc * SW: fc * SW + w], XBFb[:, fc * SW: fc * SW + w],
                           AF.Square, deps=[cps[fc]], reads=(B["XBFb"],), writes=(B["X2Bb"],))
                sqs.append(sq)
            mm = None
            for fc in range(FC):
                mm = t_mm(pbk[5][0:1, 0:w], ones_col_b[:, 0:1], XBFb[:, fc * SW: fc * SW + w],
                          fc == 0, fc == FC - 1, deps=[cps[fc]],
                          reads=(B["XBFb"], B["consts"]), writes=(B["pb5"],))
            t_copy("VE", mrow[0:1, c0:c1], pbk[5][0:1, 0:w], deps=[mm],
                   reads=(B["pb5"],), writes=(B["mrow"],), risky=True)
            mm = None
            for fc in range(FC):
                mm = t_mm(pbk[6][0:1, 0:w], ones_col_b[:, 0:1], X2Bb[:, fc * SW: fc * SW + w],
                          fc == 0, fc == FC - 1, deps=[sqs[fc]],
                          reads=(B["X2Bb"], B["consts"]), writes=(B["pb6"],))
            t_copy("VE", esqrow[0:1, c0:c1], pbk[6][0:1, 0:w], deps=[mm],
                   reads=(B["pb6"],), writes=(B["esqrow"],), risky=True)
        t_tt("VE", vrow[0:1, lo:hi], mrow[0:1, lo:hi], mrow[0:1, lo:hi], ALU.mult,
             reads=(B["mrow"],), writes=(B["vrow"],), risky=True)
        t_tt("VE", vrow[0:1, lo:hi], esqrow[0:1, lo:hi], vrow[0:1, lo:hi], ALU.subtract,
             reads=(B["esqrow"], B["vrow"]), writes=(B["vrow"],), risky=True)
        sl = secs(hi - lo)
        ns = len(sl)
        assert ns <= 17
        for si, (off, w) in enumerate(sl):
            tr = t_tr(pbk[7][0:w, si:si + 1], vrow[0:1, lo + off:lo + off + w],
                      reads=(B["vrow"], B["consts"]), writes=(B["pb7"],))
        ev = t_copy("VE", colb[:, 0:ns], pbk[7][:, 0:ns],
                    reads=(B["pb7"],), writes=(B["colb"],), risky=True)
        t_ts("VE", colb[:, 0:ns], colb[:, 0:ns], EPS, None, ALU.add,
             reads=(B["colb"],), writes=(B["colb"],), risky=True)
        t_recip("VE", colb[:, 17:17 + ns], colb[:, 0:ns],
                reads=(B["colb"],), writes=(B["colb"],), risky=True)
        t_act(colb[:, 34:34 + ns], colb[:, 17:17 + ns], AF.Sqrt,
              reads=(B["colb"],), writes=(B["colb"],), risky=True)
        t_tt("VE", colb[:, 51:51 + ns], colb[:, 34:34 + ns], colb[:, 34:34 + ns], ALU.mult,
             reads=(B["colb"],), writes=(B["colb"],), risky=True)
        t_tt("VE", colb[:, 51:51 + ns], colb[:, 51:51 + ns], colb[:, 0:ns], ALU.mult,
             reads=(B["colb"],), writes=(B["colb"],), risky=True)
        t_ts("VE", colb[:, 51:51 + ns], colb[:, 51:51 + ns], -0.5, 1.5, ALU.mult, ALU.add,
             reads=(B["colb"],), writes=(B["colb"],), risky=True)
        t_tt("VE", colb[:, 34:34 + ns], colb[:, 34:34 + ns], colb[:, 51:51 + ns], ALU.mult,
             reads=(B["colb"],), writes=(B["colb"],), risky=True)
        for si, (off, w) in enumerate(sl):
            tr = t_tr(pbk[5][0:1, 0:w], colb[0:w, 34 + si:35 + si],
                      reads=(B["colb"], B["consts"]), writes=(B["pb5"],))
            t_copy("VE", rstdb[0:1, lo + off:lo + off + w], pbk[5][0:1, 0:w],
                   deps=[tr], reads=(B["pb5"],), writes=(B["rstdb"],), risky=True)
        dm = dma("SE", rowslice(slot_m, lo, hi), mrow[0:1, lo:hi],
                 reads=(B["mrow"],), writes=(B[f"rows{slot_m}"],), dsem=f"rs{slot_m}")
        dr = dma("SE", rowslice(slot_rstd, lo, hi), rstdb[0:1, lo:hi],
                 reads=(B["rstdb"],), writes=(B[f"rows{slot_rstd}"],), dsem=f"rs{slot_rstd}")
        return dm, dr

    def mx_chain(w, src_red, src_bufs, pband, pbB, colap, colB, rowap, rowB, mbap, mbB,
                 bdst, bdstB, pre_m1=None):
        if pre_m1 is None:
            ap, n_inner = src_red
            m1 = t_red("VE", mbap[:, 0:w], ap, ALU.max, absval=True,
                       reads=src_bufs, writes=(mbB,), risky=True)
        else:
            m1 = pre_m1
        sl = secs(w)
        ns = len(sl)
        for si, (off, sw_) in enumerate(sl):
            reg = 128 + (si % 2) * 128
            tr = t_tr(pband[0:sw_, reg:reg + 128], mbap[:, off:off + sw_],
                      deps=[m1], reads=(mbB, B["consts"]), writes=(pbB,))
            t_red("VE", colap[0:sw_, si:si + 1], pband[0:sw_, reg:reg + 128],
                  ALU.max, deps=[tr], reads=(pbB,), writes=(colB,), risky=True)
        t_ts("VE", colap[:, 3:3 + ns], colap[:, 0:ns], EPS, None,
             ALU.max, reads=(colB,), writes=(colB,), risky=True)
        t_recip("VE", colap[:, 6:6 + ns], colap[:, 3:3 + ns],
                reads=(colB,), writes=(colB,), risky=True)
        t_ts("VE", colap[:, 9:9 + ns], colap[:, 6:6 + ns],
             127.0, None, ALU.mult, reads=(colB,), writes=(colB,), risky=True)
        for si, (off, sw_) in enumerate(sl):
            tr = t_tr(pband[0:1, 0:sw_], colap[0:sw_, 9 + si:10 + si],
                      reads=(colB, B["consts"]), writes=(pbB,))
            t_copy("VE", rowap[0:1, off:off + sw_], pband[0:1, 0:sw_],
                   deps=[tr], reads=(pbB,), writes=(rowB,), risky=True)
            tr2 = t_tr(pband[0:1, 0:sw_], colap[0:sw_, 3 + si:4 + si],
                       reads=(colB, B["consts"]), writes=(pbB,))
            t_copy("VE", rowap[0:1, SW + off:SW + off + sw_], pband[0:1, 0:sw_],
                   deps=[tr2], reads=(pbB,), writes=(rowB,), risky=True)
        return mmbcast(rowap[0:1, 0:w], rowB, w, pband, pbB, bdst, bdstB)

    def mmbcast(row_ap, rowB, w, pband, pbB, dst, dstB):
        mm = t_mm(pband[:, 0:w], ones_row_b[0:1, :], row_ap, True, True,
                  reads=(rowB, B["consts"]), writes=(pbB,))
        return t_copy("VE", dst, pband[:, 0:w], deps=[mm],
                      reads=(pbB,), writes=(dstB,), risky=True)

    def ffn_call(chunks, slot_m, ev_m, ev_r, dbg=False, emit_extra=None):
        pend = []

        def emit_down(st_):
            c0, c1, w, rnd, bfd = st_["c0"], st_["c1"], st_["w"], st_["rnd"], st_["bfd"]
            for hc in range(8):
                ws_ = hc % 2
                wdB = B[f"WDb{ws_}"]
                wdt = WDb[:, ws_ * 4096:(ws_ + 1) * 4096]
                dma("SE", wdt,
                    wd_s.rearrange("(ic p) h -> p ic h", p=128)[:, :, hc * 128:(hc + 1) * 128],
                    deps=[qd[-1]], writes=(wdB,), dsem=f"wd{ws_}")
                bank = pbk[4 + hc % 2]
                bB = B[f"pb{4 + hc % 2}"]
                dmm = None
                for ic in range(32):
                    dmm = t_mm(bank[:, 0:w], wdt[:, ic * 128:(ic + 1) * 128],
                               HIb[:, ic * SW: ic * SW + w], ic == 0, ic == 31,
                               deps=[rnd[ic]], reads=(wdB, B["HIb"]), writes=(bB,))
                tt = t_tt("VE", GTb[:, 2 * SW:2 * SW + w], bank[:, 0:w],
                          BCb[:, 4 * SW:4 * SW + w], ALU.mult, deps=[dmm, bfd],
                          reads=(bB, B["BC4"]), writes=(B["GTb"],))
                t_tt("VE", xb_t[:, hc * TH + c0: hc * TH + c0 + w],
                     xb_t[:, hc * TH + c0: hc * TH + c0 + w], GTb[:, 2 * SW:2 * SW + w],
                     ALU.add, deps=[tt], reads=(B["GTb"],) + tuple(xbufs(c0, c1)),
                     writes=tuple(xbufs(c0, c1)))

        def dtap(name, src, srcbuf, shape, dt):
            if not (DEBUG_TAPS and dbg):
                return
            td = nc.dram_tensor(name, shape, dt, kind="ExternalOutput").ap()
            dma("SY", td, src, reads=(srcbuf,))
        qstate = {}

        def emit_quant(ci):
            c0, c1 = chunks[ci]
            w = c1 - c0
            ui = ci % 2
            uB = B[f"UIb{ui}"]
            uiT = UIb[:, ui * FC * SW:(ui + 1) * FC * SW]
            mb = bcast(rowslice(slot_m, c0, c1), BCb[:, 0:w], writes=(B["BC0"],),
                       deps=[ev_m], dsem="bc0")
            for fc in range(FC):
                t_tt("GP", Cb[:, fc * SW: fc * SW + w], xb_t[:, fc * TH + c0: fc * TH + c0 + w],
                     BCb[:, 0:w], ALU.subtract, deps=[mb],
                     reads=tuple(xbufs(c0, c1)) + (B["BC0"],), writes=(B["Cb"],))
            bsu = mx_chain(w, (Cb[:, 0:FC * SW].rearrange("p (fc t) -> p t fc", fc=FC)[:, 0:w, :], FC),
                           (B["Cb"],), pbk[6], B["pb6"], colbU, B["colbU"], rowbU, B["rowbU"],
                           MbU, B["MbU"], BCb[:, SW:SW + w], B["BC1"])
            t0 = t_tt("VE", rowbU[0:1, 2 * SW:2 * SW + w], rowbU[0:1, SW:SW + w],
                      rstdb[0:1, c0:c1], ALU.mult, deps=[ev_r],
                      reads=(B["rowbU"], B["rstdb"]), writes=(B["rowbU"],), risky=True)
            fg = t_ts("VE", rowbU[0:1, 3 * SW:3 * SW + w], rowbU[0:1, 2 * SW:2 * SW + w],
                      wsrow[0:1, 0:1], 1.0 / 127, ALU.mult, ALU.mult, deps=[wsud],
                      reads=(B["rowbU"], B["wsrow"]), writes=(B["rowbU"],), risky=True)
            fsl = 4 * SW + (ci % 2) * SW
            fdu = t_ts("VE", rowbU[0:1, fsl:fsl + w], rowbU[0:1, 2 * SW:2 * SW + w],
                       wsrow[0:1, 3:4], 0.5 / (127.0 * 127.0), ALU.mult, ALU.mult,
                       reads=(B["rowbU"], B["wsrow"]), writes=(B["rowbU"],), risky=True)
            bfg = mmbcast(rowbU[0:1, 3 * SW:3 * SW + w], B["rowbU"], w, pbk[6], B["pb6"],
                          BCb[:, 2 * SW:2 * SW + w], B["BC2"])
            for fc in range(FC):
                sl8 = (fc % 2) * SW
                u8B = B["u8a"] if fc % 2 == 0 else B["u8b"]
                t_tt("VE", Cb[:, fc * SW: fc * SW + w], Cb[:, fc * SW: fc * SW + w],
                     BCb[:, SW:SW + w], ALU.mult, deps=[bsu],
                     reads=(B["Cb"], B["BC1"]), writes=(B["Cb"],))
                t_copy("VE", u8roll[:, sl8:sl8 + w], Cb[:, fc * SW: fc * SW + w],
                       reads=(B["Cb"],), writes=(u8B,), risky=True)
                t_copy("SE", uiT[:, fc * SW: fc * SW + w], u8roll[:, sl8:sl8 + w],
                       reads=(u8B,), writes=(uB,), risky=True)
            ev_ui = P.ops[-1]
            qstate[ci] = dict(uiT=uiT, uB=uB, ev_ui=ev_ui, bfg=bfg, w=w, c0=c0, c1=c1)

        emit_quant(0)
        extras = list(emit_extra or [])
        for ci, (c0, c1) in enumerate(chunks):
            st0 = qstate[ci]
            uiT, uB, ev_ui, bfg, w = st0["uiT"], st0["uB"], st0["ev_ui"], st0["bfg"], st0["w"]
            for oc in range(32):
                pr = oc // 2
                j = oc % 2
                ws_ = pr % 2
                wgB, wuB = B[f"WGb{ws_}"], B[f"WUb{ws_}"]
                wgt = WGb[:, ws_ * 2048:(ws_ + 1) * 2048]
                wut = WUb[:, ws_ * 2048:(ws_ + 1) * 2048]
                if j == 0:
                    dma("SY", wgt,
                        wg_s.rearrange("(fc p) o -> p fc o", p=128)[:, :, pr * 256:(pr + 1) * 256],
                        deps=[qg[-1]], writes=(wgB,), dsem=f"wg{ws_}")
                    dma("SY", wut,
                        wu_s.rearrange("(fc p) o -> p fc o", p=128)[:, :, pr * 256:(pr + 1) * 256],
                        deps=[qu[-1]], writes=(wuB,), dsem=f"wu{ws_}")
                gbank = pbk[oc % 3]
                gB = B[f"pb{oc % 3}"]
                ubank = pbk[3 + oc % 2]
                uB2 = B[f"pb{3 + oc % 2}"]
                gmm = None
                for f in range(FC):
                    gmm = t_mm(gbank[:, 0:w], wgt[:, f * 256 + j * 128: f * 256 + j * 128 + 128],
                               uiT[:, f * SW: f * SW + w], f == 0, f == FC - 1,
                               deps=[ev_ui], reads=(wgB, uB), writes=(gB,))
                umm = None
                for f in range(FC):
                    umm = t_mm(ubank[:, 0:w], wut[:, f * 256 + j * 128: f * 256 + j * 128 + 128],
                               uiT[:, f * SW: f * SW + w], f == 0, f == FC - 1,
                               deps=[ev_ui], reads=(wuB, uB), writes=(uB2,))
                gt = GTb[:, (oc % 2) * SW:(oc % 2) * SW + w]
                g1 = t_tt("VE", gt, gbank[:, 0:w], BCb[:, 2 * SW:2 * SW + w], ALU.mult,
                          deps=[gmm, bfg], reads=(gB, B["BC2"]), writes=(B["GTb"],))
                sl_ = t_act(gt, gt, AF.Silu, deps=[g1], reads=(B["GTb"],), writes=(B["GTb"],))
                hh = t_tt("VE", Hb[:, oc * SW: oc * SW + w], ubank[:, 0:w], gt, ALU.mult,
                          deps=[umm, sl_], reads=(uB2, B["GTb"]), writes=(B["Hb"],))
                if oc == 15:
                    m2a = t_red("VE", MbH[:, 0:w],
                                Hb[:, 0:16 * SW].rearrange("p (oc t) -> p t oc", oc=16)[:, 0:w, :],
                                ALU.max, absval=True, deps=[hh], reads=(B["Hb"],),
                                writes=(B["MbH"],), risky=True)
                if oc == 31:
                    m2h = t_red("VE", MbU[:, 0:w],
                                Hb[:, 16 * SW:32 * SW].rearrange("p (oc t) -> p t oc", oc=16)[:, 0:w, :],
                                ALU.max, absval=True, deps=[hh], reads=(B["Hb"],),
                                writes=(B["MbU"],), risky=True)
                    m2b = t_tt("VE", MbH[:, 0:w], MbH[:, 0:w], MbU[:, 0:w], ALU.max,
                               deps=[m2a, m2h], reads=(B["MbH"], B["MbU"]),
                               writes=(B["MbH"],), risky=True)
            if ci == 0:
                dtap("tap_h", Hb[:, :], B["Hb"], [128, 32 * SW], F32)
            if ci + 1 < len(chunks):
                emit_quant(ci + 1)
            if pend:
                emit_down(pend.pop(0))
            if extras:
                extras.pop(0)()
            bsh = mx_chain(w, None, (B["Hb"],), pbk[7], B["pb7"], colbH, B["colbH"],
                           rowbH, B["rowbH"], MbH, B["MbH"],
                           BCb[:, 3 * SW:3 * SW + w], B["BC3"], pre_m1=m2b)
            fsl2 = 4 * SW + (ci % 2) * SW
            fd = t_tt("VE", rowbH[0:1, SW:SW + w], rowbH[0:1, SW:SW + w],
                      rowbU[0:1, fsl2:fsl2 + w], ALU.mult,
                      reads=(B["rowbH"], B["rowbU"]), writes=(B["rowbH"],), risky=True)
            bfd = mmbcast(rowbH[0:1, SW:SW + w], B["rowbH"], w, pbk[7], B["pb7"],
                          BCb[:, 4 * SW:4 * SW + w], B["BC4"])
            rnd = []
            for oc in range(32):
                sl8 = (oc % 2) * SW
                h8B = B["h8a"] if oc % 2 == 0 else B["h8b"]
                t_tt("VE", Hb[:, oc * SW: oc * SW + w], Hb[:, oc * SW: oc * SW + w],
                     BCb[:, 3 * SW:3 * SW + w], ALU.mult, deps=[bsh],
                     reads=(B["Hb"], B["BC3"]), writes=(B["Hb"],))
                t_copy("VE", h8roll[:, sl8:sl8 + w], Hb[:, oc * SW: oc * SW + w],
                       reads=(B["Hb"],), writes=(h8B,), risky=True)
                rn = t_copy("SE", HIb[:, oc * SW: oc * SW + w], h8roll[:, sl8:sl8 + w],
                            reads=(h8B,), writes=(B["HIb"],), risky=True)
                rnd.append(rn)
            if ci == 0:
                dtap("tap_hi", HIb[:, :], B["HIb"], [128, 32 * SW], BF16)
            pend.append(dict(c0=c0, c1=c1, w=w, rnd=rnd, bfd=bfd))
        emit_down(pend.pop(0))

    def attn_phase(ev_m2, ev_r2):
        bV = [B[f"Vb{i}"] for i in range(5)]
        for q in range(4):
            Qs = 512 * q
            new0 = 0 if q == 0 else 128
            hi_local = min(640, TH - Qs)
            if q > 0:
                for fc in range(FC):
                    t_copy("SE", U2b[:, fc * 640: fc * 640 + 128],
                           stashb[:, fc * 128:(fc + 1) * 128],
                           reads=(B["stash"],), writes=(B["U2b"],), risky=True)
                t_copy("SE", Vb[:, 0:1024], Vb[:, 4 * 1024:5 * 1024],
                       reads=(bV[4],), writes=(bV[0],), risky=True)
                for g in range(4):
                    t_copy("SE", KPb[:, g * 640:g * 640 + 128],
                           KPb[:, g * 640 + 512:g * 640 + 640],
                           reads=(B["KPb"],), writes=(B["KPb"],), risky=True)
            g0, g1_ = Qs + new0, Qs + hi_local
            nw = g1_ - g0
            bm2 = bcast(rows_s[R_M2, g0:g1_].rearrange("(one t) -> one t", one=1),
                        PHt2[:, 0:nw], writes=(B["PHt2"],), deps=[ev_m2], dsem="ph2")
            br2 = bcast(rows_s[R_RSTD2, g0:g1_].rearrange("(one t) -> one t", one=1),
                        PHt2[:, 640:640 + nw], writes=(B["PHt2"],), deps=[ev_r2], dsem="ph2")
            u2evs = []
            for fc in range(FC):
                s1 = t_tt("GP", PHt[:, 0:nw], xb_t[:, fc * TH + g0: fc * TH + g1_],
                          PHt2[:, 0:nw], ALU.subtract, deps=[bm2],
                          reads=tuple(xbufs(g0, g1_)) + (B["PHt2"],), writes=(B["PHt"],))
                s2 = t_tt("GP", U2b[:, fc * 640 + new0: fc * 640 + hi_local], PHt[:, 0:nw],
                          PHt2[:, 640:640 + nw], ALU.mult, deps=[br2, s1],
                          reads=(B["PHt"], B["PHt2"]), writes=(B["U2b"],))
                u2evs.append(s2)
            if hi_local < 640:
                for fc in range(FC):
                    u2evs.append(t_memset("VE", U2b[:, fc * 640 + hi_local: fc * 640 + 640],
                                          0.0, writes=(B["U2b"],), risky=True))
            ev_u2 = u2evs[-1]
            if q < 3:
                for fc in range(FC):
                    t_copy("SE", stashb[:, fc * 128:(fc + 1) * 128],
                           U2b[:, fc * 640 + 512: fc * 640 + 640],
                           deps=[ev_u2], reads=(B["U2b"],), writes=(B["stash"],), risky=True)
            tl_range = list(range(0 if q == 0 else 1, 5))
            for tl in tl_range:
                if min(128, max(0, TH - (Qs + tl * 128))) < 128:
                    t_memset("VE", Vb[:, tl * 1024:(tl + 1) * 1024], 0.0,
                             writes=(bV[tl],), risky=True)
            for vb in range(4):
                if vb % 2 == 0:
                    vreg, vbufs = WVb[:, 0:2048], (B["WVb0"], B["WVb1"])
                else:
                    vreg, vbufs = WQKVb[:, 0:2048], (B["WQKVb0"], B["WQKVb1"])
                ldv = dma("SY", vreg,
                          wqkv_bf.rearrange("(fc p) c -> p fc c", p=128)[:, :, 2 * D + vb * 256: 2 * D + (vb + 1) * 256],
                          deps=[pcs[-1]], writes=vbufs, dsem="wv")
                for tl in tl_range:
                    gt0 = Qs + tl * 128
                    m_rows = min(128, max(0, TH - gt0))
                    if m_rows == 0:
                        continue
                    vbank = pbk[0] if tl % 2 == 0 else pbk[1]
                    vbB = B["pb0"] if tl % 2 == 0 else B["pb1"]
                    vmm = None
                    for f in range(FC):
                        vmm = t_mm(vbank[0:m_rows, 0:256],
                                   U2b[:, f * 640 + tl * 128: f * 640 + tl * 128 + m_rows],
                                   vreg[:, f * 256: f * 256 + 256],
                                   f == 0, f == FC - 1,
                                   deps=[ev_u2, ldv], reads=(B["U2b"],) + vbufs,
                                   writes=(vbB,))
                    t_copy("VE", Vb[0:m_rows, tl * 1024 + vb * 256: tl * 1024 + vb * 256 + 256],
                           vbank[0:m_rows, 0:256], deps=[vmm], reads=(vbB,), writes=(bV[tl],))
            q_lo, q_hi = 3, 515
            phi_evs = []
            for h in range(HEADS):
                g = h // 2
                rbase = (h % 2) * 64
                for which in ("q", "k"):
                    if which == "q":
                        ranges = [(q_lo, q_hi)]
                    else:
                        mid = min(new0 + 512, hi_local)
                        ranges = [(new0, mid)]
                        if mid < hi_local:
                            ranges.append((mid, hi_local))
                    slotn = h % 2
                    wB = B[f"WQKVb{slotn}"]
                    base = 0 if which == "q" else D
                    qi = h * 2 + (0 if which == "q" else 1)
                    rbank, rbB = (pbk[1], B["pb1"]) if qi % 2 == 0 else (pbk[3], B["pb3"])
                    phbank, phB = (pbk[2], B["pb2"]) if qi % 2 == 0 else (pbk[4], B["pb4"])
                    ldw = dma("SY", WQKVb[:, slotn * 1024: slotn * 1024 + 1024],
                              wqkv_bf.rearrange("(fc p) c -> p fc c", p=128)[:, :, base + h * 128: base + h * 128 + 128],
                              deps=[pcs[-1]], writes=(wB,), dsem=f"qk{slotn}")
                    for clo, chi in ranges:
                        cw = chi - clo
                        rmm = None
                        for f in range(FC):
                            rmm = t_mm(rbank[:, 0:cw],
                                       WQKVb[:, slotn * 1024 + f * 128: slotn * 1024 + f * 128 + 128],
                                       U2b[:, f * 640 + clo: f * 640 + chi], f == 0, f == FC - 1,
                                       deps=[ldw, ev_u2], reads=(wB, B["U2b"]), writes=(rbB,))
                        rb = QRb[:, (0 if which == "q" else 1) * 640 + clo:
                                 (0 if which == "q" else 1) * 640 + chi]
                        ev_r = t_copy("VE", rb, rbank[:, 0:cw], deps=[rmm],
                                      reads=(rbB,), writes=(B["QRb"],))
                        ph = t_mm(phbank[0:FEAT, 0:cw], proj_b[:, :], rb, True, True,
                                  deps=[ev_r], reads=(B["QRb"], B["consts"]), writes=(phB,))
                        e1 = t_ts("VE", PHt[0:FEAT, 0:cw], phbank[0:FEAT, 0:cw], 0.0, None,
                                  ALU.min, deps=[ph], reads=(phB,), writes=(B["PHt"],))
                        e2 = t_act(PHt2[0:FEAT, 0:cw], PHt[0:FEAT, 0:cw], AF.Exp,
                                   deps=[e1], reads=(B["PHt"],), writes=(B["PHt2"],))
                        e3 = t_ts("VE", PHt[0:FEAT, 0:cw], phbank[0:FEAT, 0:cw], 0.0, None,
                                  ALU.max, deps=[ph], reads=(phB,), writes=(B["PHt"],))
                        dstb, dB = (QPb, B["QPb"]) if which == "q" else (KPb, B["KPb"])
                        if rbase == 0:
                            dst = dstb[0:64, g * 640 + clo: g * 640 + chi]
                            mk = maskk_b[0:64, Qs + clo:Qs + chi]
                        else:
                            dst = dstb[64:128, g * 640 + clo: g * 640 + chi]
                            mk = maskk_b[0:64, Qs + clo:Qs + chi]
                        if which == "q":
                            e4 = t_tt("VE", dst, PHt2[0:FEAT, 0:cw], PHt[0:FEAT, 0:cw], ALU.add,
                                      deps=[e2, e3], reads=(B["PHt"], B["PHt2"]), writes=(dB,),
                                      risky=True)
                            phi_evs.append(e4)
                        else:
                            e4 = t_tt("VE", PHt[0:FEAT, 0:cw], PHt2[0:FEAT, 0:cw],
                                      PHt[0:FEAT, 0:cw], ALU.add, deps=[e2, e3],
                                      reads=(B["PHt"], B["PHt2"]), writes=(B["PHt"],), risky=True)
                            e5 = t_tt("VE", dst, PHt[0:FEAT, 0:cw], mk, ALU.mult,
                                      reads=(B["PHt"], B["consts"]), writes=(dB,), risky=True)
                            phi_evs.append(e5)
            if hi_local < 640:
                for g in range(4):
                    phi_evs.append(t_memset("VE", KPb[:, g * 640 + hi_local:(g + 1) * 640],
                                            0.0, writes=(B["KPb"],), risky=True))
            ev_phi = phi_evs[-1]
            for g in range(4):
                kp = KPb[:, g * 640:(g + 1) * 640]
                kw = KWb[:, 0:520]
                a1 = t_tt("VE", kw, kp[:, 0:520], kp[:, 1:521], ALU.add, deps=[ev_phi],
                          reads=(B["KPb"],), writes=(B["KWb"],), risky=True)
                a2 = t_tt("VE", PHt[:, 0:520], kp[:, 2:522], kp[:, 3:523], ALU.add,
                          deps=[ev_phi], reads=(B["KPb"],), writes=(B["PHt"],), risky=True)
                a3 = t_tt("VE", PHt2[:, 0:520], kp[:, 4:524], kp[:, 5:525], ALU.add,
                          deps=[ev_phi], reads=(B["KPb"],), writes=(B["PHt2"],), risky=True)
                t_tt("VE", kw, kw, PHt[:, 0:520], ALU.add,
                     reads=(B["KWb"], B["PHt"]), writes=(B["KWb"],), risky=True)
                t_tt("VE", kw, kw, PHt2[:, 0:520], ALU.add,
                     reads=(B["KWb"], B["PHt2"]), writes=(B["KWb"],), risky=True)
                a6 = t_tt("VE", kw, kw, kp[:, 6:526], ALU.add,
                          reads=(B["KWb"], B["KPb"]), writes=(B["KWb"],), risky=True)
                pp = t_tt("VE", QRb[:, 0:512], QPb[:, g * 640 + q_lo: g * 640 + q_hi],
                          kw[:, 0:512], ALU.mult, deps=[a6, ev_phi],
                          reads=(B["QPb"], B["KWb"]), writes=(B["QRb"],), risky=True)
                zmm = []
                for b_ in range(4):
                    mm = t_mm(pbk[3][:, b_ * 2: b_ * 2 + 2], QRb[:, b_ * 128:(b_ + 1) * 128],
                              pairmask_b[:, :], True, True, deps=[pp],
                              reads=(B["QRb"], B["consts"]), writes=(B["pb3"],))
                    zmm.append(mm)
                t_ts("VE", zq[:, 0:8], pbk[3][:, 0:8], 1e-8, None, ALU.add,
                     deps=[zmm[-1]], reads=(B["pb3"],), writes=(B["zq"],), risky=True)
                t_recip("VE", zq[:, 8:16], zq[:, 0:8], reads=(B["zq"],), writes=(B["zq"],),
                        risky=True)
                sc_evs = []
                for h2 in range(2):
                    for b_ in range(4):
                        col = 8 + b_ * 2 + h2
                        tr = t_tr(pbk[3][0:1, b_ * 128:128 + b_ * 128],
                                  zq[:, col:col + 1],
                                  reads=(B["zq"], B["consts"]), writes=(B["pb3"],))
                    ev = t_copy("VE", rowbU[0:1, 0:512], pbk[3][0:1, 0:512],
                                reads=(B["pb3"],), writes=(B["rowbU"],), risky=True)
                    mm = t_mm(pbk[3][:, 0:512] if False else pbk[2][:, 0:512],
                              ones_row_b[0:1, :], rowbU[0:1, 0:512], True, True,
                              deps=[ev], reads=(B["rowbU"], B["consts"]), writes=(B["pb2"],))
                    ev2 = t_copy("VE", ZBb[:, 0:512], pbk[2][:, 0:512], deps=[mm],
                                 reads=(B["pb2"],), writes=(B["ZBb"],), risky=True)
                    rb_ = h2 * 64
                    sc_evs.append(t_tt("VE", QPb[rb_:rb_ + 64, g * 640 + q_lo: g * 640 + q_hi],
                                       QPb[rb_:rb_ + 64, g * 640 + q_lo: g * 640 + q_hi],
                                       ZBb[rb_:rb_ + 64, 0:512], ALU.mult, deps=[ev2],
                                       reads=(B["QPb"], B["ZBb"]), writes=(B["QPb"],),
                                       risky=True))
                sc_ = sc_evs[-1]
                for h2 in range(2):
                    rbase = h2 * 64
                    h = 2 * g + h2
                    stj = {}
                    for j in range(5):
                        if j < 4:
                            qs = j * 128 + 3
                            sbank = pbk[6] if j % 2 == 0 else pbk[0]
                            sbB = B["pb6"] if j % 2 == 0 else B["pb0"]
                            sA = SBA[:, 0:128] if j % 2 == 0 else KWb[:, 0:128]
                            sAB = B["SBA"] if j % 2 == 0 else B["KWb"]
                            sB2 = SBB[0:8, 0:128] if j % 2 == 0 else KWb[0:8, 128:256]
                            sBB = B["SBB"] if j % 2 == 0 else B["KWb"]
                            sa = t_mm(sbank[:, 0:128],
                                      KPb[rbase:rbase + 64, g * 640 + j * 128: g * 640 + (j + 1) * 128],
                                      QPb[rbase:rbase + 64, g * 640 + qs: g * 640 + qs + 128],
                                      True, True, deps=[sc_], reads=(B["KPb"], B["QPb"]),
                                      writes=(sbB,))
                            sb_ = t_mm(sbank[0:8, 128:256],
                                       KPb[rbase:rbase + 64, g * 640 + (j + 1) * 128: g * 640 + (j + 1) * 128 + 8],
                                       QPb[rbase:rbase + 64, g * 640 + qs: g * 640 + qs + 128],
                                       True, True, deps=[sc_], reads=(B["KPb"], B["QPb"]),
                                       writes=(sbB,))
                            maA = t_tt("VE", sA, sbank[:, 0:128], maskA_b[:, :],
                                       ALU.mult, deps=[sa], reads=(sbB, B["consts"]),
                                       writes=(sAB,))
                            maB = t_tt("VE", sB2, sbank[0:8, 128:256], maskB_b[:, :],
                                       ALU.mult, deps=[sb_], reads=(sbB, B["consts"]),
                                       writes=(sBB,))
                            stj[j] = (sA, sAB, sB2, sBB, maA, maB)
                        if j >= 1:
                            jj = j - 1
                            sA, sAB, sB2, sBB, maA, maB = stj.pop(jj)
                            abank = pbk[5] if jj % 2 == 0 else pbk[1]
                            abB = B["pb5"] if jj % 2 == 0 else B["pb1"]
                            am1 = t_mm(abank[:, 256:384],
                                       Vb[:, jj * 1024 + h * 128: jj * 1024 + (h + 1) * 128],
                                       sA, True, False, deps=[maA],
                                       reads=(bV[jj], sAB), writes=(abB,))
                            am2 = t_mm(abank[:, 256:384],
                                       Vb[0:6, (jj + 1) * 1024 + h * 128: (jj + 1) * 1024 + (h + 1) * 128],
                                       sB2[0:6, 0:128], False, True, deps=[maB],
                                       reads=(bV[jj + 1], sBB), writes=(abB,))
                            t_copy("VE", AFMb[:, h * 512 + jj * 128: h * 512 + (jj + 1) * 128],
                                   abank[:, 256:384], deps=[am2], reads=(abB,),
                                   writes=(B["AFMb"],))
            g0 = Qs + 3
            for oc in range(8):
                wsl = oc % 2
                wtile = WOb[:, 0:1024] if wsl == 0 else WQKVb[:, 0:1024]
                wtB = B["WOb"] if wsl == 0 else B["WQKVb0"]
                ldo = dma("SY", wtile,
                          wout_bf.rearrange("(fc p) c -> p fc c", p=128)[:, :, oc * 128:(oc + 1) * 128],
                          deps=[pcs[-1]], writes=(wtB,), dsem=f"wo{wsl}")
                obank = pbk[4] if wsl == 0 else pbk[5]
                obB = B["pb4"] if wsl == 0 else B["pb5"]
                omm = None
                for d_ in range(FC):
                    omm = t_mm(obank[:, 0:512],
                               wtile[:, d_ * 128:(d_ + 1) * 128],
                               AFMb[:, d_ * 512:(d_ + 1) * 512], d_ == 0, d_ == FC - 1,
                               deps=[ldo], reads=(wtB, B["AFMb"]), writes=(obB,))
                t_stt("VE", xb_t[:, oc * TH + g0: oc * TH + g0 + 512], obank[:, 0:512],
                      bout_b[:, oc:oc + 1], xb_t[:, oc * TH + g0: oc * TH + g0 + 512],
                      ALU.add, ALU.add, deps=[omm],
                      reads=(obB, B["consts"]) + tuple(xbufs(g0, g0 + 512)),
                      writes=tuple(xbufs(g0, g0 + 512)))

    def tap(name):
        if not DEBUG_TAPS:
            return
        td = nc.dram_tensor(name, [D, TH], F32, kind="ExternalOutput").ap()
        for fc in range(FC):
            dma("SY", td[fc * 128:(fc + 1) * 128, :], xb_t[:, fc * TH: (fc + 1) * TH],
                reads=tuple(XR))

    dm1, dr1 = ln_prologue(FFN1_CH, R_M1, R_RSTD1)
    quant_p1(wgT_d, D, INNER, 0)
    qg.extend(quant_p2(wgT_d, wg_s, D, INNER, 0, "qgS", WGb[:, 0:2048]))
    quant_p1(wuT_d, D, INNER, 1)
    qu.extend(quant_p2(wuT_d, wu_s, D, INNER, 1, "quS", WUb[:, 0:2048]))
    quant_p1(wdT_d, INNER, D, 2)
    wsud = t_tt("VE", wsrow[0:1, 3:4], wsrow[0:1, 1:2], wsrow[0:1, 2:3], ALU.mult,
                reads=(B["wsrow"],), writes=(B["wsrow"],), risky=True)
    P.fence(PROSET, CHUNKSET)
    ffn_call(FFN1_CH, R_M1, dm1, dr1, dbg=True, emit_extra=[
        lambda: qd.extend(quant_p2(wdT_d, wd_s, INNER, D, 2, "qdS", WDb[:, 0:2048])),
        lambda: precast(wqkv_d, wqkv_bf, 0, 512, 3 * D, pcs),
        lambda: precast(wqkv_d, wqkv_bf, 512, D, 3 * D, pcs),
        lambda: precast(wout_d, wout_bf, 0, D, D, pcs)])
    tap("tap_x1")
    if DEBUG_TAPS:
        trows = nc.dram_tensor("tap_rows", [12, TH], F32, kind="ExternalOutput").ap()
        dma("SY", trows, rows_s, reads=tuple(B[f"rows{i}"] for i in range(12)))
        twg = nc.dram_tensor("tap_wg", [D, INNER], I8, kind="ExternalOutput").ap()
        dma("SY", twg, wg_s, reads=(B["wg_s"],))
    P.fence(CHUNKSET, PROSET)
    dm2, dr2 = ln_prologue(FFN1_CH, R_M2, R_RSTD2)
    P.fence(PROSET + CHUNKSET, ATTNSET)
    attn_phase(dm2, dr2)
    tap("tap_x2")
    P.fence(ATTNSET, PROSET)
    dm3, dr3 = ln_prologue(FFN2_CH, R_M3, R_RSTD3)
    P.fence(PROSET, CHUNKSET)
    ffn_call(FFN2_CH, R_M3, dm3, dr3)

    outs = []
    for fc in range(FC):
        st = dma("SY", y_d[fc * 128:(fc + 1) * 128, :], xb_t[:, fc * TH + 3: fc * TH + 3 + T],
                 reads=tuple(XR), dsem="out")
        outs.append(st)
    P.op("SY", lambda e: e.nop(), deps=[outs[-1]])

    dsem_names = set()
    for r in P.ops:
        if r.dma:
            dsem_names.add(r.dsem if r.dsem is not None else ("d" + r.eng))
    with ExitStack() as st:
        sems = {n: st.enter_context(nc.semaphore(n))
                for n in ["sTE", "sVE", "sSE", "sGP"] + sorted(dsem_names)}
        block = st.enter_context(nc.Block())
        P.finalize(block, sems)
    return nc


_NC_CACHE = {}


def make_inmaps(L0_attn_x, L0_attn_w_qkv, L0_attn_proj, L0_attn_w_out, L0_attn_b_out,
                L0_ffn_w_gate, L0_ffn_w_up, L0_ffn_w_down):
    x = np.asarray(L0_attn_x, dtype=np.float32)
    wqkv = np.ascontiguousarray(np.asarray(L0_attn_w_qkv, dtype=np.float32))
    proj = np.ascontiguousarray(np.asarray(L0_attn_proj, dtype=np.float32))
    wout = np.ascontiguousarray(np.asarray(L0_attn_w_out, dtype=np.float32))
    bout = np.ascontiguousarray(np.asarray(L0_attn_b_out, dtype=np.float32))
    wgT = np.ascontiguousarray(np.asarray(L0_ffn_w_gate, dtype=np.float32).T)
    wuT = np.ascontiguousarray(np.asarray(L0_ffn_w_up, dtype=np.float32).T)
    wdT = np.ascontiguousarray(np.asarray(L0_ffn_w_down, dtype=np.float32).T)

    Bsz, N, _ = x.shape
    eye = np.eye(128, dtype=np.float32)
    kp = np.arange(128)[:, None]
    jj = np.arange(128)[None, :]
    maskA = ((kp - jj >= 0) & (kp - jj <= 6)).astype(np.float32)
    kp8 = np.arange(8)[:, None]
    maskB = ((jj - kp8 >= 122) & (kp8 < 6)).astype(np.float32)
    ones_col = np.full((128, 1), 1.0 / 1024, dtype=ml_dtypes.bfloat16)
    ones_row = np.ones((1, 128), dtype=np.float32)
    pairmask = np.zeros((128, 2), dtype=ml_dtypes.bfloat16)
    pairmask[0:64, 0] = 1
    pairmask[64:128, 1] = 1

    in_maps = []
    for cix in range(NCORE):
        b = cix // 4
        s = 2048 * (cix % 4)
        lo = s - 3
        shard = np.zeros((TH, D), dtype=np.float32)
        a0, a1 = max(lo, 0), min(s + 2051, N)
        shard[a0 - lo:a1 - lo] = x[b, a0:a1]
        x_fm = np.ascontiguousarray(shard.T)
        mask = np.zeros(TQG, dtype=np.float32)
        tt = np.arange(TH)
        valid = (tt + lo >= 0) & (tt + lo < N)
        mask[0:TH][valid] = 1.0
        maskk = np.ascontiguousarray(np.broadcast_to(mask, (128, TQG))).astype(ml_dtypes.bfloat16)
        in_maps.append(dict(
            x_fm=x_fm, wqkv=wqkv, proj=proj, wout=wout, bout=bout,
            wgT=wgT, wuT=wuT, wdT=wdT, maskk=maskk, maskA=maskA, maskB=maskB,
            eye=eye, ones_col=ones_col, ones_row=ones_row, pairmask=pairmask))

    return in_maps, Bsz, N


def kernel(**inputs):
    in_maps, Bsz, N = make_inmaps(**inputs)
    if "nc" not in _NC_CACHE:
        _NC_CACHE["nc"] = build_nc()
    nc = _NC_CACHE["nc"]
    res = run_bass_kernel_spmd(nc, in_maps, core_ids=list(range(NCORE)))
    out = np.empty((Bsz, N, D), dtype=np.float32)
    for cix in range(NCORE):
        b = cix // 4
        s = 2048 * (cix % 4)
        out[b, s:s + 2048] = res.results[cix]["y_fm"].T
    return out


# version 27
# speedup vs baseline: 1.4024x; 1.0435x over previous
import numpy as np
import ml_dtypes
import concourse.bass as bass
import concourse.mybir as mybir
from concourse.bass_utils import run_bass_kernel_spmd
from contextlib import ExitStack

F32 = mybir.dt.float32
BF16 = mybir.dt.bfloat16
I8 = mybir.dt.int8
AF = mybir.ActivationFunctionType
ALU = mybir.AluOpType
AX = mybir.AxisListType

C_RND = float(np.float32(1.5 * 2.0**23))
D = 1024
INNER = 4096
HEADS = 8
FEAT = 64
NCORE = 8
T = 2048
TH = 2054
TQG = 2176
FC = 8
EPS = 1e-5
SW = 264

FFN1_CH = [(0, 259)] + [(259 + 256 * i, 515 + 256 * i) for i in range(6)] + [(1795, 2054)]
FFN2_CH = [(3 + 256 * i, 259 + 256 * i) for i in range(8)]


class Rec:
    __slots__ = ("idx", "eng", "emit", "deps", "dma", "risky", "needs_inc", "sem", "val",
                 "dsem")

    def __init__(self, idx, eng, emit, deps, dma, risky):
        self.idx, self.eng, self.emit = idx, eng, emit
        self.deps, self.dma, self.risky = deps, dma, risky
        self.needs_inc = False
        self.sem = None
        self.val = None


class Buf:
    def __init__(self, name):
        self.name = name
        self.last_write = None
        self.readers = []


class Prog:
    ENGS = ("TE", "VE", "SE", "GP", "SY")

    def __init__(self):
        self.ops = []

    def op(self, eng, emit, deps=(), reads=(), writes=(), dma=False, risky=False,
           dsem=None):
        dl = [d for d in deps if d is not None]
        for b in reads:
            if b.last_write is not None:
                dl.append(b.last_write)
        for b in writes:
            if b.last_write is not None:
                dl.append(b.last_write)
            dl.extend(b.readers)
        r = Rec(len(self.ops), eng, emit, dl, dma, risky)
        r.dsem = dsem
        self.ops.append(r)
        for b in reads:
            b.readers.append(r)
        for b in writes:
            b.last_write = r
            b.readers = []
        return r

    def fence(self, from_bufs, to_bufs):
        deps = []
        for b in from_bufs:
            if b.last_write is not None:
                deps.append(b.last_write)
            deps.extend(b.readers)
        rec = self.op("VE", lambda e: e.engine_nop(), deps=deps, risky=True)
        for b in to_bufs:
            b.last_write = rec
            b.readers = []
        return rec

    def finalize(self, block, sems):
        for r in self.ops:
            for d in r.deps:
                if d.dma:
                    continue
                if d.eng == r.eng and not d.risky:
                    continue
                d.needs_inc = True
        cnt = {e: 0 for e in self.ENGS}
        dcnt = {}
        for r in self.ops:
            if r.dma:
                nm = r.dsem if r.dsem is not None else ("d" + r.eng)
                dcnt[nm] = dcnt.get(nm, 0) + 16
                r.sem = nm
                r.val = dcnt[nm]
            elif r.needs_inc:
                cnt[r.eng] += 1
                r.sem = "s" + r.eng
                r.val = cnt[r.eng]
        self.dma_sems = sorted(dcnt.keys())
        for eng_name, starter in (("TE", block.tensor), ("VE", block.vector),
                                  ("SE", block.scalar), ("GP", block.gpsimd),
                                  ("SY", block.sync)):
            my = [r for r in self.ops if r.eng == eng_name]
            if not my:
                continue

            def body(e, my=my):
                last_wait = {}
                for r in my:
                    need = {}
                    for d in r.deps:
                        if d.sem is None:
                            continue
                        if d.eng == r.eng and not d.dma and not d.risky:
                            continue
                        if last_wait.get(d.sem, -1) >= d.val:
                            continue
                        if need.get(d.sem, -1) < d.val:
                            need[d.sem] = d.val
                    for sm, v in need.items():
                        e.wait_ge(sems[sm], v)
                        last_wait[sm] = v
                    inst = r.emit(e)
                    if r.sem is not None:
                        inst.then_inc(sems[r.sem], 16 if r.dma else 1)
            starter(body)


def secs(width, step=128):
    out, off = [], 0
    while off < width:
        w = min(step, width - off)
        out.append((off, w))
        off += w
    return out


DEBUG_TAPS = False


def build_nc():
    nc = bass.Bass("TRN2", target_bir_lowering=False)
    P = Prog()

    din = lambda n, s, dt=F32: nc.dram_tensor(n, s, dt, kind="ExternalInput").ap()
    x_d = din("x_fm", [D, TH])
    wqkv_d = din("wqkv", [D, 3 * D])
    proj_d = din("proj", [128, FEAT])
    wout_d = din("wout", [D, D])
    bout_d = din("bout", [D])
    wgT_d = din("wgT", [D, INNER])
    wuT_d = din("wuT", [D, INNER])
    wdT_d = din("wdT", [INNER, D])
    maskk_d = din("maskk", [128, TQG], BF16)
    maskA_d = din("maskA", [128, 128])
    maskB_d = din("maskB", [8, 128])
    eye_d = din("eye", [128, 128])
    ones_col_d = din("ones_col", [128, 1], BF16)
    ones_row_d = din("ones_row", [1, 128])
    pairmask_d = din("pairmask", [128, 2], BF16)
    y_d = nc.dram_tensor("y_fm", [D, T], F32, kind="ExternalOutput").ap()

    scr = lambda n, s, dt: nc.dram_tensor(n, s, dt, kind="Internal").ap()
    wg_s = scr("wg_s", [D, INNER], BF16)
    wu_s = scr("wu_s", [D, INNER], BF16)
    wd_s = scr("wd_s", [INNER, D], BF16)
    wqkv_bf = scr("wqkv_bf", [D, 3 * D], BF16)
    wout_bf = scr("wout_bf", [D, D], BF16)
    rows_s = scr("rows_s", [12, TH], F32)
    zrow_s = scr("zrow_s", [2, 512], BF16)
    R_M1, R_RSTD1, R_M2, R_RSTD2, R_M3, R_RSTD3, R_SU, R_FG, R_FDU, R_FD, R_SH = range(11)

    xb_t = nc.alloc_sbuf_tensor("xb", [128, FC * TH], F32)
    PF = nc.alloc_sbuf_tensor("PF", [128, 13440], F32)
    pcstage = nc.alloc_sbuf_tensor("pcstage", [128, 1024], BF16)
    PB = nc.alloc_sbuf_tensor("PB", [128, 29056], BF16)
    w8stage = nc.alloc_sbuf_tensor("w8stage", [128, 2048], I8)
    u8roll = nc.alloc_sbuf_tensor("u8roll", [128, 2 * SW], I8)
    h8roll = nc.alloc_sbuf_tensor("h8roll", [128, 2 * SW], I8)
    rstdb = nc.alloc_sbuf_tensor("rstdb", [1, TH], F32)
    BCb = nc.alloc_sbuf_tensor("BCb", [128, 5 * SW], F32)
    rowb = nc.alloc_sbuf_tensor("rowb", [1, 8 * SW], F32)
    colb = nc.alloc_sbuf_tensor("colb", [128, 72], F32)
    maskk_b = nc.alloc_sbuf_tensor("maskk_b", [128, TQG], BF16)
    maskA_b = nc.alloc_sbuf_tensor("maskA_b", [128, 128], F32)
    maskB_b = nc.alloc_sbuf_tensor("maskB_b", [8, 128], F32)
    eye_b = nc.alloc_sbuf_tensor("eye_b", [128, 128], F32)
    ones_col_b = nc.alloc_sbuf_tensor("ones_col_b", [128, 1], BF16)
    ones_row_b = nc.alloc_sbuf_tensor("ones_row_b", [1, 128], F32)
    pairmask_b = nc.alloc_sbuf_tensor("pairmask_b", [128, 2], BF16)
    bout_b = nc.alloc_sbuf_tensor("bout_b", [128, 8], F32)
    proj_b = nc.alloc_sbuf_tensor("proj_b", [128, FEAT], BF16)
    zq = nc.alloc_sbuf_tensor("zq", [128, 16], F32)
    wsc = nc.alloc_sbuf_tensor("wsc", [128, 4], F32)
    wsrow = nc.alloc_sbuf_tensor("wsrow", [1, 8], F32)
    sumc = nc.alloc_sbuf_tensor("sumc", [128, 8], F32)
    p1scr = nc.alloc_sbuf_tensor("p1scr", [128, 64], F32)
    print('SBUF remaining after allocs:', nc.sbuf_bytes_remaining)

    Hb = PF[:, 0:8448]
    vrow = PF[0:1, 0:TH]
    mrow = PF[0:1, TH:2 * TH]
    esqrow = PF[0:1, 2 * TH:3 * TH]
    Cb = PF[:, 8448:10560]
    GTb = PF[:, 10560:11616]
    MbU = PF[:, 11616:11880]
    WSTG = PF[:, 11880:13416]
    MbH = PF[:, 10560 + 3 * SW:10560 + 4 * SW]
    colbU = colb[:, 0:36]
    colbH = colb[:, 36:72]
    rowbU = rowb[0:1, 0:6 * SW]
    rowbH = rowb[0:1, 6 * SW:8 * SW]
    PHt = PF[:, 0:1280]
    PHt2 = PF[:, 1280:2560]
    zrowb = PF[0:8, 2560:3072]

    HIb = PB[:, 0:8448]
    XBFb = PB[:, 0:2112]
    X2Bb = PB[:, 2112:4224]
    UIb = PB[:, 8448:12672]
    wbfstage = PB[:, 8448:10496]
    WGb = PB[:, 12672:16768]
    WUb = PB[:, 16768:20864]
    WDb = PB[:, 20864:29056]
    U2b = PB[:, 0:5120]
    stashb = PB[:, 5120:6144]
    Vb = PB[:, 6144:11264]
    QPb = PB[:, 11264:13824]
    KPb = PB[:, 13824:16384]
    KWb = PB[:, 16384:17024]
    QRb = PB[:, 17024:18304]
    ZBb = PB[:, 18304:18848]
    AFMb = PB[:, 18848:22944]
    WQKVb = PB[:, 22944:24992]
    WVb = PB[:, 24992:27040]
    WOb = PB[:, 27040:28064]
    SBA = PB[:, 28064:28192]
    SBB = PB[0:8, 28192:28320]

    pbk = [nc.alloc_psum_tensor(f"pb{i}", [128, 512], F32) for i in range(8)]

    names = (["Cb", "GTb", "MbU", "MbH", "WSTG", "rowbU", "rowbH", "colbU", "colbH", "colb", "rstdb", "vrow", "mrow",
              "esqrow", "wsc", "wsrow", "sumc", "p1V", "p1G", "p1q0", "p1q1", "p1q2", "p1q3", "consts", "w8stage", "w8stage1", "wbf0", "wbf1", "WSTG0", "WSTG1", "WSTG2", "WSTG3", "WSTG4", "WSTG5", "WSTG6", "u8a", "u8b", "h8a", "h8b",
              "wqb0", "wqb1",
              "Hb", "HIb", "UIb0", "UIb1", "WGb0", "WGb1", "WUb0", "WUb1", "WDb0", "WDb1",
              "XBFb", "X2Bb", "PHt", "PHt2", "zrowb",
              "U2b", "stash", "Vb0", "Vb1", "Vb2", "Vb3", "Vb4", "QPb", "KPb",
              "KWb", "QRb", "ZBb", "AFMb", "WQKVb0", "WQKVb1", "WVb0", "WVb1",
              "WOb", "SBA", "SBB", "zq", "zrow_s",
              "wg_s", "wu_s", "wd_s"]
             + [f"BC{i}" for i in range(5)]
             + [f"pb{i}" for i in range(8)]
             + [f"rows{i}" for i in range(12)])
    B = {n: Buf(n) for n in names}
    XR = [Buf(f"x{i}") for i in range(17)]

    def xbufs(a, b):
        return [XR[i] for i in range(a // 128, min((b + 127) // 128, 17))]

    CHUNKSET = [B[n] for n in ["Hb", "HIb", "UIb0", "UIb1", "WGb0", "WGb1", "WUb0",
                               "WUb1", "WDb0", "WDb1", "Cb", "GTb", "MbU", "MbH",
                               "colbU", "colbH", "rowbU", "rowbH"]]
    PROSET = [B[n] for n in ["XBFb", "X2Bb", "vrow", "mrow", "esqrow", "colb"]]
    ATTNSET = [B[n] for n in ["U2b", "stash", "Vb0", "Vb1", "Vb2", "Vb3", "Vb4",
                              "QPb", "KPb", "KWb", "QRb", "ZBb", "AFMb",
                              "WQKVb0", "WQKVb1", "WVb0", "WVb1", "WOb",
                              "SBA", "SBB", "PHt", "PHt2", "zrowb", "WSTG"]]

    def dma(eng, out, in_, reads=(), writes=(), deps=(), dsem=None):
        return P.op(eng, lambda e, o=out, i=in_: e.dma_start(out=o, in_=i),
                    deps=deps, reads=reads, writes=writes, dma=True, dsem=dsem)

    def t_copy(eng, out, in_, **kw):
        if eng == "SE":
            return P.op(eng, lambda e, o=out, i=in_: e.copy(out=o, in_=i), **kw)
        return P.op(eng, lambda e, o=out, i=in_: e.tensor_copy(out=o, in_=i), **kw)

    def t_tt(eng, out, in0, in1, op, **kw):
        return P.op(eng, lambda e, o=out, a=in0, b=in1, p=op:
                    e.tensor_tensor(out=o, in0=a, in1=b, op=p), **kw)

    def t_ts(eng, out, in0, s1, s2, op0, op1=None, **kw):
        if op1 is None:
            return P.op(eng, lambda e, o=out, i=in0, a=s1, p0=op0:
                        e.tensor_single_scalar(out=o, in_=i, scalar=a, op=p0), **kw)
        return P.op(eng, lambda e, o=out, i=in0, a=s1, b=s2, p0=op0, p1=op1:
                    e.tensor_scalar(out=o, in0=i, scalar1=a, scalar2=b, op0=p0, op1=p1), **kw)

    def t_red(eng, out, in_, op, absval=False, **kw):
        return P.op(eng, lambda e, o=out, i=in_, p=op, a=absval:
                    e.tensor_reduce(out=o, in_=i, axis=AX.X, op=p, apply_absolute_value=a), **kw)

    def t_act(out, in_, func, bias=0.0, scale=1.0, **kw):
        return P.op("SE", lambda e, o=out, i=in_, f=func, b=bias, s=scale:
                    e.activation(out=o, in_=i, func=f, bias=b, scale=s), **kw)

    def t_mm(out, lhsT, rhs, start, stop, **kw):
        return P.op("TE", lambda e, o=out, l=lhsT, r=rhs, st=start, sp=stop:
                    e.matmul(o, l, r, start=st, stop=sp), **kw)

    def t_tr(out, in_, **kw):
        pd = in_.partition_size
        pd = pd() if callable(pd) else pd
        return P.op("TE", lambda e, o=out, i=in_, p=pd:
                    e.transpose(o, i, eye_b[0:p, 0:p]), **kw)

    def t_stt(eng, out, in0, scalar, in1, op0, op1, **kw):
        return P.op(eng, lambda e, o=out, a=in0, s=scalar, b=in1, p0=op0, p1=op1:
                    e.scalar_tensor_tensor(out=o, in0=a, scalar=s, in1=b, op0=p0, op1=p1), **kw)

    def t_recip(eng, out, in_, **kw):
        return P.op(eng, lambda e, o=out, i=in_: e.reciprocal(out=o, in_=i), **kw)

    def t_memset(eng, ap, val, **kw):
        return P.op(eng, lambda e, a=ap, v=val: e.memset(a, v), **kw)

    def bcast(dram_row_slice, dst_tile, writes, deps=(), dsem=None):
        return dma("SE", dst_tile, dram_row_slice.partition_broadcast(128),
                   deps=deps, writes=writes, dsem=dsem)

    def rowslice(slot, a, b):
        return rows_s[slot, a:b].rearrange("(one t) -> one t", one=1)

    c = B["consts"]
    for dst, src in [(eye_b[:, :], eye_d), (maskA_b[:, :], maskA_d),
                     (maskB_b[:, :], maskB_d), (maskk_b[:, :], maskk_d),
                     (ones_col_b[:, :], ones_col_d), (ones_row_b[:, :], ones_row_d),
                     (pairmask_b[:, :], pairmask_d)]:
        dma("SY", dst, src, writes=(c,), dsem="cst")
    dma("GP", proj_b[:, :], proj_d, writes=(c,), dsem="cst2")
    for oc in range(8):
        dma("SY", bout_b[:, oc:oc + 1],
            bout_d[oc * 128:(oc + 1) * 128].rearrange("(p one) -> p one", one=1),
            writes=(c,), dsem="cst")
    dma("SY", xb_t[:, :].rearrange("p (fc t) -> p fc t", fc=FC),
        x_d.rearrange("(fc p) t -> p fc t", p=128), writes=tuple(XR), dsem="xld")

    P0SLOTS = ([(WSTG[:, k * 512:(k + 1) * 512], f"WSTG{k}") for k in range(3)]
               + [(Cb[:, k * 512:(k + 1) * 512], f"WSTG{3 + k}") for k in range(4)])

    def quant_p1(w_d, nrows, ncols, ws_idx, slots=P0SLOTS):
        hw = 512
        tiles = [(r, h) for r in range(nrows // 128) for h in range(ncols // hw)]
        nt = len(tiles)
        NS = len(slots)
        for i, (r, h) in enumerate(tiles):
            sap, sbn = slots[i % NS]
            wB = B[sbn]
            src = w_d[r * 128:(r + 1) * 128, h * hw:(h + 1) * hw]
            dma("SY", sap, src, writes=(wB,), dsem=f"stg{i % NS}")
            t_red("VE", p1scr[:, i:i + 1], sap, ALU.add, absval=True,
                  reads=(wB,), writes=(B[f"p1q{i % 4}"],), risky=True)
        cb = t_red("VE", sumc[:, ws_idx:ws_idx + 1], p1scr[:, 0:nt], ALU.add,
                   reads=(B["p1q0"], B["p1q1"], B["p1q2"], B["p1q3"]),
                   writes=(B["sumc"],), risky=True)
        tr = t_tr(pbk[7][0:1, 0:128], sumc[:, ws_idx:ws_idx + 1],
                  deps=[cb], reads=(B["sumc"], B["consts"]), writes=(B["pb7"],))
        t_red("VE", wsrow[0:1, 4:5], pbk[7][0:1, 0:128], ALU.add,
              deps=[tr], reads=(B["pb7"],), writes=(B["wsrow"],), risky=True)
        t_ts("VE", wsrow[0:1, ws_idx:ws_idx + 1], wsrow[0:1, 4:5], 1.0 / (nrows * ncols),
             1e-5, ALU.mult, ALU.max, reads=(B["wsrow"],), writes=(B["wsrow"],), risky=True)
        t_recip("VE", wsrow[0:1, 5:6], wsrow[0:1, ws_idx:ws_idx + 1],
                reads=(B["wsrow"],), writes=(B["wsrow"],), risky=True)
        bc = t_mm(pbk[7][:, 128:129], ones_row_b[0:1, :], wsrow[0:1, 5:6], True, True,
                  reads=(B["wsrow"], B["consts"]), writes=(B["pb7"],))
        t_copy("VE", wsc[:, ws_idx:ws_idx + 1], pbk[7][:, 128:129],
               deps=[bc], reads=(B["pb7"],), writes=(B["wsc"],), risky=True)

    def quant_p2(w_d, w_scr, nrows, ncols, ws_idx, ssem, stage, q_load="SY",
                 slots=P0SLOTS):
        hw = 512
        tiles = [(r, h) for r in range(nrows // 128) for h in range(ncols // hw)]
        NS = len(slots)
        stores = []
        for i, (r, h) in enumerate(tiles):
            sap, sbn = slots[i % NS]
            wB = B[sbn]
            w8B = B["w8stage"] if i % 2 == 0 else B["w8stage1"]
            s8 = (i % 2) * 1024
            eng = "VE"
            src = w_d[r * 128:(r + 1) * 128, h * hw:(h + 1) * hw]
            ldsem = f"stg{i % NS}" if q_load == "SY" else f"sg2{i % NS}"
            dma(q_load, sap, src, writes=(wB,), dsem=ldsem)
            t_ts(eng, sap, sap, wsc[:, ws_idx:ws_idx + 1],
                 1.0, ALU.mult, ALU.min, reads=(wB, B["wsc"]), writes=(wB,))
            t_ts(eng, sap, sap, -1.0, None,
                 ALU.max, reads=(wB,), writes=(wB,))
            t_copy("VE", w8stage[:, s8:s8 + hw], sap,
                   reads=(wB,), writes=(w8B,))
            wbfB = B["wbf0"] if i % 2 == 0 else B["wbf1"]
            t_copy("SE", stage[:, s8:s8 + hw], w8stage[:, s8:s8 + hw],
                   reads=(w8B,), writes=(wbfB,))
            st = dma("SE", w_scr[r * 128:(r + 1) * 128, h * hw:(h + 1) * hw],
                     stage[:, s8:s8 + hw], reads=(wbfB,), dsem=ssem)
            stores.append(st)
        return stores

    def precast(w_d, w_bf, rlo, rhi, ncols, stores):
        hw = 1024
        for r in range(rlo // 128, rhi // 128):
            for h in range(ncols // hw):
                dma("GP", pcstage[:, :],
                    w_d[r * 128:(r + 1) * 128, h * hw:(h + 1) * hw],
                    writes=(B["wqb0"],), dsem="pcL")
                st = dma("GP", w_bf[r * 128:(r + 1) * 128, h * hw:(h + 1) * hw],
                         pcstage[:, :], reads=(B["wqb0"],), dsem="pcS")
                stores.append(st)

    qd, qg, qu, pcs = [], [], [], []

    def ln_prologue(chunks, slot_m, slot_rstd):
        lo, hi = chunks[0][0], chunks[-1][1]
        for (c0, c1) in chunks:
            w = c1 - c0
            cps, sqs = [], []
            for fc in range(FC):
                cp = t_copy("VE", XBFb[:, fc * SW: fc * SW + w],
                            xb_t[:, fc * TH + c0: fc * TH + c0 + w],
                            reads=tuple(xbufs(c0, c1)), writes=(B["XBFb"],))
                cps.append(cp)
            for fc in range(FC):
                sq = t_act(X2Bb[:, fc * SW: fc * SW + w], XBFb[:, fc * SW: fc * SW + w],
                           AF.Square, deps=[cps[fc]], reads=(B["XBFb"],), writes=(B["X2Bb"],))
                sqs.append(sq)
            mm = None
            for fc in range(FC):
                mm = t_mm(pbk[5][0:1, 0:w], ones_col_b[:, 0:1], XBFb[:, fc * SW: fc * SW + w],
                          fc == 0, fc == FC - 1, deps=[cps[fc]],
                          reads=(B["XBFb"], B["consts"]), writes=(B["pb5"],))
            t_copy("VE", mrow[0:1, c0:c1], pbk[5][0:1, 0:w], deps=[mm],
                   reads=(B["pb5"],), writes=(B["mrow"],), risky=True)
            mm = None
            for fc in range(FC):
                mm = t_mm(pbk[6][0:1, 0:w], ones_col_b[:, 0:1], X2Bb[:, fc * SW: fc * SW + w],
                          fc == 0, fc == FC - 1, deps=[sqs[fc]],
                          reads=(B["X2Bb"], B["consts"]), writes=(B["pb6"],))
            t_copy("VE", esqrow[0:1, c0:c1], pbk[6][0:1, 0:w], deps=[mm],
                   reads=(B["pb6"],), writes=(B["esqrow"],), risky=True)
        t_tt("VE", vrow[0:1, lo:hi], mrow[0:1, lo:hi], mrow[0:1, lo:hi], ALU.mult,
             reads=(B["mrow"],), writes=(B["vrow"],), risky=True)
        t_tt("VE", vrow[0:1, lo:hi], esqrow[0:1, lo:hi], vrow[0:1, lo:hi], ALU.subtract,
             reads=(B["esqrow"], B["vrow"]), writes=(B["vrow"],), risky=True)
        sl = secs(hi - lo)
        ns = len(sl)
        assert ns <= 17
        for si, (off, w) in enumerate(sl):
            tr = t_tr(pbk[7][0:w, si:si + 1], vrow[0:1, lo + off:lo + off + w],
                      reads=(B["vrow"], B["consts"]), writes=(B["pb7"],))
        ev = t_copy("VE", colb[:, 0:ns], pbk[7][:, 0:ns],
                    reads=(B["pb7"],), writes=(B["colb"],), risky=True)
        t_ts("VE", colb[:, 0:ns], colb[:, 0:ns], EPS, None, ALU.add,
             reads=(B["colb"],), writes=(B["colb"],), risky=True)
        t_recip("VE", colb[:, 17:17 + ns], colb[:, 0:ns],
                reads=(B["colb"],), writes=(B["colb"],), risky=True)
        t_act(colb[:, 34:34 + ns], colb[:, 17:17 + ns], AF.Sqrt,
              reads=(B["colb"],), writes=(B["colb"],), risky=True)
        t_tt("VE", colb[:, 51:51 + ns], colb[:, 34:34 + ns], colb[:, 34:34 + ns], ALU.mult,
             reads=(B["colb"],), writes=(B["colb"],), risky=True)
        t_tt("VE", colb[:, 51:51 + ns], colb[:, 51:51 + ns], colb[:, 0:ns], ALU.mult,
             reads=(B["colb"],), writes=(B["colb"],), risky=True)
        t_ts("VE", colb[:, 51:51 + ns], colb[:, 51:51 + ns], -0.5, 1.5, ALU.mult, ALU.add,
             reads=(B["colb"],), writes=(B["colb"],), risky=True)
        t_tt("VE", colb[:, 34:34 + ns], colb[:, 34:34 + ns], colb[:, 51:51 + ns], ALU.mult,
             reads=(B["colb"],), writes=(B["colb"],), risky=True)
        for si, (off, w) in enumerate(sl):
            tr = t_tr(pbk[5][0:1, 0:w], colb[0:w, 34 + si:35 + si],
                      reads=(B["colb"], B["consts"]), writes=(B["pb5"],))
            t_copy("VE", rstdb[0:1, lo + off:lo + off + w], pbk[5][0:1, 0:w],
                   deps=[tr], reads=(B["pb5"],), writes=(B["rstdb"],), risky=True)
        dm = dma("SE", rowslice(slot_m, lo, hi), mrow[0:1, lo:hi],
                 reads=(B["mrow"],), writes=(B[f"rows{slot_m}"],), dsem=f"rs{slot_m}")
        dr = dma("SE", rowslice(slot_rstd, lo, hi), rstdb[0:1, lo:hi],
                 reads=(B["rstdb"],), writes=(B[f"rows{slot_rstd}"],), dsem=f"rs{slot_rstd}")
        return dm, dr

    def mx_chain(w, src_red, src_bufs, pband, pbB, colap, colB, rowap, rowB, mbap, mbB,
                 bdst, bdstB, pre_m1=None):
        if pre_m1 is None:
            ap, n_inner = src_red
            m1 = t_red("VE", mbap[:, 0:w], ap, ALU.max, absval=True,
                       reads=src_bufs, writes=(mbB,), risky=True)
        else:
            m1 = pre_m1
        sl = secs(w)
        ns = len(sl)
        for si, (off, sw_) in enumerate(sl):
            reg = 128 + (si % 2) * 128
            tr = t_tr(pband[0:sw_, reg:reg + 128], mbap[:, off:off + sw_],
                      deps=[m1], reads=(mbB, B["consts"]), writes=(pbB,))
            t_red("VE", colap[0:sw_, si:si + 1], pband[0:sw_, reg:reg + 128],
                  ALU.max, deps=[tr], reads=(pbB,), writes=(colB,), risky=True)
        t_ts("VE", colap[:, 3:3 + ns], colap[:, 0:ns], EPS, None,
             ALU.max, reads=(colB,), writes=(colB,), risky=True)
        t_recip("VE", colap[:, 6:6 + ns], colap[:, 3:3 + ns],
                reads=(colB,), writes=(colB,), risky=True)
        t_ts("VE", colap[:, 9:9 + ns], colap[:, 6:6 + ns],
             127.0, None, ALU.mult, reads=(colB,), writes=(colB,), risky=True)
        for si, (off, sw_) in enumerate(sl):
            tr = t_tr(pband[0:1, 0:sw_], colap[0:sw_, 9 + si:10 + si],
                      reads=(colB, B["consts"]), writes=(pbB,))
            t_copy("VE", rowap[0:1, off:off + sw_], pband[0:1, 0:sw_],
                   deps=[tr], reads=(pbB,), writes=(rowB,), risky=True)
            tr2 = t_tr(pband[0:1, 0:sw_], colap[0:sw_, 3 + si:4 + si],
                       reads=(colB, B["consts"]), writes=(pbB,))
            t_copy("VE", rowap[0:1, SW + off:SW + off + sw_], pband[0:1, 0:sw_],
                   deps=[tr2], reads=(pbB,), writes=(rowB,), risky=True)
        return mmbcast(rowap[0:1, 0:w], rowB, w, pband, pbB, bdst, bdstB)

    def mmbcast(row_ap, rowB, w, pband, pbB, dst, dstB):
        mm = t_mm(pband[:, 0:w], ones_row_b[0:1, :], row_ap, True, True,
                  reads=(rowB, B["consts"]), writes=(pbB,))
        return t_copy("VE", dst, pband[:, 0:w], deps=[mm],
                      reads=(pbB,), writes=(dstB,), risky=True)

    def ffn_call(chunks, slot_m, ev_m, ev_r, dbg=False, emit_extra=None):
        pend = []

        def emit_down(st_):
            c0, c1, w, rnd, bfd = st_["c0"], st_["c1"], st_["w"], st_["rnd"], st_["bfd"]
            for hc in range(8):
                ws_ = hc % 2
                wdB = B[f"WDb{ws_}"]
                wdt = WDb[:, ws_ * 4096:(ws_ + 1) * 4096]
                dma("SE", wdt,
                    wd_s.rearrange("(ic p) h -> p ic h", p=128)[:, :, hc * 128:(hc + 1) * 128],
                    deps=[qd[-1]], writes=(wdB,), dsem=f"wd{ws_}")
                bank = pbk[4 + hc % 2]
                bB = B[f"pb{4 + hc % 2}"]
                dmm = None
                for ic in range(32):
                    dmm = t_mm(bank[:, 0:w], wdt[:, ic * 128:(ic + 1) * 128],
                               HIb[:, ic * SW: ic * SW + w], ic == 0, ic == 31,
                               deps=[rnd[ic]], reads=(wdB, B["HIb"]), writes=(bB,))
                tt = t_tt("VE", GTb[:, 2 * SW:2 * SW + w], bank[:, 0:w],
                          BCb[:, 4 * SW:4 * SW + w], ALU.mult, deps=[dmm, bfd],
                          reads=(bB, B["BC4"]), writes=(B["GTb"],))
                t_tt("VE", xb_t[:, hc * TH + c0: hc * TH + c0 + w],
                     xb_t[:, hc * TH + c0: hc * TH + c0 + w], GTb[:, 2 * SW:2 * SW + w],
                     ALU.add, deps=[tt], reads=(B["GTb"],) + tuple(xbufs(c0, c1)),
                     writes=tuple(xbufs(c0, c1)))

        def dtap(name, src, srcbuf, shape, dt):
            if not (DEBUG_TAPS and dbg):
                return
            td = nc.dram_tensor(name, shape, dt, kind="ExternalOutput").ap()
            dma("SY", td, src, reads=(srcbuf,))
        qstate = {}

        def emit_quant(ci):
            c0, c1 = chunks[ci]
            w = c1 - c0
            ui = ci % 2
            uB = B[f"UIb{ui}"]
            uiT = UIb[:, ui * FC * SW:(ui + 1) * FC * SW]
            mb = bcast(rowslice(slot_m, c0, c1), BCb[:, 0:w], writes=(B["BC0"],),
                       deps=[ev_m], dsem="bc0")
            for fc in range(FC):
                t_tt("GP", Cb[:, fc * SW: fc * SW + w], xb_t[:, fc * TH + c0: fc * TH + c0 + w],
                     BCb[:, 0:w], ALU.subtract, deps=[mb],
                     reads=tuple(xbufs(c0, c1)) + (B["BC0"],), writes=(B["Cb"],))
            bsu = mx_chain(w, (Cb[:, 0:FC * SW].rearrange("p (fc t) -> p t fc", fc=FC)[:, 0:w, :], FC),
                           (B["Cb"],), pbk[6], B["pb6"], colbU, B["colbU"], rowbU, B["rowbU"],
                           MbU, B["MbU"], BCb[:, SW:SW + w], B["BC1"])
            t0 = t_tt("VE", rowbU[0:1, 2 * SW:2 * SW + w], rowbU[0:1, SW:SW + w],
                      rstdb[0:1, c0:c1], ALU.mult, deps=[ev_r],
                      reads=(B["rowbU"], B["rstdb"]), writes=(B["rowbU"],), risky=True)
            fg = t_ts("VE", rowbU[0:1, 3 * SW:3 * SW + w], rowbU[0:1, 2 * SW:2 * SW + w],
                      wsrow[0:1, 0:1], 1.0 / 127, ALU.mult, ALU.mult, deps=[wsud],
                      reads=(B["rowbU"], B["wsrow"]), writes=(B["rowbU"],), risky=True)
            fsl = 4 * SW + (ci % 2) * SW
            fdu = t_ts("VE", rowbU[0:1, fsl:fsl + w], rowbU[0:1, 2 * SW:2 * SW + w],
                       wsrow[0:1, 3:4], 0.5 / (127.0 * 127.0), ALU.mult, ALU.mult,
                       reads=(B["rowbU"], B["wsrow"]), writes=(B["rowbU"],), risky=True)
            bfg = mmbcast(rowbU[0:1, 3 * SW:3 * SW + w], B["rowbU"], w, pbk[6], B["pb6"],
                          BCb[:, 2 * SW:2 * SW + w], B["BC2"])
            for fc in range(FC):
                sl8 = (fc % 2) * SW
                u8B = B["u8a"] if fc % 2 == 0 else B["u8b"]
                t_tt("VE", Cb[:, fc * SW: fc * SW + w], Cb[:, fc * SW: fc * SW + w],
                     BCb[:, SW:SW + w], ALU.mult, deps=[bsu],
                     reads=(B["Cb"], B["BC1"]), writes=(B["Cb"],))
                t_copy("VE", u8roll[:, sl8:sl8 + w], Cb[:, fc * SW: fc * SW + w],
                       reads=(B["Cb"],), writes=(u8B,), risky=True)
                t_copy("SE", uiT[:, fc * SW: fc * SW + w], u8roll[:, sl8:sl8 + w],
                       reads=(u8B,), writes=(uB,), risky=True)
            ev_ui = P.ops[-1]
            qstate[ci] = dict(uiT=uiT, uB=uB, ev_ui=ev_ui, bfg=bfg, w=w, c0=c0, c1=c1)

        emit_quant(0)
        extras = list(emit_extra or [])
        for ci, (c0, c1) in enumerate(chunks):
            st0 = qstate[ci]
            uiT, uB, ev_ui, bfg, w = st0["uiT"], st0["uB"], st0["ev_ui"], st0["bfg"], st0["w"]
            for oc in range(32):
                pr = oc // 2
                j = oc % 2
                ws_ = pr % 2
                wgB, wuB = B[f"WGb{ws_}"], B[f"WUb{ws_}"]
                wgt = WGb[:, ws_ * 2048:(ws_ + 1) * 2048]
                wut = WUb[:, ws_ * 2048:(ws_ + 1) * 2048]
                if j == 0:
                    dma("SY", wgt,
                        wg_s.rearrange("(fc p) o -> p fc o", p=128)[:, :, pr * 256:(pr + 1) * 256],
                        deps=[qg[-1]], writes=(wgB,), dsem=f"wg{ws_}")
                    dma("SY", wut,
                        wu_s.rearrange("(fc p) o -> p fc o", p=128)[:, :, pr * 256:(pr + 1) * 256],
                        deps=[qu[-1]], writes=(wuB,), dsem=f"wu{ws_}")
                gbank = pbk[oc % 3]
                gB = B[f"pb{oc % 3}"]
                ubank = pbk[3 + oc % 2]
                uB2 = B[f"pb{3 + oc % 2}"]
                gmm = None
                for f in range(FC):
                    gmm = t_mm(gbank[:, 0:w], wgt[:, f * 256 + j * 128: f * 256 + j * 128 + 128],
                               uiT[:, f * SW: f * SW + w], f == 0, f == FC - 1,
                               deps=[ev_ui], reads=(wgB, uB), writes=(gB,))
                umm = None
                for f in range(FC):
                    umm = t_mm(ubank[:, 0:w], wut[:, f * 256 + j * 128: f * 256 + j * 128 + 128],
                               uiT[:, f * SW: f * SW + w], f == 0, f == FC - 1,
                               deps=[ev_ui], reads=(wuB, uB), writes=(uB2,))
                gt = GTb[:, (oc % 2) * SW:(oc % 2) * SW + w]
                g1 = t_tt("VE", gt, gbank[:, 0:w], BCb[:, 2 * SW:2 * SW + w], ALU.mult,
                          deps=[gmm, bfg], reads=(gB, B["BC2"]), writes=(B["GTb"],))
                sl_ = t_act(gt, gt, AF.Silu, deps=[g1], reads=(B["GTb"],), writes=(B["GTb"],))
                hh = t_tt("VE", Hb[:, oc * SW: oc * SW + w], ubank[:, 0:w], gt, ALU.mult,
                          deps=[umm, sl_], reads=(uB2, B["GTb"]), writes=(B["Hb"],))
                if oc == 15:
                    m2a = t_red("VE", MbH[:, 0:w],
                                Hb[:, 0:16 * SW].rearrange("p (oc t) -> p t oc", oc=16)[:, 0:w, :],
                                ALU.max, absval=True, deps=[hh], reads=(B["Hb"],),
                                writes=(B["MbH"],), risky=True)
                if oc == 31:
                    m2h = t_red("VE", MbU[:, 0:w],
                                Hb[:, 16 * SW:32 * SW].rearrange("p (oc t) -> p t oc", oc=16)[:, 0:w, :],
                                ALU.max, absval=True, deps=[hh], reads=(B["Hb"],),
                                writes=(B["MbU"],), risky=True)
                    m2b = t_tt("VE", MbH[:, 0:w], MbH[:, 0:w], MbU[:, 0:w], ALU.max,
                               deps=[m2a, m2h], reads=(B["MbH"], B["MbU"]),
                               writes=(B["MbH"],), risky=True)
            if ci == 0:
                dtap("tap_h", Hb[:, :], B["Hb"], [128, 32 * SW], F32)
            if ci + 1 < len(chunks):
                emit_quant(ci + 1)
            if pend:
                emit_down(pend.pop(0))
            if extras:
                extras.pop(0)()
            bsh = mx_chain(w, None, (B["Hb"],), pbk[7], B["pb7"], colbH, B["colbH"],
                           rowbH, B["rowbH"], MbH, B["MbH"],
                           BCb[:, 3 * SW:3 * SW + w], B["BC3"], pre_m1=m2b)
            fsl2 = 4 * SW + (ci % 2) * SW
            fd = t_tt("VE", rowbH[0:1, SW:SW + w], rowbH[0:1, SW:SW + w],
                      rowbU[0:1, fsl2:fsl2 + w], ALU.mult,
                      reads=(B["rowbH"], B["rowbU"]), writes=(B["rowbH"],), risky=True)
            bfd = mmbcast(rowbH[0:1, SW:SW + w], B["rowbH"], w, pbk[7], B["pb7"],
                          BCb[:, 4 * SW:4 * SW + w], B["BC4"])
            rnd = []
            for oc in range(32):
                sl8 = (oc % 2) * SW
                h8B = B["h8a"] if oc % 2 == 0 else B["h8b"]
                t_tt("VE", Hb[:, oc * SW: oc * SW + w], Hb[:, oc * SW: oc * SW + w],
                     BCb[:, 3 * SW:3 * SW + w], ALU.mult, deps=[bsh],
                     reads=(B["Hb"], B["BC3"]), writes=(B["Hb"],))
                t_copy("VE", h8roll[:, sl8:sl8 + w], Hb[:, oc * SW: oc * SW + w],
                       reads=(B["Hb"],), writes=(h8B,), risky=True)
                rn = t_copy("SE", HIb[:, oc * SW: oc * SW + w], h8roll[:, sl8:sl8 + w],
                            reads=(h8B,), writes=(B["HIb"],), risky=True)
                rnd.append(rn)
            if ci == 0:
                dtap("tap_hi", HIb[:, :], B["HIb"], [128, 32 * SW], BF16)
            pend.append(dict(c0=c0, c1=c1, w=w, rnd=rnd, bfd=bfd))
        emit_down(pend.pop(0))

    def attn_phase(ev_m2, ev_r2):
        bV = [B[f"Vb{i}"] for i in range(5)]
        for q in range(4):
            Qs = 512 * q
            new0 = 0 if q == 0 else 128
            hi_local = min(640, TH - Qs)
            if q > 0:
                for fc in range(FC):
                    t_copy("SE", U2b[:, fc * 640: fc * 640 + 128],
                           stashb[:, fc * 128:(fc + 1) * 128],
                           reads=(B["stash"],), writes=(B["U2b"],), risky=True)
                t_copy("SE", Vb[:, 0:1024], Vb[:, 4 * 1024:5 * 1024],
                       reads=(bV[4],), writes=(bV[0],), risky=True)
                for g in range(4):
                    t_copy("SE", KPb[:, g * 640:g * 640 + 128],
                           KPb[:, g * 640 + 512:g * 640 + 640],
                           reads=(B["KPb"],), writes=(B["KPb"],), risky=True)
            g0, g1_ = Qs + new0, Qs + hi_local
            nw = g1_ - g0
            bm2 = bcast(rows_s[R_M2, g0:g1_].rearrange("(one t) -> one t", one=1),
                        PHt2[:, 0:nw], writes=(B["PHt2"],), deps=[ev_m2], dsem="ph2")
            br2 = bcast(rows_s[R_RSTD2, g0:g1_].rearrange("(one t) -> one t", one=1),
                        PHt2[:, 640:640 + nw], writes=(B["PHt2"],), deps=[ev_r2], dsem="ph2")
            u2evs = []
            for fc in range(FC):
                s1 = t_tt("GP", PHt[:, 0:nw], xb_t[:, fc * TH + g0: fc * TH + g1_],
                          PHt2[:, 0:nw], ALU.subtract, deps=[bm2],
                          reads=tuple(xbufs(g0, g1_)) + (B["PHt2"],), writes=(B["PHt"],))
                s2 = t_tt("GP", U2b[:, fc * 640 + new0: fc * 640 + hi_local], PHt[:, 0:nw],
                          PHt2[:, 640:640 + nw], ALU.mult, deps=[br2, s1],
                          reads=(B["PHt"], B["PHt2"]), writes=(B["U2b"],))
                u2evs.append(s2)
            if hi_local < 640:
                for fc in range(FC):
                    u2evs.append(t_memset("VE", U2b[:, fc * 640 + hi_local: fc * 640 + 640],
                                          0.0, writes=(B["U2b"],), risky=True))
            ev_u2 = u2evs[-1]
            if q < 3:
                for fc in range(FC):
                    t_copy("SE", stashb[:, fc * 128:(fc + 1) * 128],
                           U2b[:, fc * 640 + 512: fc * 640 + 640],
                           deps=[ev_u2], reads=(B["U2b"],), writes=(B["stash"],), risky=True)
            tl_range = list(range(0 if q == 0 else 1, 5))
            for tl in tl_range:
                if min(128, max(0, TH - (Qs + tl * 128))) < 128:
                    t_memset("VE", Vb[:, tl * 1024:(tl + 1) * 1024], 0.0,
                             writes=(bV[tl],), risky=True)
            for vb in range(4):
                if vb % 2 == 0:
                    vreg, vbufs = WVb[:, 0:2048], (B["WVb0"], B["WVb1"])
                else:
                    vreg, vbufs = WQKVb[:, 0:2048], (B["WQKVb0"], B["WQKVb1"])
                ldv = dma("SY", vreg,
                          wqkv_bf.rearrange("(fc p) c -> p fc c", p=128)[:, :, 2 * D + vb * 256: 2 * D + (vb + 1) * 256],
                          deps=[pcs[-1]], writes=vbufs, dsem="wv")
                for tl in tl_range:
                    gt0 = Qs + tl * 128
                    m_rows = min(128, max(0, TH - gt0))
                    if m_rows == 0:
                        continue
                    vbank = pbk[0] if tl % 2 == 0 else pbk[1]
                    vbB = B["pb0"] if tl % 2 == 0 else B["pb1"]
                    vmm = None
                    for f in range(FC):
                        vmm = t_mm(vbank[0:m_rows, 0:256],
                                   U2b[:, f * 640 + tl * 128: f * 640 + tl * 128 + m_rows],
                                   vreg[:, f * 256: f * 256 + 256],
                                   f == 0, f == FC - 1,
                                   deps=[ev_u2, ldv], reads=(B["U2b"],) + vbufs,
                                   writes=(vbB,))
                    t_copy("VE", Vb[0:m_rows, tl * 1024 + vb * 256: tl * 1024 + vb * 256 + 256],
                           vbank[0:m_rows, 0:256], deps=[vmm], reads=(vbB,), writes=(bV[tl],))
            q_lo, q_hi = 3, 515
            phi_evs = []
            for h in range(HEADS):
                g = h // 2
                rbase = (h % 2) * 64
                for which in ("q", "k"):
                    if which == "q":
                        ranges = [(q_lo, q_hi)]
                    else:
                        mid = min(new0 + 512, hi_local)
                        ranges = [(new0, mid)]
                        if mid < hi_local:
                            ranges.append((mid, hi_local))
                    slotn = h % 2
                    wB = B[f"WQKVb{slotn}"]
                    base = 0 if which == "q" else D
                    qi = h * 2 + (0 if which == "q" else 1)
                    rbank, rbB = (pbk[1], B["pb1"]) if qi % 2 == 0 else (pbk[3], B["pb3"])
                    phbank, phB = (pbk[2], B["pb2"]) if qi % 2 == 0 else (pbk[4], B["pb4"])
                    ldw = dma("SY", WQKVb[:, slotn * 1024: slotn * 1024 + 1024],
                              wqkv_bf.rearrange("(fc p) c -> p fc c", p=128)[:, :, base + h * 128: base + h * 128 + 128],
                              deps=[pcs[-1]], writes=(wB,), dsem=f"qk{slotn}")
                    for clo, chi in ranges:
                        cw = chi - clo
                        rmm = None
                        for f in range(FC):
                            rmm = t_mm(rbank[:, 0:cw],
                                       WQKVb[:, slotn * 1024 + f * 128: slotn * 1024 + f * 128 + 128],
                                       U2b[:, f * 640 + clo: f * 640 + chi], f == 0, f == FC - 1,
                                       deps=[ldw, ev_u2], reads=(wB, B["U2b"]), writes=(rbB,))
                        rb = QRb[:, (0 if which == "q" else 1) * 640 + clo:
                                 (0 if which == "q" else 1) * 640 + chi]
                        ev_r = t_copy("VE", rb, rbank[:, 0:cw], deps=[rmm],
                                      reads=(rbB,), writes=(B["QRb"],))
                        ph = t_mm(phbank[0:FEAT, 0:cw], proj_b[:, :], rb, True, True,
                                  deps=[ev_r], reads=(B["QRb"], B["consts"]), writes=(phB,))
                        e1 = t_ts("VE", PHt[0:FEAT, 0:cw], phbank[0:FEAT, 0:cw], 0.0, None,
                                  ALU.min, deps=[ph], reads=(phB,), writes=(B["PHt"],))
                        e2 = t_act(PHt2[0:FEAT, 0:cw], PHt[0:FEAT, 0:cw], AF.Exp,
                                   deps=[e1], reads=(B["PHt"],), writes=(B["PHt2"],))
                        e3 = t_ts("VE", PHt[0:FEAT, 0:cw], phbank[0:FEAT, 0:cw], 0.0, None,
                                  ALU.max, deps=[ph], reads=(phB,), writes=(B["PHt"],))
                        dstb, dB = (QPb, B["QPb"]) if which == "q" else (KPb, B["KPb"])
                        if rbase == 0:
                            dst = dstb[0:64, g * 640 + clo: g * 640 + chi]
                            mk = maskk_b[0:64, Qs + clo:Qs + chi]
                        else:
                            dst = dstb[64:128, g * 640 + clo: g * 640 + chi]
                            mk = maskk_b[0:64, Qs + clo:Qs + chi]
                        if which == "q":
                            e4 = t_tt("VE", dst, PHt2[0:FEAT, 0:cw], PHt[0:FEAT, 0:cw], ALU.add,
                                      deps=[e2, e3], reads=(B["PHt"], B["PHt2"]), writes=(dB,),
                                      risky=True)
                            phi_evs.append(e4)
                        else:
                            e4 = t_tt("VE", PHt[0:FEAT, 0:cw], PHt2[0:FEAT, 0:cw],
                                      PHt[0:FEAT, 0:cw], ALU.add, deps=[e2, e3],
                                      reads=(B["PHt"], B["PHt2"]), writes=(B["PHt"],), risky=True)
                            e5 = t_tt("VE", dst, PHt[0:FEAT, 0:cw], mk, ALU.mult,
                                      reads=(B["PHt"], B["consts"]), writes=(dB,), risky=True)
                            phi_evs.append(e5)
            if hi_local < 640:
                for g in range(4):
                    phi_evs.append(t_memset("VE", KPb[:, g * 640 + hi_local:(g + 1) * 640],
                                            0.0, writes=(B["KPb"],), risky=True))
            ev_phi = phi_evs[-1]
            for g in range(4):
                kp = KPb[:, g * 640:(g + 1) * 640]
                kw = KWb[:, 0:520]
                a1 = t_tt("VE", kw, kp[:, 0:520], kp[:, 1:521], ALU.add, deps=[ev_phi],
                          reads=(B["KPb"],), writes=(B["KWb"],), risky=True)
                a2 = t_tt("VE", PHt[:, 0:520], kp[:, 2:522], kp[:, 3:523], ALU.add,
                          deps=[ev_phi], reads=(B["KPb"],), writes=(B["PHt"],), risky=True)
                a3 = t_tt("VE", PHt2[:, 0:520], kp[:, 4:524], kp[:, 5:525], ALU.add,
                          deps=[ev_phi], reads=(B["KPb"],), writes=(B["PHt2"],), risky=True)
                t_tt("VE", kw, kw, PHt[:, 0:520], ALU.add,
                     reads=(B["KWb"], B["PHt"]), writes=(B["KWb"],), risky=True)
                t_tt("VE", kw, kw, PHt2[:, 0:520], ALU.add,
                     reads=(B["KWb"], B["PHt2"]), writes=(B["KWb"],), risky=True)
                a6 = t_tt("VE", kw, kw, kp[:, 6:526], ALU.add,
                          reads=(B["KWb"], B["KPb"]), writes=(B["KWb"],), risky=True)
                pp = t_tt("VE", QRb[:, 0:512], QPb[:, g * 640 + q_lo: g * 640 + q_hi],
                          kw[:, 0:512], ALU.mult, deps=[a6, ev_phi],
                          reads=(B["QPb"], B["KWb"]), writes=(B["QRb"],), risky=True)
                zmm = []
                for b_ in range(4):
                    mm = t_mm(pbk[3][:, b_ * 2: b_ * 2 + 2], QRb[:, b_ * 128:(b_ + 1) * 128],
                              pairmask_b[:, :], True, True, deps=[pp],
                              reads=(B["QRb"], B["consts"]), writes=(B["pb3"],))
                    zmm.append(mm)
                t_ts("VE", zq[:, 0:8], pbk[3][:, 0:8], 1e-8, None, ALU.add,
                     deps=[zmm[-1]], reads=(B["pb3"],), writes=(B["zq"],), risky=True)
                t_recip("VE", zq[:, 8:16], zq[:, 0:8], reads=(B["zq"],), writes=(B["zq"],),
                        risky=True)
                sc_evs = []
                for h2 in range(2):
                    for b_ in range(4):
                        col = 8 + b_ * 2 + h2
                        tr = t_tr(pbk[3][0:1, b_ * 128:128 + b_ * 128],
                                  zq[:, col:col + 1],
                                  reads=(B["zq"], B["consts"]), writes=(B["pb3"],))
                    ev = t_copy("VE", rowbU[0:1, 0:512], pbk[3][0:1, 0:512],
                                reads=(B["pb3"],), writes=(B["rowbU"],), risky=True)
                    mm = t_mm(pbk[3][:, 0:512] if False else pbk[2][:, 0:512],
                              ones_row_b[0:1, :], rowbU[0:1, 0:512], True, True,
                              deps=[ev], reads=(B["rowbU"], B["consts"]), writes=(B["pb2"],))
                    ev2 = t_copy("VE", ZBb[:, 0:512], pbk[2][:, 0:512], deps=[mm],
                                 reads=(B["pb2"],), writes=(B["ZBb"],), risky=True)
                    rb_ = h2 * 64
                    sc_evs.append(t_tt("VE", QPb[rb_:rb_ + 64, g * 640 + q_lo: g * 640 + q_hi],
                                       QPb[rb_:rb_ + 64, g * 640 + q_lo: g * 640 + q_hi],
                                       ZBb[rb_:rb_ + 64, 0:512], ALU.mult, deps=[ev2],
                                       reads=(B["QPb"], B["ZBb"]), writes=(B["QPb"],),
                                       risky=True))
                sc_ = sc_evs[-1]
                for h2 in range(2):
                    rbase = h2 * 64
                    h = 2 * g + h2
                    stj = {}
                    for j in range(5):
                        if j < 4:
                            qs = j * 128 + 3
                            sbank = pbk[6] if j % 2 == 0 else pbk[0]
                            sbB = B["pb6"] if j % 2 == 0 else B["pb0"]
                            sA = SBA[:, 0:128] if j % 2 == 0 else KWb[:, 0:128]
                            sAB = B["SBA"] if j % 2 == 0 else B["KWb"]
                            sB2 = SBB[0:8, 0:128] if j % 2 == 0 else KWb[0:8, 128:256]
                            sBB = B["SBB"] if j % 2 == 0 else B["KWb"]
                            sa = t_mm(sbank[:, 0:128],
                                      KPb[rbase:rbase + 64, g * 640 + j * 128: g * 640 + (j + 1) * 128],
                                      QPb[rbase:rbase + 64, g * 640 + qs: g * 640 + qs + 128],
                                      True, True, deps=[sc_], reads=(B["KPb"], B["QPb"]),
                                      writes=(sbB,))
                            sb_ = t_mm(sbank[0:8, 128:256],
                                       KPb[rbase:rbase + 64, g * 640 + (j + 1) * 128: g * 640 + (j + 1) * 128 + 8],
                                       QPb[rbase:rbase + 64, g * 640 + qs: g * 640 + qs + 128],
                                       True, True, deps=[sc_], reads=(B["KPb"], B["QPb"]),
                                       writes=(sbB,))
                            maA = t_tt("VE", sA, sbank[:, 0:128], maskA_b[:, :],
                                       ALU.mult, deps=[sa], reads=(sbB, B["consts"]),
                                       writes=(sAB,))
                            maB = t_tt("VE", sB2, sbank[0:8, 128:256], maskB_b[:, :],
                                       ALU.mult, deps=[sb_], reads=(sbB, B["consts"]),
                                       writes=(sBB,))
                            stj[j] = (sA, sAB, sB2, sBB, maA, maB)
                        if j >= 1:
                            jj = j - 1
                            sA, sAB, sB2, sBB, maA, maB = stj.pop(jj)
                            abank = pbk[5] if jj % 2 == 0 else pbk[1]
                            abB = B["pb5"] if jj % 2 == 0 else B["pb1"]
                            am1 = t_mm(abank[:, 256:384],
                                       Vb[:, jj * 1024 + h * 128: jj * 1024 + (h + 1) * 128],
                                       sA, True, False, deps=[maA],
                                       reads=(bV[jj], sAB), writes=(abB,))
                            am2 = t_mm(abank[:, 256:384],
                                       Vb[0:6, (jj + 1) * 1024 + h * 128: (jj + 1) * 1024 + (h + 1) * 128],
                                       sB2[0:6, 0:128], False, True, deps=[maB],
                                       reads=(bV[jj + 1], sBB), writes=(abB,))
                            t_copy("VE", AFMb[:, h * 512 + jj * 128: h * 512 + (jj + 1) * 128],
                                   abank[:, 256:384], deps=[am2], reads=(abB,),
                                   writes=(B["AFMb"],))
            g0 = Qs + 3
            for oc in range(8):
                wsl = oc % 2
                wtile = WOb[:, 0:1024] if wsl == 0 else WQKVb[:, 0:1024]
                wtB = B["WOb"] if wsl == 0 else B["WQKVb0"]
                ldo = dma("SY", wtile,
                          wout_bf.rearrange("(fc p) c -> p fc c", p=128)[:, :, oc * 128:(oc + 1) * 128],
                          deps=[pcs[-1]], writes=(wtB,), dsem=f"wo{wsl}")
                obank = pbk[4] if wsl == 0 else pbk[5]
                obB = B["pb4"] if wsl == 0 else B["pb5"]
                omm = None
                for d_ in range(FC):
                    omm = t_mm(obank[:, 0:512],
                               wtile[:, d_ * 128:(d_ + 1) * 128],
                               AFMb[:, d_ * 512:(d_ + 1) * 512], d_ == 0, d_ == FC - 1,
                               deps=[ldo], reads=(wtB, B["AFMb"]), writes=(obB,))
                t_stt("VE", xb_t[:, oc * TH + g0: oc * TH + g0 + 512], obank[:, 0:512],
                      bout_b[:, oc:oc + 1], xb_t[:, oc * TH + g0: oc * TH + g0 + 512],
                      ALU.add, ALU.add, deps=[omm],
                      reads=(obB, B["consts"]) + tuple(xbufs(g0, g0 + 512)),
                      writes=tuple(xbufs(g0, g0 + 512)))

    def tap(name):
        if not DEBUG_TAPS:
            return
        td = nc.dram_tensor(name, [D, TH], F32, kind="ExternalOutput").ap()
        for fc in range(FC):
            dma("SY", td[fc * 128:(fc + 1) * 128, :], xb_t[:, fc * TH: (fc + 1) * TH],
                reads=tuple(XR))

    dm1, dr1 = ln_prologue(FFN1_CH, R_M1, R_RSTD1)
    quant_p1(wgT_d, D, INNER, 0)
    qg.extend(quant_p2(wgT_d, wg_s, D, INNER, 0, "qgS", WGb[:, 0:2048]))
    quant_p1(wuT_d, D, INNER, 1)
    qu.extend(quant_p2(wuT_d, wu_s, D, INNER, 1, "quS", WUb[:, 0:2048]))
    quant_p1(wdT_d, INNER, D, 2)
    wsud = t_tt("VE", wsrow[0:1, 3:4], wsrow[0:1, 1:2], wsrow[0:1, 2:3], ALU.mult,
                reads=(B["wsrow"],), writes=(B["wsrow"],), risky=True)
    P.fence(PROSET + [B[f"WSTG{k}"] for k in range(7)], CHUNKSET)
    ffn_call(FFN1_CH, R_M1, dm1, dr1, dbg=True, emit_extra=[
        lambda: qd.extend(quant_p2(wdT_d, wd_s, INNER, D, 2, "qdS", WDb[:, 0:2048],
                                   slots=P0SLOTS[:3])),
        lambda: precast(wqkv_d, wqkv_bf, 0, 512, 3 * D, pcs),
        lambda: precast(wqkv_d, wqkv_bf, 512, D, 3 * D, pcs),
        lambda: precast(wout_d, wout_bf, 0, D, D, pcs)])
    tap("tap_x1")
    if DEBUG_TAPS:
        trows = nc.dram_tensor("tap_rows", [12, TH], F32, kind="ExternalOutput").ap()
        dma("SY", trows, rows_s, reads=tuple(B[f"rows{i}"] for i in range(12)))
        twg = nc.dram_tensor("tap_wg", [D, INNER], I8, kind="ExternalOutput").ap()
        dma("SY", twg, wg_s, reads=(B["wg_s"],))
    P.fence(CHUNKSET, PROSET)
    dm2, dr2 = ln_prologue(FFN1_CH, R_M2, R_RSTD2)
    P.fence(PROSET + CHUNKSET, ATTNSET)
    attn_phase(dm2, dr2)
    tap("tap_x2")
    P.fence(ATTNSET, PROSET)
    dm3, dr3 = ln_prologue(FFN2_CH, R_M3, R_RSTD3)
    P.fence(PROSET, CHUNKSET)
    ffn_call(FFN2_CH, R_M3, dm3, dr3)

    outs = []
    for fc in range(FC):
        st = dma("SY", y_d[fc * 128:(fc + 1) * 128, :], xb_t[:, fc * TH + 3: fc * TH + 3 + T],
                 reads=tuple(XR), dsem="out")
        outs.append(st)
    P.op("SY", lambda e: e.nop(), deps=[outs[-1]])

    dsem_names = set()
    for r in P.ops:
        if r.dma:
            dsem_names.add(r.dsem if r.dsem is not None else ("d" + r.eng))
    with ExitStack() as st:
        sems = {n: st.enter_context(nc.semaphore(n))
                for n in ["sTE", "sVE", "sSE", "sGP"] + sorted(dsem_names)}
        block = st.enter_context(nc.Block())
        P.finalize(block, sems)
    return nc


_NC_CACHE = {}


def make_inmaps(L0_attn_x, L0_attn_w_qkv, L0_attn_proj, L0_attn_w_out, L0_attn_b_out,
                L0_ffn_w_gate, L0_ffn_w_up, L0_ffn_w_down):
    x = np.asarray(L0_attn_x, dtype=np.float32)
    wqkv = np.ascontiguousarray(np.asarray(L0_attn_w_qkv, dtype=np.float32))
    proj = np.ascontiguousarray(np.asarray(L0_attn_proj, dtype=np.float32))
    wout = np.ascontiguousarray(np.asarray(L0_attn_w_out, dtype=np.float32))
    bout = np.ascontiguousarray(np.asarray(L0_attn_b_out, dtype=np.float32))
    wgT = np.ascontiguousarray(np.asarray(L0_ffn_w_gate, dtype=np.float32).T)
    wuT = np.ascontiguousarray(np.asarray(L0_ffn_w_up, dtype=np.float32).T)
    wdT = np.ascontiguousarray(np.asarray(L0_ffn_w_down, dtype=np.float32).T)

    Bsz, N, _ = x.shape
    eye = np.eye(128, dtype=np.float32)
    kp = np.arange(128)[:, None]
    jj = np.arange(128)[None, :]
    maskA = ((kp - jj >= 0) & (kp - jj <= 6)).astype(np.float32)
    kp8 = np.arange(8)[:, None]
    maskB = ((jj - kp8 >= 122) & (kp8 < 6)).astype(np.float32)
    ones_col = np.full((128, 1), 1.0 / 1024, dtype=ml_dtypes.bfloat16)
    ones_row = np.ones((1, 128), dtype=np.float32)
    pairmask = np.zeros((128, 2), dtype=ml_dtypes.bfloat16)
    pairmask[0:64, 0] = 1
    pairmask[64:128, 1] = 1

    in_maps = []
    for cix in range(NCORE):
        b = cix // 4
        s = 2048 * (cix % 4)
        lo = s - 3
        shard = np.zeros((TH, D), dtype=np.float32)
        a0, a1 = max(lo, 0), min(s + 2051, N)
        shard[a0 - lo:a1 - lo] = x[b, a0:a1]
        x_fm = np.ascontiguousarray(shard.T)
        mask = np.zeros(TQG, dtype=np.float32)
        tt = np.arange(TH)
        valid = (tt + lo >= 0) & (tt + lo < N)
        mask[0:TH][valid] = 1.0
        maskk = np.ascontiguousarray(np.broadcast_to(mask, (128, TQG))).astype(ml_dtypes.bfloat16)
        in_maps.append(dict(
            x_fm=x_fm, wqkv=wqkv, proj=proj, wout=wout, bout=bout,
            wgT=wgT, wuT=wuT, wdT=wdT, maskk=maskk, maskA=maskA, maskB=maskB,
            eye=eye, ones_col=ones_col, ones_row=ones_row, pairmask=pairmask))

    return in_maps, Bsz, N


def kernel(**inputs):
    in_maps, Bsz, N = make_inmaps(**inputs)
    if "nc" not in _NC_CACHE:
        _NC_CACHE["nc"] = build_nc()
    nc = _NC_CACHE["nc"]
    res = run_bass_kernel_spmd(nc, in_maps, core_ids=list(range(NCORE)))
    out = np.empty((Bsz, N, D), dtype=np.float32)
    for cix in range(NCORE):
        b = cix // 4
        s = 2048 * (cix % 4)
        out[b, s:s + 2048] = res.results[cix]["y_fm"].T
    return out
